# version 29
# baseline (speedup 1.0000x reference)
import numpy as np

import concourse.bass as bass
import concourse.mybir as mybir
from concourse.bass_utils import run_bass_kernel_spmd
from concourse.alu_op_type import AluOpType

F32 = mybir.dt.float32
BF16 = mybir.dt.bfloat16
AF = mybir.ActivationFunctionType
MAGIC = 12582912.0

N_T = 2048
D = 1024
NH = 16
NTT = 16
NQG = 4
NHP = 8

QGKC = [(qg, kc) for qg in range(NQG) for kc in range(4 * qg + 4)]
QGKC_IDX = {p: i for i, p in enumerate(QGKC)}


def n_cols(qg, kc):
    off = max(0, kc * 128 - qg * 512)
    return 512 - off, off


class Op:
    __slots__ = ["kind", "args", "waits", "inc", "stream"]

    def __init__(self, kind, args):
        self.kind, self.args, self.waits, self.inc, self.stream = kind, args, [], False, None


class Sched:
    def __init__(self):
        self.streams = {n: [] for n in ["pe", "act", "dve", "sync", "gp"]}

    def emit(self, stream, kind, waits=(), **args):
        op = Op(kind, args)
        op.stream = stream
        for w in waits:
            if w is not None:
                op.waits.append(w)
        self.streams[stream].append(op)
        return (stream, len(self.streams[stream]) - 1)

    NDMA = {"dma_small": 3, "dma_pospm": 1, "dma_x": 1, "dma_x2": 1,
            "dma_g": 2, "dma_wv": 1, "dma_wqk": 1, "dma_wo": 1,
            "qe_swap": 4, "rearr": 4, "dma_out": 1, "final": 0,
            "xnt_dma": 4, "ldiv_d1": 1, "ldiv_d2": 2}

    def finalize(self):
        for ops in self.streams.values():
            for op in ops:
                for (sn, i) in op.waits:
                    self.streams[sn][i].inc = True
        self.incval = {}
        for name, ops in self.streams.items():
            c, vals = 0, []
            for op in ops:
                if name in ("sync", "gp") or op.kind in self.NDMA:
                    c += 16 * self.NDMA[op.kind]
                    op.inc = True
                elif op.inc:
                    c += 1
                vals.append(c)
            self.incval[name] = vals

    def wval(self, sn, i):
        return self.incval[sn][i]


def build_schedule():
    s = Sched()
    E = s.emit

    ld_small = E("sync", "dma_small")
    sig = E("act", "gates", waits=[ld_small])
    btab_a = E("dve", "btab_a", waits=[sig])

    ln_norm = [None] * NTT
    xnt_ev = [None] * NTT
    tp_last = [None] * NTT
    sqs = [None] * NTT
    tp_ops = [[None, None] for _ in range(NTT)]
    ev_ops = [[None, None] for _ in range(NTT)]
    stage_chain = None
    wv_cast = []
    wqk_cast = {}

    def emit_back(t):
        f2 = E("dve", "ln_fin2", waits=[sqs[t]], tt=t)
        f2b = E("dve", "ln_fin2b", waits=[f2], tt=t)
        ln_norm[t] = E("dve", "ln_norm", waits=[f2b], tt=t)
        if t >= 1:
            for grp in range(2):
                ev_ops[t - 1][grp] = E("dve", "xnt_ev",
                                       waits=[tp_ops[t - 1][grp]],
                                       tt=t - 1, grp=grp)
            xnt_ev[t - 1] = (ev_ops[t - 1][0], ev_ops[t - 1][1])
        for grp in range(2):
            tw = [ln_norm[t]]
            if t >= 1:
                tw.append(ev_ops[t - 1][grp])
            else:
                tw.append(ld_small)
            tp_ops[t][grp] = E("pe", "xnt_tp", waits=tw, tt=t, grp=grp)
        tp_last[t] = tp_ops[t][1]

    for tt in range(NTT):
        w = [tp_last[tt - 4]] if tt >= 4 else []
        xd = E("sync", "dma_x", waits=w, tt=tt)
        st0 = E("dve", "ln_stats", waits=[xd], tt=tt)
        f1 = E("dve", "ln_fin1", waits=[st0], tt=tt)
        f1b = E("dve", "ln_fin1b", waits=[f1], tt=tt)
        f1c = E("dve", "ln_fin1c", waits=[f1b], tt=tt)
        sqs[tt] = E("act", "ln_sqrt", waits=[f1c], tt=tt)
        if tt < 8:
            d = E("gp", "dma_wv", waits=[stage_chain], ic=tt)
            stage_chain = E("dve", "wv_cast", waits=[d], ic=tt)
            wv_cast.append(stage_chain)
        elif tt < 10:
            sec = tt - 8
            d = E("gp", "dma_wqk", waits=[stage_chain], sec=sec, hp=0)
            stage_chain = E("dve", "wqk_cast", waits=[d], sec=sec, hp=0, slot=sec)
            wqk_cast[(sec, 0)] = stage_chain
        if tt >= 1:
            emit_back(tt - 1)
    emit_back(NTT - 1)
    for grp in range(2):
        ev_ops[NTT - 1][grp] = E("dve", "xnt_ev",
                                 waits=[tp_ops[NTT - 1][grp]],
                                 tt=NTT - 1, grp=grp)
    xnt_ev[NTT - 1] = (ev_ops[NTT - 1][0], ev_ops[NTT - 1][1])
    tp_last[NTT - 1] = tp_ops[NTT - 1][1]

    ones_set = E("dve", "consts_set")
    v_ev = [None] * NTT
    for tt in range(NTT):
        last = None
        for ic in range(8):
            for oh in range(2):
                w = []
                if ic == 0 and oh == 0:
                    w = list(xnt_ev[tt]) + [wv_cast[-1]]
                    if tt >= 2:
                        w.append(v_ev[tt - 2])
                last = E("pe", "v_mm", waits=w, tt=tt, ic=ic, oh=oh,
                         start=(ic == 0), stop=(ic == 7))
        v_ev[tt] = E("dve", "v_ev", waits=[last, ones_set], tt=tt)

    ld_pospm = E("sync", "dma_pospm",
                 waits=[ln_norm[NTT - 1], tp_last[NTT - 1]])
    last_qk_sect = {}
    rot = {}
    prev_sin = prev_cos = prev_frc = None
    pack_rot_guard = {0: None, 1: None}
    qe_by_bank = [None, None, ev_ops[NTT - 1][0], ev_ops[NTT - 1][1]]
    rot_by_buf = [None, None, None, None]
    scos_by_half = {}
    g = 0
    halves = [(hp, half) for hp in range(NHP) for half in range(2)]

    def trig_front(hp, half):
        nonlocal prev_frc
        u = E("act", "trig_u", waits=[ld_pospm, sig, prev_frc], hp=hp, half=half)
        rt = E("dve", "trig_rt", waits=[u, prev_cos], hp=hp, half=half)
        fs = E("dve", "trig_frs", waits=[prev_sin, rt], hp=hp, half=half)
        return fs

    def trig_back(hp, half, fs):
        nonlocal prev_sin, prev_cos, prev_frc
        sw_sin = [fs, pack_rot_guard[half]]
        if hp == 0 and half == 1:
            sw_sin += [v_ev[NTT - 1], v_ev[NTT - 2]]
        ssin = E("act", "trig_sin", waits=sw_sin, hp=hp, half=half)
        u25 = E("dve", "trig_u25", waits=[fs], hp=hp, half=half)
        rt2 = E("dve", "trig_rt2", waits=[u25], hp=hp, half=half)
        frc = E("dve", "trig_frc", waits=[ssin, rt2], hp=hp, half=half)
        scos = E("act", "trig_cos", waits=[frc], hp=hp, half=half)
        prev_sin, prev_cos, prev_frc = ssin, scos, frc
        scos_by_half[(hp, half)] = scos

    f0 = trig_front(*halves[0])
    trig_back(*halves[0], f0)
    for i, (hp, half) in enumerate(halves):
        fs_next = trig_front(*halves[i + 1]) if i + 1 < len(halves) else None
        scos = scos_by_half[(hp, half)]
        last_rot_this_half = None
        for sec in range(2):
            slot = sec if hp == 0 else 2 + ((hp - 1) * 2 + sec) % 6
            if half == 0 and hp > 0:
                d = E("gp", "dma_wqk", waits=[stage_chain], sec=sec, hp=hp)
                stage_chain = E("dve", "wqk_cast",
                                waits=[d, last_qk_sect.get((sec, hp - 3))],
                                sec=sec, hp=hp, slot=slot)
                wqk_cast[(sec, hp)] = stage_chain
            for tg in (2 * half, 2 * half + 1):
                bank = g % 4
                buf = g % 4
                w = ([wqk_cast[(sec, hp)], qe_by_bank[bank]]
                     + list(xnt_ev[4 * tg + 3]))
                last = None
                for ic in range(8):
                    last = E("pe", "qk_mm", waits=(w if ic == 0 else []),
                             sec=sec, hp=hp, tg=tg, ic=ic, bank=bank,
                             slot=slot, start=(ic == 0), stop=(ic == 7))
                last_qk_sect[(sec, hp)] = last
                qe = E("act", "qe_ev", waits=[last, rot_by_buf[buf]],
                       bank=bank, buf=buf)
                qe_by_bank[bank] = qe
                sw = E("sync", "qe_swap", waits=[qe], buf=buf)
                rc = E("dve", "rot_c", waits=[qe, scos, sw], tg=tg, buf=buf)
                rs = E("dve", "rot_s", waits=[sw], tg=tg, buf=buf)
                r3 = E("dve", "rot_add", waits=[rs, rc], sec=sec, hp=hp, tg=tg, buf=buf)
                rot[(sec, hp, tg)] = r3
                rot_by_buf[buf] = r3
                last_rot_this_half = r3
                g += 1
        pack_rot_guard[half] = last_rot_this_half
        if fs_next is not None:
            trig_back(*halves[i + 1], fs_next)
    a2_close = prev_cos

    ld_g = E("sync", "dma_g", waits=[a2_close, rot_by_buf[0], rot_by_buf[1],
                                     rot_by_buf[2], rot_by_buf[3]])
    mpev = []
    rearr_all = []
    wct_cast = []
    mpi = 0
    for m in range(2):
        for ch in range(8):
            bank = 4 + 2 * (mpi % 2)
            w = [ld_g]
            if mpi >= 2:
                w.append(mpev[mpi - 2])
            elif mpi == 0:
                w.append(qe_by_bank[0])
            else:
                w.append(qe_by_bank[2])
            mm = E("pe", "mprime_mm", waits=w, m=m, ch=ch, bank=bank)
            mpev.append(E("dve", "mprime_ev", waits=[mm, a2_close], m=m, ch=ch,
                          bank=bank))
            mpi += 1
        for dcq in range(4):
            rr = None
            for ci in range(8):
                rr = E("sync" if ci % 2 else "gp", "rearr",
                       waits=[mpev[m * 8 + ci],
                              (wct_cast[-1] if wct_cast else None)],
                       m=m, ci=ci, dcq=dcq)
                rearr_all.append(rr)
            wct_cast.append(E("dve", "wct_cast",
                              waits=[rearr_all[-1], rearr_all[-2],
                                     v_ev[NTT - 1], v_ev[NTT - 2]],
                              m=m, dcq=dcq, which=0))
            wct_cast.append(E("dve", "wct_cast", waits=[], m=m, dcq=dcq, which=1))
    wct_done = wct_cast[-1]

    wo_cast = []
    for dc in range(8):
        d = E("gp", "dma_wo", waits=[stage_chain if dc == 0 else wo_cast[-1],
                                     a2_close if dc == 0 else None], dc=dc)
        cw = [d]
        if dc == 0:
            cw += [a2_close, rot_by_buf[0], rot_by_buf[1], rot_by_buf[2],
                   rot_by_buf[3]] + list(xnt_ev[15]) + list(xnt_ev[14])
        wo_cast.append(E("dve", "wo_cast", waits=cw, dc=dc))

    s_hist = []
    exp_hist = []
    pv_done = []
    div_hist = []
    prev_div = {}
    prev_lev = {}
    prev_d1 = {}
    prev_d2 = {}
    prev_oev = {}
    first_pv_guard = {(0, 0): v_ev[NTT - 1], (0, 1): qe_by_bank[1],
                      (1, 0): qe_by_bank[2], (1, 1): mpev[-1]}
    rb_first_guard = qe_by_bank[3]
    pending = []
    for hp in range(NHP):
        btab = E("dve", "btab_hp",
                 waits=[btab_a, exp_hist[-1] if exp_hist else None], hp=hp)
        for qg in range(NQG):
            w = 4 * qg + 4
            gates = {}

            def emit_s(kc):
                for head in range(2):
                    n, off = n_cols(qg, kc)
                    si = len(s_hist)
                    waits = []
                    if kc == 0 and head == 0:
                        waits = [btab, rot[(1, hp, qg)], rot[(0, hp, qg)]]
                        if si < 3:
                            waits += [v_ev[NTT - 1], v_ev[NTT - 2]]
                    if si >= 3:
                        waits.append(exp_hist[si - 3])
                    smm = E("pe", "s_mm", waits=waits, hp=hp, qg=qg, kc=kc,
                            head=head, bank=si % 3, n=n, off=off)
                    s_hist.append(smm)
                    ei = len(exp_hist)
                    ew = [smm]
                    if ei >= 4:
                        ew.append(pv_done[ei - 4])
                    ex = E("act", "p_exp", waits=ew, hp=hp, qg=qg, kc=kc,
                           head=head, bank=si % 3, slot=ei % 4, n=n)
                    exp_hist.append(ex)
                    gate = ex
                    if kc >= 4 * qg:
                        gate = E("dve", "p_mask", waits=[ex], slot=ei % 4)
                    gates[(kc, head)] = (gate, ei % 4)

            def emit_pv(kc):
                n, off = n_cols(qg, kc)
                par = qg % 2
                for head in range(2):
                    gate, slot = gates[(kc, head)]
                    waits_ = [gate]
                    if kc == 0:
                        waits_.append(prev_div.get((head, par),
                                                   first_pv_guard[(head, par)]))
                    pv = E("pe", "pv_mm", waits=waits_, hp=hp, qg=qg, kc=kc,
                           head=head, par=par, slot=slot, n=n, off=off,
                           start=(kc == 0), stop=(kc == w - 1))
                    pv_done.append(pv)

            emit_s(0)
            emit_s(1)
            for kc in range(w):
                if kc == min(6, w - 1):
                    while pending:
                        args = pending.pop(0)
                        bc = E("pe", "rb_bcast",
                               waits=[args["d2"],
                                      (div_hist[-1] if div_hist else mpev[-1]),
                                      (None if div_hist else rb_first_guard)],
                               head=args["head"])
                        dv = E("dve", "div_mult", waits=[bc, args["oev"]],
                               hp=args["hp"], qg=args["qg"], head=args["head"],
                               par=args["par"])
                        div_hist.append(dv)
                        prev_div[(args["head"], args["par"])] = dv
                emit_pv(kc)
                if kc + 2 < w:
                    emit_s(kc + 2)
            par = qg % 2
            for head in range(2):
                lastA = pv_done[-1]
                dh = div_hist[-1] if div_hist else None
                lev = E("dve", "l_ev", waits=[lastA, prev_d1.get(head), dh],
                        head=head, par=par)
                oev = E("dve", "o_ev", waits=[lastA, dh], head=head, par=par)
                d1 = E("sync", "ldiv_d1", waits=[lev, prev_d2.get(head)], head=head)
                rc = E("dve", "recl", waits=[d1], head=head)
                rhi = E("dve", "recl_hi", waits=[rc], head=head)
                rhf = E("dve", "recl_hif", waits=[rhi], head=head)
                rlo = E("dve", "recl_lo", waits=[rhf], head=head)
                d2 = E("sync", "ldiv_d2", waits=[rlo, dh], head=head)
                pending.append({"d2": d2, "oev": oev, "hp": hp, "qg": qg,
                                "head": head, "par": par})
                prev_d1[head] = d1
                prev_d2[head] = d2
    while pending:
        args = pending.pop(0)
        bc = E("pe", "rb_bcast",
               waits=[args["d2"], (div_hist[-1] if div_hist else mpev[-1])],
               head=args["head"])
        dv = E("dve", "div_mult", waits=[bc, args["oev"]], hp=args["hp"],
               qg=args["qg"], head=args["head"], par=args["par"])
        div_hist.append(dv)
        prev_div[(args["head"], args["par"])] = dv
    div_all = div_hist[-1]

    res = {}
    outs = []
    for tt in range(NTT):
        w = [wct_done]
        if tt >= 2:
            w += [res[(tt - 2, 0, "xy")], res[(tt - 2, 1, "xy")]]
        x2 = E("sync", "dma_x2", waits=w, tt=tt)
        for oh in range(2):
            wlast = None
            for dc in range(8):
                ww = []
                if dc == 0:
                    ww = [wo_cast[-1], div_all]
                    if tt >= 1:
                        ww.append(res[(tt - 1, oh, "xy")])
                wlast = E("pe", "wo_mm", waits=ww, tt=tt, oh=oh, dc=dc,
                          start=(dc == 0), stop=(dc == 7))
            fw = [wct_done, div_all]
            if tt >= 1:
                fw.append(res[(tt - 1, oh, "f")])
            flast = E("pe", "ffn_pre", waits=fw, tt=tt, oh=oh)
            for ic in range(8):
                flast = E("pe", "ffn_mm", waits=[], tt=tt, oh=oh, ic=ic,
                          stop=(ic == 7))
            rw = [x2, wlast]
            if tt >= 2:
                rw.append(outs[tt - 2])
            r1 = E("dve", "res_xy", waits=rw, tt=tt, oh=oh)
            r2 = E("dve", "res_f", waits=[flast, r1], tt=tt, oh=oh)
            res[(tt, oh, "xy")] = r1
            res[(tt, oh, "f")] = r2
        outs.append(E("gp", "dma_out", waits=[res[(tt, 0, "f")], res[(tt, 1, "f")]],
                      tt=tt))
    E("gp", "final", waits=[outs[-1]])
    s.finalize()
    return s


def build_nc():
    sched = build_schedule()
    nc = bass.Bass("TRN2", target_bir_lowering=False)

    dx = nc.dram_tensor("x", [N_T, D], F32, kind="ExternalInput")
    dwqkv = nc.dram_tensor("wqkvT", [D, 3 * D], F32, kind="ExternalInput")
    dwo = nc.dram_tensor("woT", [D, D], F32, kind="ExternalInput")
    dg1 = nc.dram_tensor("g1", [4, 2048], F32, kind="ExternalInput")
    dg2 = nc.dram_tensor("g2", [4, 512], F32, kind="ExternalInput")
    dsm = nc.dram_tensor("smalls", [128, 1120], F32, kind="ExternalInput")
    dmaskb = nc.dram_tensor("maskb", [128, 256], BF16, kind="ExternalInput")
    dpospm = nc.dram_tensor("pospm", [128, N_T], F32, kind="ExternalInput")
    dident = nc.dram_tensor("ident", [128, 128], F32, kind="ExternalInput")
    dout = nc.dram_tensor("out", [N_T, D], F32, kind="ExternalOutput")

    from contextlib import ExitStack
    with ExitStack() as stk:
        en = stk.enter_context
        xnT = en(nc.sbuf_tensor("xnT", [128, 8 * N_T], BF16))
        qT = en(nc.sbuf_tensor("qT", [128, 8 * N_T], BF16))
        kT = en(nc.sbuf_tensor("kT", [128, 8 * N_T], BF16))
        Vsb = en(nc.sbuf_tensor("Vsb", [128, NTT * NH * 65], BF16))
        Wbig = en(nc.sbuf_tensor("Wbig", [128, 8192], BF16))
        Pbuf = en(nc.sbuf_tensor("Pbuf", [128, 4 * 512], BF16))
        A2S = en(nc.sbuf_tensor("A2S", [128, 8192], BF16))
        sbiasb = en(nc.sbuf_tensor("sbiasb", [1, D], BF16))
        smaskb = en(nc.sbuf_tensor("smaskb", [128, 128], BF16))
        sonesb = en(nc.sbuf_tensor("sonesb", [1, 128], BF16))
        XNTILE, WQKB, QE1, QE2, PACKC, PACKS = 0, 2048, 4096, 5120, 6144, 7168

        FA = en(nc.sbuf_tensor("FA", [128, 7264], F32))
        XBUF, TRU, TRT, TRF, WST = 0, 2048, 3072, 4096, 5120
        LSB, LQ, BTAB = 6144, 7168, 7184
        FAW = 7264
        sm = en(nc.sbuf_tensor("sm", [128, 1120], F32))
        odiv32 = en(nc.sbuf_tensor("odiv32", [64, 1024], F32))
        print("SBUF bytes remaining after alloc:", nc.sbuf_bytes_remaining)
        lns = en(nc.sbuf_tensor("lns", [128, 16], F32))
        hilo = en(nc.sbuf_tensor("hilo", [128, 16], BF16))
        rrows = en(nc.sbuf_tensor("rrows", [128, 2048], BF16))
        C_PIG, C_EG, C_LX, C_PAR, C_BIAS = 0, 8, 24, 40, 96

        pb = [en(nc.psum_tensor("pb%d" % i, [128, 512], F32)) for i in range(8)]

        sems = {n: en(nc.semaphore("sem_" + n)) for n in sched.streams}
        block = en(nc.Block())

        def waits(eng, op):
            for (sn, i) in op.waits:
                eng.wait_ge(sems[sn], sched.wval(sn, i))

        def fin(instr, op):
            if op.inc:
                instr.then_inc(sems[op.stream],
                               16 if op.stream in ("sync", "gp") else 1)

        def dinc(instr, op):
            instr.then_inc(sems[op.stream], 16)
            return instr

        def xnt(ic, t0, n):
            return xnT[:, ic * N_T + t0: ic * N_T + t0 + n]

        def qe1sl(buf, r0=0, r1=128):
            if buf < 2:
                return A2S[r0:r1, QE1 + buf * 512: QE1 + (buf + 1) * 512]
            return A2S[r0:r1, XNTILE + (buf - 2) * 512: XNTILE + (buf - 1) * 512]

        def qe2sl(buf, r0=0, r1=128):
            if buf < 2:
                return A2S[r0:r1, QE2 + buf * 512: QE2 + (buf + 1) * 512]
            return A2S[r0:r1, XNTILE + 1024 + (buf - 2) * 512:
                       XNTILE + 1024 + (buf - 1) * 512]

        def emit_xnt(eng, op):
            a = op.args
            tt, half = a["tt"], a["half"]
            b2 = tt % 2
            for ic in range(half * 4, half * 4 + 4):
                dinc(eng.dma_start_transpose(
                    xnT[:, ic * N_T + tt * 128: ic * N_T + tt * 128 + 128],
                    A2S[:, XNTILE + b2 * D + ic * 128:
                        XNTILE + b2 * D + (ic + 1) * 128]), op)

        def emit_rearr(eng, op):
            a = op.args
            m, ci, dcq = a["m"], a["ci"], a["dcq"]
            reg = TRU if m == 0 else XBUF
            for bp in range(4):
                p0 = 8 * dcq + 32 * bp
                src = FA[p0:p0 + 8, reg + ci * 256: reg + (ci + 1) * 256]
                src = src.rearrange("p (g e) -> p g e", g=16)
                c0 = TRF + (4 * ci + bp) * 16
                dst = FA[:, c0:c0 + 16]
                dinc(eng.dma_start(out=dst, in_=src), op)

        def e_sync(eng, op):
            k, a = op.kind, op.args
            if k == "dma_small":
                dinc(eng.dma_start(out=sm[:], in_=dsm[:, :]), op)
                dinc(eng.dma_start(out=smaskb[:], in_=dmaskb[:, 0:128]), op)
                dinc(eng.dma_start(out=FA[:, TRF:TRF + 128], in_=dident[:, :]), op)
            elif k == "dma_x":
                tt = a["tt"]
                b = tt % 4
                dinc(eng.dma_start(out=FA[:, XBUF + b * D: XBUF + (b + 1) * D],
                                   in_=dx[tt * 128:(tt + 1) * 128, :]), op)
            elif k == "dma_x2":
                tt = a["tt"]
                b = tt % 2
                dinc(eng.dma_start(out=FA[:, XBUF + b * D: XBUF + (b + 1) * D],
                                   in_=dx[tt * 128:(tt + 1) * 128, :]), op)
            elif k == "xnt_dma":
                emit_xnt(eng, op)
            elif k == "dma_pospm":
                dinc(eng.dma_start(out=FA[:, XBUF:XBUF + N_T], in_=dpospm[:, :]), op)
            elif k == "rearr":
                emit_rearr(eng, op)
            elif k == "xnt_dma":
                emit_xnt(eng, op)
            elif k == "qe_swap":
                buf = a["buf"]
                for blk in range(4):
                    src = [32, 0, 96, 64][blk]
                    dinc(eng.dma_start(out=qe2sl(buf, blk * 32, (blk + 1) * 32),
                                       in_=qe1sl(buf, src, src + 32)), op)
            elif k == "ldiv_d1":
                head = a["head"]
                src = FA[0:1, LSB + head * 512: LSB + (head + 1) * 512]
                dst = FA[:, LQ + head * 4: LQ + (head + 1) * 4]
                dinc(eng.dma_start(out=dst, in_=src), op)
            elif k == "ldiv_d2":
                head = a["head"]
                dinc(eng.dma_start(
                    out=rrows[0:1, head * 1024: head * 1024 + 512],
                    in_=hilo[:, head * 4: head * 4 + 4]), op)
                dinc(eng.dma_start(
                    out=rrows[0:1, head * 1024 + 512: (head + 1) * 1024],
                    in_=hilo[:, 8 + head * 4: 8 + (head + 1) * 4]), op)
            elif k == "dma_g":
                dinc(eng.dma_start(out=FA[0:4, XBUF:XBUF + 2048], in_=dg1[:, :]), op)
                dinc(eng.dma_start(out=FA[0:4, TRF + 512:TRF + 1024], in_=dg2[:, :]), op)
            else:
                raise KeyError(k)

        def e_gp(eng, op):
            k, a = op.kind, op.args
            if k == "dma_wv":
                ic = a["ic"]
                dinc(eng.dma_start(out=FA[:, WST:WST + 1024],
                                   in_=dwqkv[ic * 128:(ic + 1) * 128, 2048:3072]), op)
            elif k == "dma_wqk":
                sec, hp = a["sec"], a["hp"]
                c0 = sec * D + hp * 128
                src = bass.AP(dwqkv, c0, [[3 * D, 128], [128 * 3 * D, 8], [1, 128]])
                dinc(eng.dma_start(out=FA[:, WST:WST + 1024], in_=src), op)
            elif k == "dma_wo":
                dc = a["dc"]
                dinc(eng.dma_start(out=FA[:, WST:WST + 1024],
                                   in_=dwo[dc * 128:(dc + 1) * 128, :]), op)
            elif k == "qe_swap":
                buf = a["buf"]
                for blk in range(4):
                    src = [32, 0, 96, 64][blk]
                    dinc(eng.dma_start(out=qe2sl(buf, blk * 32, (blk + 1) * 32),
                                       in_=qe1sl(buf, src, src + 32)), op)
            elif k == "rearr":
                emit_rearr(eng, op)
            elif k == "xnt_dma":
                emit_xnt(eng, op)
            elif k == "dma_out":
                tt = a["tt"]
                dinc(eng.dma_start(out=dout[tt * 128:(tt + 1) * 128, :],
                                   in_=FA[:, TRU + (tt % 2) * D: TRU + (tt % 2 + 1) * D]), op)
            elif k == "final":
                pass
            else:
                raise KeyError(k)

        def e_act(eng, op):
            k, a = op.kind, op.args
            if k == "gates":
                eng.activation(out=sm[:, C_PIG:C_PIG + 8], in_=sm[:, C_PIG:C_PIG + 8],
                               func=AF.Sigmoid)
                eng.activation(out=sm[:, C_EG:C_EG + 16], in_=sm[:, C_EG:C_EG + 16],
                               func=AF.Sigmoid)
                fin(eng.activation(out=sm[:, C_LX:C_LX + 16], in_=sm[:, C_LX:C_LX + 16],
                                   func=AF.Exp, scale=-1.0), op)
            elif k == "xnt_dma":
                emit_xnt(eng, op)
            elif k == "ln_sqrt":
                b = (a["tt"] % 2) * 8
                fin(eng.activation(out=lns[:, b + 4:b + 5], in_=lns[:, b + 3:b + 4],
                                   func=AF.Sqrt, bias=sm[:, 95:96]), op)
            elif k == "trig_u":
                hp, half = a["hp"], a["half"]
                fin(eng.activation(out=FA[:, TRU:TRU + 1024],
                                   in_=FA[:, XBUF + half * 1024: XBUF + (half + 1) * 1024],
                                   func=AF.Copy, scale=sm[:, C_PIG + hp:C_PIG + hp + 1]), op)
            elif k == "trig_sin":
                out = (A2S[:, PACKS:PACKS + 1024] if a["half"] == 0
                       else Wbig[:, 0:1024])
                fin(eng.activation(out=out, in_=FA[:, TRF:TRF + 1024],
                                   func=AF.Sin, scale=2 * np.pi), op)
            elif k == "trig_cos":
                out = (A2S[:, PACKC:PACKC + 1024] if a["half"] == 0
                       else Wbig[:, 1024:2048])
                fin(eng.activation(out=out, in_=FA[:, TRT:TRT + 1024],
                                   func=AF.Sin, scale=2 * np.pi), op)
            elif k == "qe_ev":
                bank, buf = a["bank"], a["buf"]
                fin(eng.copy(out=qe1sl(buf), in_=pb[4 + bank][:]), op)
            elif k == "p_exp":
                bank, slot, n = a["bank"], a["slot"], a["n"]
                col = BTAB + a["head"] * 40 + QGKC_IDX[(a["qg"], a["kc"])]
                fin(eng.activation(out=Pbuf[:, slot * 512: slot * 512 + n],
                                   in_=pb[bank][:, 0:n], func=AF.Exp, scale=0.125,
                                   bias=FA[:, col:col + 1]), op)
            else:
                raise KeyError(k)

        def e_dve(eng, op):
            k, a = op.kind, op.args
            if k == "btab_a":
                fin(eng.tensor_mul(out=sm[:, C_EG:C_EG + 16], in0=sm[:, C_EG:C_EG + 16],
                                   in1=sm[:, C_LX:C_LX + 16]), op)
            elif k == "btab_hp":
                hp = a["hp"]
                for head in range(2):
                    h = 2 * hp + head
                    i = eng.tensor_scalar_mul(out=FA[:, BTAB + head * 40:
                                                     BTAB + (head + 1) * 40],
                                              in0=sm[:, C_PAR:C_PAR + 40],
                                              scalar1=sm[:, C_EG + h:C_EG + h + 1])
                fin(i, op)
            elif k == "v_ev":
                tt = a["tt"]
                p0, p1 = (pb[0], pb[1]) if tt % 2 == 0 else (pb[2], pb[3])
                eng.tensor_copy(out=bass.AP(Vsb, tt * NH * 65,
                                            [[16640, 128], [65, 8], [1, 64]]),
                                in_=p0[:])
                fin(eng.tensor_copy(out=bass.AP(Vsb, tt * NH * 65 + 8 * 65,
                                                [[16640, 128], [65, 8], [1, 64]]),
                                    in_=p1[:]), op)
            elif k == "ln_norm":
                tt = a["tt"]
                b4, b = tt % 4, (tt % 2) * 8
                sl = FA[:, XBUF + b4 * D: XBUF + (b4 + 1) * D]
                fin(eng.tensor_scalar(out=sl, in0=sl,
                                      scalar1=lns[:, b + 5:b + 6],
                                      scalar2=lns[:, b + 6:b + 7],
                                      op0=AluOpType.mult, op1=AluOpType.add), op)
            elif k == "ln_stats":
                tt = a["tt"]
                b4, b = tt % 4, (tt % 2) * 8
                xa = FA[:, XBUF + b4 * D: XBUF + (b4 + 1) * D]
                eng.reduce_sum(out=lns[:, b:b + 1], in_=xa, axis=mybir.AxisListType.X)
                fin(eng.scalar_tensor_tensor(out=Pbuf[:, 0:1024], in0=xa,
                                             scalar=1.0, in1=xa, op0=AluOpType.mult,
                                             op1=AluOpType.mult,
                                             accum_out=lns[:, b + 1:b + 2]), op)
            elif k == "ln_fin1":
                b = (a["tt"] % 2) * 8
                c = lambda j: lns[:, b + j:b + j + 1]
                fin(eng.tensor_scalar_mul(out=c(2), in0=c(0), scalar1=1.0 / D), op)
            elif k == "ln_fin1b":
                b = (a["tt"] % 2) * 8
                c = lambda j: lns[:, b + j:b + j + 1]
                fin(eng.scalar_tensor_tensor(out=c(7), in0=c(2), scalar=0.0, in1=c(2),
                                             op0=AluOpType.add, op1=AluOpType.mult), op)
            elif k == "ln_fin1c":
                b = (a["tt"] % 2) * 8
                c = lambda j: lns[:, b + j:b + j + 1]
                fin(eng.scalar_tensor_tensor(out=c(3), in0=c(1), scalar=1.0 / D,
                                             in1=c(7), op0=AluOpType.mult,
                                             op1=AluOpType.subtract), op)
            elif k == "ln_fin2":
                b = (a["tt"] % 2) * 8
                c = lambda j: lns[:, b + j:b + j + 1]
                fin(eng.reciprocal(out=c(5), in_=c(4)), op)
            elif k == "ln_fin2b":
                b = (a["tt"] % 2) * 8
                c = lambda j: lns[:, b + j:b + j + 1]
                fin(eng.scalar_tensor_tensor(out=c(6), in0=c(2), scalar=-1.0, in1=c(5),
                                             op0=AluOpType.mult, op1=AluOpType.mult), op)
            elif k == "xnt_ev":
                tt, grp = a["tt"], a["grp"]
                bank = pb[6 + grp]
                i = None
                for j in range(4):
                    ic = grp * 4 + j
                    i = eng.tensor_copy(out=xnT[:, ic * N_T + tt * 128:
                                                ic * N_T + tt * 128 + 128],
                                        in_=bank[:, j * 128:(j + 1) * 128])
                fin(i, op)
            elif k == "wv_cast":
                ic = a["ic"]
                fin(eng.tensor_copy(out=Wbig[:, ic * 1024:(ic + 1) * 1024],
                                    in_=FA[:, WST:WST + 1024]), op)
            elif k == "consts_set":
                eng.memset(sonesb[:], 1.0)
                for tt in range(NTT):
                    eng.memset(bass.AP(Vsb, tt * NH * 65 + 64,
                                       [[16640, 128], [65, NH]]), 1.0)
                fin(eng.tensor_copy(out=sbiasb[:], in_=sm[0:1, C_BIAS:C_BIAS + D]), op)
            elif k == "wqk_cast":
                slot = a["slot"]
                if slot < 2:
                    dst = A2S[:, WQKB + slot * 1024: WQKB + (slot + 1) * 1024]
                else:
                    dst = Wbig[:, 2048 + (slot - 2) * 1024:
                               2048 + (slot - 1) * 1024]
                fin(eng.tensor_copy(out=dst, in_=FA[:, WST:WST + 1024]), op)
            elif k == "wo_cast":
                dc = a["dc"]
                fin(eng.tensor_copy(out=A2S[:, dc * 1024:(dc + 1) * 1024],
                                    in_=FA[:, WST:WST + 1024]), op)
            elif k == "trig_rt":
                fin(eng.tensor_scalar(out=FA[:, TRT:TRT + 1024],
                                      in0=FA[:, TRU:TRU + 1024],
                                      scalar1=MAGIC, scalar2=-MAGIC,
                                      op0=AluOpType.add, op1=AluOpType.add), op)
            elif k == "trig_frs":
                fin(eng.scalar_tensor_tensor(out=FA[:, TRF:TRF + 1024],
                                             in0=FA[:, TRU:TRU + 1024], scalar=0.0,
                                             in1=FA[:, TRT:TRT + 1024],
                                             op0=AluOpType.add,
                                             op1=AluOpType.subtract), op)
            elif k == "trig_u25":
                fin(eng.tensor_scalar_add(out=FA[:, TRU:TRU + 1024],
                                          in0=FA[:, TRU:TRU + 1024], scalar1=0.25), op)
            elif k == "trig_rt2":
                fin(eng.tensor_scalar(out=FA[:, TRT:TRT + 1024],
                                      in0=FA[:, TRU:TRU + 1024],
                                      scalar1=MAGIC, scalar2=-MAGIC,
                                      op0=AluOpType.add, op1=AluOpType.add), op)
            elif k == "trig_frc":
                fin(eng.scalar_tensor_tensor(out=FA[:, TRT:TRT + 1024],
                                             in0=FA[:, TRU:TRU + 1024], scalar=0.0,
                                             in1=FA[:, TRT:TRT + 1024],
                                             op0=AluOpType.add,
                                             op1=AluOpType.subtract), op)
            elif k == "rot_c":
                buf, tg = a["buf"], a["tg"]
                t2 = (tg % 2) * 512
                c_sl = (A2S[:, PACKC + t2:PACKC + t2 + 512] if tg < 2
                        else Wbig[:, 1024 + t2:1024 + t2 + 512])
                fin(eng.tensor_mul(out=qe1sl(buf), in0=qe1sl(buf), in1=c_sl), op)
            elif k == "rot_s":
                buf, tg = a["buf"], a["tg"]
                t2 = (tg % 2) * 512
                s_sl = (A2S[:, PACKS + t2:PACKS + t2 + 512] if tg < 2
                        else Wbig[:, t2:t2 + 512])
                fin(eng.tensor_mul(out=qe2sl(buf), in0=qe2sl(buf), in1=s_sl), op)
            elif k == "rot_add":
                sec, hp, tg, buf = a["sec"], a["hp"], a["tg"], a["buf"]
                dst = (qT if sec == 0 else kT)[:, hp * N_T + tg * 512:
                                              hp * N_T + (tg + 1) * 512]
                fin(eng.tensor_add(out=dst, in0=qe1sl(buf), in1=qe2sl(buf)), op)
            elif k == "mprime_ev":
                m, ch = a["m"], a["ch"]
                reg = TRU if m == 0 else XBUF
                fin(eng.tensor_copy(out=FA[:, reg + ch * 256: reg + (ch + 1) * 256],
                                    in_=pb[a["bank"]][:, 0:256]), op)
            elif k == "wct_cast":
                m, dcq, which = a["m"], a["dcq"], a["which"]
                src = FA[:, TRF:TRF + 512]
                if m == 0 and which == 0:
                    fin(eng.tensor_copy(out=Wbig[:, dcq * 1024: dcq * 1024 + 512],
                                        in_=src), op)
                elif m == 0 and which == 1:
                    fin(eng.tensor_copy(
                        out=Wbig[:, (4 + dcq) * 1024 + 512:(4 + dcq) * 1024 + 1024],
                        in_=src), op)
                elif m == 1 and which == 0:
                    fin(eng.tensor_copy(out=Wbig[:, dcq * 1024 + 512: dcq * 1024 + 1024],
                                        in_=src), op)
                else:
                    fin(eng.tensor_scalar_mul(
                        out=Wbig[:, (4 + dcq) * 1024:(4 + dcq) * 1024 + 512],
                        in0=src, scalar1=-1.0), op)
            elif k == "p_mask":
                slot = a["slot"]
                fin(eng.tensor_mul(out=Pbuf[:, slot * 512: slot * 512 + 128],
                                   in0=Pbuf[:, slot * 512: slot * 512 + 128],
                                   in1=smaskb[:, 0:128]), op)
            elif k == "o_ev":
                head, par = a["head"], a["par"]
                ob = pb[3 + 2 * head + par]
                fin(eng.tensor_copy(out=odiv32[0:64, head * 512:(head + 1) * 512],
                                    in_=ob[0:64, :]), op)
            elif k == "l_ev":
                head, par = a["head"], a["par"]
                ob = pb[3 + 2 * head + par]
                fin(eng.tensor_copy(out=FA[0:1, LSB + head * 512:
                                           LSB + (head + 1) * 512],
                                    in_=ob[64:65, 0:512]), op)
            elif k == "recl_hi":
                head = a["head"]
                fin(eng.tensor_copy(out=hilo[:, head * 4: head * 4 + 4],
                                    in_=FA[:, LQ + head * 4: LQ + (head + 1) * 4]), op)
            elif k == "recl_hif":
                head = a["head"]
                fin(eng.tensor_copy(out=FA[:, LQ + 8 + head * 4:
                                           LQ + 8 + (head + 1) * 4],
                                    in_=hilo[:, head * 4: head * 4 + 4]), op)
            elif k == "recl_lo":
                head = a["head"]
                fin(eng.scalar_tensor_tensor(
                    out=hilo[:, 8 + head * 4: 8 + (head + 1) * 4],
                    in0=FA[:, LQ + head * 4: LQ + (head + 1) * 4], scalar=0.0,
                    in1=FA[:, LQ + 8 + head * 4: LQ + 8 + (head + 1) * 4],
                    op0=AluOpType.add, op1=AluOpType.subtract), op)
            elif k == "recl":
                head = a["head"]
                fin(eng.reciprocal(out=FA[:, LQ + head * 4: LQ + (head + 1) * 4],
                                   in_=FA[:, LQ + head * 4: LQ + (head + 1) * 4]), op)
            elif k == "div_mult":
                hp, qg, head = a["hp"], a["qg"], a["head"]
                r0 = head * 64
                dst = qT[r0:r0 + 64, hp * N_T + qg * 512: hp * N_T + (qg + 1) * 512]
                fin(eng.tensor_mul(out=dst,
                                   in0=odiv32[0:64, head * 512:(head + 1) * 512],
                                   in1=pb[7][0:64, :]), op)
            elif k == "res_xy":
                tt, oh = a["tt"], a["oh"]
                b = tt % 2
                sl = slice(b * D + oh * 512, b * D + (oh + 1) * 512)
                fin(eng.tensor_add(out=FA[:, TRU + sl.start: TRU + sl.stop],
                                   in0=FA[:, XBUF + sl.start: XBUF + sl.stop],
                                   in1=pb[oh][:, :]), op)
            elif k == "res_f":
                tt, oh = a["tt"], a["oh"]
                b = tt % 2
                sl = slice(b * D + oh * 512, b * D + (oh + 1) * 512)
                fin(eng.tensor_add(out=FA[:, TRU + sl.start: TRU + sl.stop],
                                   in0=FA[:, TRU + sl.start: TRU + sl.stop],
                                   in1=pb[2 + oh][:, :]), op)
            else:
                raise KeyError(k)

        def e_pe(eng, op):
            k, a = op.kind, op.args
            if k == "v_mm":
                tt, ic, oh = a["tt"], a["ic"], a["oh"]
                p = pb[oh] if tt % 2 == 0 else pb[2 + oh]
                fin(eng.matmul(p[:], xnt(ic, tt * 128, 128),
                               Wbig[:, ic * 1024 + oh * 512: ic * 1024 + (oh + 1) * 512],
                               start=a["start"], stop=a["stop"]), op)
            elif k == "qk_mm":
                tg, ic, bank, slot = a["tg"], a["ic"], a["bank"], a["slot"]
                if slot < 2:
                    wsl = A2S[:, WQKB + slot * 1024 + ic * 128:
                              WQKB + slot * 1024 + (ic + 1) * 128]
                else:
                    wsl = Wbig[:, 2048 + (slot - 2) * 1024 + ic * 128:
                               2048 + (slot - 2) * 1024 + (ic + 1) * 128]
                fin(eng.matmul(pb[4 + bank][:], wsl, xnt(ic, tg * 512, 512),
                               start=a["start"], stop=a["stop"]), op)
            elif k == "xnt_tp":
                tt, grp = a["tt"], a["grp"]
                b4 = tt % 4
                bank = pb[6 + grp]
                i = None
                for j in range(4):
                    ic = grp * 4 + j
                    i = eng.transpose(bank[:, j * 128:(j + 1) * 128],
                                      FA[:, XBUF + b4 * D + ic * 128:
                                         XBUF + b4 * D + (ic + 1) * 128],
                                      FA[:, TRF:TRF + 128])
                fin(i, op)
            elif k == "mprime_mm":
                m, ch = a["m"], a["ch"]
                fin(eng.matmul(pb[a["bank"]][:, 0:256],
                               FA[0:4, XBUF + m * 1024 + ch * 128:
                                  XBUF + m * 1024 + (ch + 1) * 128],
                               FA[0:4, TRF + 512 + m * 256: TRF + 512 + (m + 1) * 256],
                               start=True, stop=True), op)
            elif k == "s_mm":
                hp, qg, kc, head, bank = a["hp"], a["qg"], a["kc"], a["head"], a["bank"]
                n, off = a["n"], a["off"]
                r0 = head * 64
                fin(eng.matmul(pb[bank][:, 0:n],
                               kT[r0:r0 + 64, hp * N_T + kc * 128:
                                  hp * N_T + (kc + 1) * 128],
                               qT[r0:r0 + 64, hp * N_T + qg * 512 + off:
                                  hp * N_T + (qg + 1) * 512],
                               start=True, stop=True), op)
            elif k == "pv_mm":
                hp, kc, head, par = a["hp"], a["kc"], a["head"], a["par"]
                slot, n, off = a["slot"], a["n"], a["off"]
                h = 2 * hp + head
                fin(eng.matmul(pb[3 + 2 * head + par][0:65, off:512],
                               Vsb[:, kc * NH * 65 + h * 65:
                                   kc * NH * 65 + (h + 1) * 65],
                               Pbuf[:, slot * 512: slot * 512 + n],
                               start=a["start"], stop=a["stop"]), op)
            elif k == "rb_bcast":
                head = a["head"]
                eng.matmul(pb[7][0:64, :], sonesb[0:1, 0:64],
                           rrows[0:1, head * 1024: head * 1024 + 512],
                           start=True, stop=False)
                fin(eng.matmul(pb[7][0:64, :], sonesb[0:1, 0:64],
                               rrows[0:1, head * 1024 + 512: (head + 1) * 1024],
                               start=False, stop=True), op)
            elif k == "wo_mm":
                tt, oh, dc = a["tt"], a["oh"], a["dc"]
                fin(eng.matmul(pb[oh][:],
                               qT[:, dc * N_T + tt * 128: dc * N_T + (tt + 1) * 128],
                               A2S[:, dc * 1024 + oh * 512: dc * 1024 + (oh + 1) * 512],
                               start=a["start"], stop=a["stop"]), op)
            elif k == "ffn_pre":
                oh = a["oh"]
                fin(eng.matmul(pb[2 + oh][:], sonesb[0:1, :],
                               sbiasb[0:1, oh * 512:(oh + 1) * 512],
                               start=True, stop=False), op)
            elif k == "ffn_mm":
                tt, oh, ic = a["tt"], a["oh"], a["ic"]
                fin(eng.matmul(pb[2 + oh][:], xnt(ic, tt * 128, 128),
                               Wbig[:, ic * 1024 + oh * 512: ic * 1024 + (oh + 1) * 512],
                               start=False, stop=a["stop"]), op)
            else:
                raise KeyError(k)

        @block.sync
        def _(eng):
            for op in sched.streams["sync"]:
                waits(eng, op)
                e_sync(eng, op)

        @block.gpsimd
        def _(eng):
            for op in sched.streams["gp"]:
                waits(eng, op)
                e_gp(eng, op)

        @block.scalar
        def _(eng):
            for op in sched.streams["act"]:
                waits(eng, op)
                e_act(eng, op)

        @block.vector
        def _(eng):
            for op in sched.streams["dve"]:
                waits(eng, op)
                e_dve(eng, op)

        @block.tensor
        def _(eng):
            for op in sched.streams["pe"]:
                waits(eng, op)
                e_pe(eng, op)

    return nc


_NC_CACHE = None


def _get_nc():
    global _NC_CACHE
    if _NC_CACHE is None:
        _NC_CACHE = build_nc()
    return _NC_CACHE


def _host_inputs(L0_attn_x, L0_attn_Wqkv, L0_attn_Wo, L0_attn_log_xi,
                 L0_attn_pi_gate_logit, L0_attn_e_gate_logit,
                 L0_ffn_G1_R, L0_ffn_G2_R, L0_ffn_G1_I, L0_ffn_G2_I,
                 L0_ffn_bias_real, L0_ffn_bias_imag):
    import ml_dtypes
    x = np.asarray(L0_attn_x, dtype=np.float32)
    Wqkv = np.asarray(L0_attn_Wqkv, dtype=np.float32)
    Wo = np.asarray(L0_attn_Wo, dtype=np.float32)

    idx = np.empty(D, dtype=np.int64)
    for h in range(NH):
        for par in range(2):
            for j in range(32):
                idx[h * 64 + par * 32 + j] = h * 64 + 2 * j + par
    WqkvT = np.ascontiguousarray(Wqkv.T)
    perm = np.concatenate([idx, D + idx, 2 * D + np.arange(D)])
    WqkvT = np.ascontiguousarray(WqkvT[:, perm])
    WoT = np.ascontiguousarray(Wo.T)

    G1R = np.asarray(L0_ffn_G1_R, dtype=np.float32)[0]
    G2R = np.asarray(L0_ffn_G2_R, dtype=np.float32)[:, :, 0, :]
    G1I = np.asarray(L0_ffn_G1_I, dtype=np.float32)[0]
    G2I = np.asarray(L0_ffn_G2_I, dtype=np.float32)[:, :, 0, :]
    g1 = np.concatenate(
        [np.ascontiguousarray(G1R.transpose(1, 0, 2)).reshape(4, 1024),
         np.ascontiguousarray(G1I.transpose(1, 0, 2)).reshape(4, 1024)], axis=1)
    g2 = np.concatenate(
        [np.ascontiguousarray(G2R.transpose(0, 2, 1)).reshape(4, 256),
         np.ascontiguousarray(G2I.transpose(0, 2, 1)).reshape(4, 256)], axis=1)

    sm = np.zeros((128, 1120), dtype=np.float32)
    pig = np.asarray(L0_attn_pi_gate_logit, dtype=np.float32)
    eg = np.asarray(L0_attn_e_gate_logit, dtype=np.float32)
    lx = np.asarray(L0_attn_log_xi, dtype=np.float32)
    for hp in range(NHP):
        sm[0:64, hp] = pig[2 * hp]
        sm[64:128, hp] = pig[2 * hp + 1]
    sm[:, 8:24] = eg[None, :]
    sm[:, 24:40] = lx[None, :]
    p = np.arange(128, dtype=np.float32)
    for i, (qg, kc) in enumerate(QGKC):
        sm[:, 40 + i] = p + 128 * kc - 512 * qg - 511
    sm[:, 95] = 1e-5
    sm[0, 96:96 + D] = np.concatenate(
        [np.asarray(L0_ffn_bias_real, np.float32),
         np.asarray(L0_ffn_bias_imag, np.float32)])

    tri = np.triu(np.ones((128, 128), np.float32))
    maskb = np.concatenate([tri, np.eye(128, dtype=np.float32)],
                           axis=1).astype(ml_dtypes.bfloat16)

    j = np.arange(32, dtype=np.float64)
    f = np.pi ** (1.0 - j / 32.0)
    t = np.arange(N_T, dtype=np.float64)
    base = np.outer(f, t) / (2 * np.pi)
    pospm = np.zeros((128, N_T), dtype=np.float32)
    for H in range(2):
        pospm[H * 64 + 0:H * 64 + 32] = -base
        pospm[H * 64 + 32:H * 64 + 64] = base
    ident = np.eye(128, dtype=np.float32)
    return x, WqkvT, WoT, g1, g2, sm, maskb, pospm, ident


def kernel(**inputs):
    x, WqkvT, WoT, g1, g2, sm, maskb, pospm, ident = _host_inputs(**inputs)
    nc = _get_nc()
    in_maps = [dict(x=np.ascontiguousarray(x[c]), wqkvT=WqkvT, woT=WoT, g1=g1,
                    g2=g2, smalls=sm, maskb=maskb, pospm=pospm, ident=ident)
               for c in range(8)]
    res = run_bass_kernel_spmd(nc, in_maps, core_ids=list(range(8)))
    return np.stack([np.asarray(r["out"]) for r in res.results],
                    axis=0).astype(np.float32)


# version 30
# speedup vs baseline: 1.0101x; 1.0101x over previous
import numpy as np

import concourse.bass as bass
import concourse.mybir as mybir
from concourse.bass_utils import run_bass_kernel_spmd
from concourse.alu_op_type import AluOpType

F32 = mybir.dt.float32
BF16 = mybir.dt.bfloat16
AF = mybir.ActivationFunctionType
MAGIC = 12582912.0

N_T = 2048
D = 1024
NH = 16
NTT = 16
NQG = 4
NHP = 8

QGKC = [(qg, kc) for qg in range(NQG) for kc in range(4 * qg + 4)]
QGKC_IDX = {p: i for i, p in enumerate(QGKC)}


def n_cols(qg, kc):
    off = max(0, kc * 128 - qg * 512)
    return 512 - off, off


class Op:
    __slots__ = ["kind", "args", "waits", "inc", "stream"]

    def __init__(self, kind, args):
        self.kind, self.args, self.waits, self.inc, self.stream = kind, args, [], False, None


class Sched:
    def __init__(self):
        self.streams = {n: [] for n in ["pe", "act", "dve", "sync", "gp"]}

    def emit(self, stream, kind, waits=(), **args):
        op = Op(kind, args)
        op.stream = stream
        for w in waits:
            if w is not None:
                op.waits.append(w)
        self.streams[stream].append(op)
        return (stream, len(self.streams[stream]) - 1)

    NDMA = {"dma_small": 3, "dma_pospm": 1, "dma_x": 1, "dma_x2": 1,
            "dma_g": 2, "dma_wv": 1, "dma_wqk": 1, "dma_wo": 1,
            "qe_swap": 4, "rearr": 4, "dma_out": 1, "final": 0,
            "xnt_dma": 4, "ldiv_d1": 1, "ldiv_d2": 2}

    def finalize(self):
        for ops in self.streams.values():
            for op in ops:
                for (sn, i) in op.waits:
                    self.streams[sn][i].inc = True
        self.incval = {}
        for name, ops in self.streams.items():
            c, vals = 0, []
            for op in ops:
                if name in ("sync", "gp") or op.kind in self.NDMA:
                    c += 16 * self.NDMA[op.kind]
                    op.inc = True
                elif op.inc:
                    c += 1
                vals.append(c)
            self.incval[name] = vals

    def wval(self, sn, i):
        return self.incval[sn][i]


def build_schedule():
    s = Sched()
    E = s.emit

    ld_small = E("sync", "dma_small")
    sig = E("act", "gates", waits=[ld_small])
    btab_a = E("dve", "btab_a", waits=[sig])

    ln_norm = [None] * NTT
    xnt_ev = [None] * NTT
    tp_last = [None] * NTT
    sqs = [None] * NTT
    tp_ops = [[None, None] for _ in range(NTT)]
    ev_ops = [[None, None] for _ in range(NTT)]
    stage_chain = None
    wv_cast = []
    wqk_cast = {}

    def emit_back(t):
        f2 = E("dve", "ln_fin2", waits=[sqs[t]], tt=t)
        f2b = E("dve", "ln_fin2b", waits=[f2], tt=t)
        ln_norm[t] = E("dve", "ln_norm", waits=[f2b], tt=t)
        if t >= 1:
            for grp in range(2):
                ev_ops[t - 1][grp] = E("dve", "xnt_ev",
                                       waits=[tp_ops[t - 1][grp]],
                                       tt=t - 1, grp=grp)
            xnt_ev[t - 1] = (ev_ops[t - 1][0], ev_ops[t - 1][1])
        for grp in range(2):
            tw = [ln_norm[t]]
            if t >= 1:
                tw.append(ev_ops[t - 1][grp])
            else:
                tw.append(ld_small)
            tp_ops[t][grp] = E("pe", "xnt_tp", waits=tw, tt=t, grp=grp)
        tp_last[t] = tp_ops[t][1]

    for tt in range(NTT):
        w = [tp_last[tt - 4]] if tt >= 4 else []
        xd = E("sync", "dma_x", waits=w, tt=tt)
        st0 = E("dve", "ln_stats", waits=[xd], tt=tt)
        f1 = E("dve", "ln_fin1", waits=[st0], tt=tt)
        f1b = E("dve", "ln_fin1b", waits=[f1], tt=tt)
        f1c = E("dve", "ln_fin1c", waits=[f1b], tt=tt)
        sqs[tt] = E("act", "ln_sqrt", waits=[f1c], tt=tt)
        if tt < 8:
            d = E("gp", "dma_wv", waits=[stage_chain], ic=tt)
            stage_chain = E("dve", "wv_cast", waits=[d], ic=tt)
            wv_cast.append(stage_chain)
        elif tt < 10:
            sec = tt - 8
            d = E("gp", "dma_wqk", waits=[stage_chain], sec=sec, hp=0)
            stage_chain = E("dve", "wqk_cast", waits=[d], sec=sec, hp=0, slot=sec)
            wqk_cast[(sec, 0)] = stage_chain
        if tt >= 1:
            emit_back(tt - 1)
    emit_back(NTT - 1)
    for grp in range(2):
        ev_ops[NTT - 1][grp] = E("dve", "xnt_ev",
                                 waits=[tp_ops[NTT - 1][grp]],
                                 tt=NTT - 1, grp=grp)
    xnt_ev[NTT - 1] = (ev_ops[NTT - 1][0], ev_ops[NTT - 1][1])
    tp_last[NTT - 1] = tp_ops[NTT - 1][1]

    ones_set = E("dve", "consts_set")
    v_ev = [None] * NTT
    for tt in range(NTT):
        last = None
        for ic in range(8):
            for oh in range(2):
                w = []
                if ic == 0 and oh == 0:
                    w = list(xnt_ev[tt]) + [wv_cast[-1]]
                    if tt >= 2:
                        w.append(v_ev[tt - 2])
                last = E("pe", "v_mm", waits=w, tt=tt, ic=ic, oh=oh,
                         start=(ic == 0), stop=(ic == 7))
        v_ev[tt] = E("dve", "v_ev", waits=[last, ones_set], tt=tt)

    ld_pospm = E("sync", "dma_pospm",
                 waits=[ln_norm[NTT - 1], tp_last[NTT - 1]])
    last_qk_sect = {}
    rot = {}
    prev_sin = prev_cos = prev_frc = None
    pack_rot_guard = {0: None, 1: None}
    qe_by_bank = [None, None, ev_ops[NTT - 1][0], ev_ops[NTT - 1][1]]
    rot_by_buf = [None, None, None, None]
    scos_by_half = {}
    g = 0
    halves = [(hp, half) for hp in range(NHP) for half in range(2)]

    def trig_full(hp, half):
        nonlocal prev_sin, prev_cos, prev_frc
        u = E("act", "trig_u", waits=[ld_pospm, sig, prev_frc], hp=hp, half=half)
        rt = E("dve", "trig_rt", waits=[u, prev_cos], hp=hp, half=half)
        fs = E("dve", "trig_frs", waits=[prev_sin, rt], hp=hp, half=half)
        sw_sin = [fs, pack_rot_guard[half]]
        if hp == 0 and half == 1:
            sw_sin += [v_ev[NTT - 1], v_ev[NTT - 2]]
        ssin = E("act", "trig_sin", waits=sw_sin, hp=hp, half=half)
        u25 = E("dve", "trig_u25", waits=[fs], hp=hp, half=half)
        rt2 = E("dve", "trig_rt2", waits=[u25], hp=hp, half=half)
        frc = E("dve", "trig_frc", waits=[ssin, rt2], hp=hp, half=half)
        scos = E("act", "trig_cos", waits=[frc], hp=hp, half=half)
        prev_sin, prev_cos, prev_frc = ssin, scos, frc
        scos_by_half[(hp, half)] = scos

    trig_full(*halves[0])
    for i, (hp, half) in enumerate(halves):
        scos = scos_by_half[(hp, half)]
        last_rot_this_half = None
        for sec in range(2):
            slot = sec if hp == 0 else 2 + ((hp - 1) * 2 + sec) % 6
            if half == 0 and hp > 0:
                d = E("gp", "dma_wqk", waits=[stage_chain], sec=sec, hp=hp)
                stage_chain = E("dve", "wqk_cast",
                                waits=[d, last_qk_sect.get((sec, hp - 3))],
                                sec=sec, hp=hp, slot=slot)
                wqk_cast[(sec, hp)] = stage_chain
            for tg in (2 * half, 2 * half + 1):
                bank = g % 4
                buf = g % 4
                w = ([wqk_cast[(sec, hp)], qe_by_bank[bank]]
                     + list(xnt_ev[4 * tg + 3]))
                last = None
                for ic in range(8):
                    last = E("pe", "qk_mm", waits=(w if ic == 0 else []),
                             sec=sec, hp=hp, tg=tg, ic=ic, bank=bank,
                             slot=slot, start=(ic == 0), stop=(ic == 7))
                last_qk_sect[(sec, hp)] = last
                qe = E("act", "qe_ev", waits=[last, rot_by_buf[buf]],
                       bank=bank, buf=buf)
                qe_by_bank[bank] = qe
                sw = E("sync", "qe_swap", waits=[qe], buf=buf)
                rc = E("dve", "rot_c", waits=[qe, scos, sw], tg=tg, buf=buf)
                rs = E("dve", "rot_s", waits=[sw], tg=tg, buf=buf)
                r3 = E("dve", "rot_add", waits=[rs, rc], sec=sec, hp=hp, tg=tg, buf=buf)
                rot[(sec, hp, tg)] = r3
                rot_by_buf[buf] = r3
                last_rot_this_half = r3
                g += 1
        pack_rot_guard[half] = last_rot_this_half
        if i + 1 < len(halves):
            trig_full(*halves[i + 1])
    a2_close = prev_cos

    ld_g = E("sync", "dma_g", waits=[a2_close, rot_by_buf[0], rot_by_buf[1],
                                     rot_by_buf[2], rot_by_buf[3]])
    mpev = []
    rearr_all = []
    wct_cast = []
    mpi = 0
    for m in range(2):
        for ch in range(8):
            bank = 4 + 2 * (mpi % 2)
            w = [ld_g]
            if mpi >= 2:
                w.append(mpev[mpi - 2])
            elif mpi == 0:
                w.append(qe_by_bank[0])
            else:
                w.append(qe_by_bank[2])
            mm = E("pe", "mprime_mm", waits=w, m=m, ch=ch, bank=bank)
            mpev.append(E("dve", "mprime_ev", waits=[mm, a2_close], m=m, ch=ch,
                          bank=bank))
            mpi += 1
        for dcq in range(4):
            rr = None
            for ci in range(8):
                rr = E("sync" if ci % 2 else "gp", "rearr",
                       waits=[mpev[m * 8 + ci],
                              (wct_cast[-1] if wct_cast else None)],
                       m=m, ci=ci, dcq=dcq)
                rearr_all.append(rr)
            wct_cast.append(E("dve", "wct_cast",
                              waits=[rearr_all[-1], rearr_all[-2],
                                     v_ev[NTT - 1], v_ev[NTT - 2]],
                              m=m, dcq=dcq, which=0))
            wct_cast.append(E("dve", "wct_cast", waits=[], m=m, dcq=dcq, which=1))
    wct_done = wct_cast[-1]

    wo_cast = []
    for dc in range(8):
        d = E("gp", "dma_wo", waits=[stage_chain if dc == 0 else wo_cast[-1],
                                     a2_close if dc == 0 else None], dc=dc)
        cw = [d]
        if dc == 0:
            cw += [a2_close, rot_by_buf[0], rot_by_buf[1], rot_by_buf[2],
                   rot_by_buf[3]] + list(xnt_ev[15]) + list(xnt_ev[14])
        wo_cast.append(E("dve", "wo_cast", waits=cw, dc=dc))

    s_hist = []
    exp_hist = []
    pv_done = []
    div_hist = []
    prev_div = {}
    prev_lev = {}
    prev_d1 = {}
    prev_d2 = {}
    prev_oev = {}
    first_pv_guard = {(0, 0): v_ev[NTT - 1], (0, 1): qe_by_bank[1],
                      (1, 0): qe_by_bank[2], (1, 1): mpev[-1]}
    rb_first_guard = qe_by_bank[3]
    pending = []
    for hp in range(NHP):
        btab = E("dve", "btab_hp",
                 waits=[btab_a, exp_hist[-1] if exp_hist else None], hp=hp)
        for qg in range(NQG):
            w = 4 * qg + 4
            gates = {}

            def emit_s(kc):
                for head in range(2):
                    n, off = n_cols(qg, kc)
                    si = len(s_hist)
                    waits = []
                    if kc == 0 and head == 0:
                        waits = [btab, rot[(1, hp, qg)], rot[(0, hp, qg)]]
                        if si < 3:
                            waits += [v_ev[NTT - 1], v_ev[NTT - 2]]
                    if si >= 3:
                        waits.append(exp_hist[si - 3])
                    smm = E("pe", "s_mm", waits=waits, hp=hp, qg=qg, kc=kc,
                            head=head, bank=si % 3, n=n, off=off)
                    s_hist.append(smm)
                    ei = len(exp_hist)
                    ew = [smm]
                    if ei >= 4:
                        ew.append(pv_done[ei - 4])
                    ex = E("act", "p_exp", waits=ew, hp=hp, qg=qg, kc=kc,
                           head=head, bank=si % 3, slot=ei % 4, n=n)
                    exp_hist.append(ex)
                    gate = ex
                    if kc >= 4 * qg:
                        gate = E("dve", "p_mask", waits=[ex], slot=ei % 4)
                    gates[(kc, head)] = (gate, ei % 4)

            def emit_pv(kc):
                n, off = n_cols(qg, kc)
                par = qg % 2
                for head in range(2):
                    gate, slot = gates[(kc, head)]
                    waits_ = [gate]
                    if kc == 0:
                        waits_.append(prev_div.get((head, par),
                                                   first_pv_guard[(head, par)]))
                    pv = E("pe", "pv_mm", waits=waits_, hp=hp, qg=qg, kc=kc,
                           head=head, par=par, slot=slot, n=n, off=off,
                           start=(kc == 0), stop=(kc == w - 1))
                    pv_done.append(pv)

            emit_s(0)
            emit_s(1)
            for kc in range(w):
                if kc == min(6, w - 1):
                    while pending:
                        args = pending.pop(0)
                        bc = E("pe", "rb_bcast",
                               waits=[args["d2"],
                                      (div_hist[-1] if div_hist else mpev[-1]),
                                      (None if div_hist else rb_first_guard)],
                               head=args["head"])
                        dv = E("dve", "div_mult", waits=[bc, args["oev"]],
                               hp=args["hp"], qg=args["qg"], head=args["head"],
                               par=args["par"])
                        div_hist.append(dv)
                        prev_div[(args["head"], args["par"])] = dv
                emit_pv(kc)
                if kc + 2 < w:
                    emit_s(kc + 2)
            par = qg % 2
            for head in range(2):
                lastA = pv_done[-1]
                dh = div_hist[-1] if div_hist else None
                lev = E("dve", "l_ev", waits=[lastA, prev_d1.get(head), dh],
                        head=head, par=par)
                oev = E("dve", "o_ev", waits=[lastA, dh], head=head, par=par)
                d1 = E("sync", "ldiv_d1", waits=[lev, prev_d2.get(head)], head=head)
                rc = E("dve", "recl", waits=[d1], head=head)
                rhi = E("dve", "recl_hi", waits=[rc], head=head)
                rhf = E("dve", "recl_hif", waits=[rhi], head=head)
                rlo = E("dve", "recl_lo", waits=[rhf], head=head)
                d2 = E("sync", "ldiv_d2", waits=[rlo, dh], head=head)
                pending.append({"d2": d2, "oev": oev, "hp": hp, "qg": qg,
                                "head": head, "par": par})
                prev_d1[head] = d1
                prev_d2[head] = d2
    while pending:
        args = pending.pop(0)
        bc = E("pe", "rb_bcast",
               waits=[args["d2"], (div_hist[-1] if div_hist else mpev[-1])],
               head=args["head"])
        dv = E("dve", "div_mult", waits=[bc, args["oev"]], hp=args["hp"],
               qg=args["qg"], head=args["head"], par=args["par"])
        div_hist.append(dv)
        prev_div[(args["head"], args["par"])] = dv
    div_all = div_hist[-1]

    res = {}
    outs = []
    for tt in range(NTT):
        w = [wct_done]
        if tt >= 2:
            w += [res[(tt - 2, 0, "xy")], res[(tt - 2, 1, "xy")]]
        x2 = E("sync", "dma_x2", waits=w, tt=tt)
        for oh in range(2):
            wlast = None
            for dc in range(8):
                ww = []
                if dc == 0:
                    ww = [wo_cast[-1], div_all]
                    if tt >= 1:
                        ww.append(res[(tt - 1, oh, "xy")])
                wlast = E("pe", "wo_mm", waits=ww, tt=tt, oh=oh, dc=dc,
                          start=(dc == 0), stop=(dc == 7))
            fw = [wct_done, div_all]
            if tt >= 1:
                fw.append(res[(tt - 1, oh, "f")])
            flast = E("pe", "ffn_pre", waits=fw, tt=tt, oh=oh)
            for ic in range(8):
                flast = E("pe", "ffn_mm", waits=[], tt=tt, oh=oh, ic=ic,
                          stop=(ic == 7))
            rw = [x2, wlast]
            if tt >= 2:
                rw.append(outs[tt - 2])
            r1 = E("dve", "res_xy", waits=rw, tt=tt, oh=oh)
            r2 = E("dve", "res_f", waits=[flast, r1], tt=tt, oh=oh)
            res[(tt, oh, "xy")] = r1
            res[(tt, oh, "f")] = r2
        outs.append(E("gp", "dma_out", waits=[res[(tt, 0, "f")], res[(tt, 1, "f")]],
                      tt=tt))
    E("gp", "final", waits=[outs[-1]])
    s.finalize()
    return s


def build_nc():
    sched = build_schedule()
    nc = bass.Bass("TRN2", target_bir_lowering=False)

    dx = nc.dram_tensor("x", [N_T, D], F32, kind="ExternalInput")
    dwqkv = nc.dram_tensor("wqkvT", [D, 3 * D], F32, kind="ExternalInput")
    dwo = nc.dram_tensor("woT", [D, D], F32, kind="ExternalInput")
    dg1 = nc.dram_tensor("g1", [4, 2048], F32, kind="ExternalInput")
    dg2 = nc.dram_tensor("g2", [4, 512], F32, kind="ExternalInput")
    dsm = nc.dram_tensor("smalls", [128, 1120], F32, kind="ExternalInput")
    dmaskb = nc.dram_tensor("maskb", [128, 256], BF16, kind="ExternalInput")
    dpospm = nc.dram_tensor("pospm", [128, N_T], F32, kind="ExternalInput")
    dident = nc.dram_tensor("ident", [128, 128], F32, kind="ExternalInput")
    dout = nc.dram_tensor("out", [N_T, D], F32, kind="ExternalOutput")

    from contextlib import ExitStack
    with ExitStack() as stk:
        en = stk.enter_context
        xnT = en(nc.sbuf_tensor("xnT", [128, 8 * N_T], BF16))
        qT = en(nc.sbuf_tensor("qT", [128, 8 * N_T], BF16))
        kT = en(nc.sbuf_tensor("kT", [128, 8 * N_T], BF16))
        Vsb = en(nc.sbuf_tensor("Vsb", [128, NTT * NH * 65], BF16))
        Wbig = en(nc.sbuf_tensor("Wbig", [128, 8192], BF16))
        Pbuf = en(nc.sbuf_tensor("Pbuf", [128, 4 * 512], BF16))
        A2S = en(nc.sbuf_tensor("A2S", [128, 8192], BF16))
        sbiasb = en(nc.sbuf_tensor("sbiasb", [1, D], BF16))
        smaskb = en(nc.sbuf_tensor("smaskb", [128, 128], BF16))
        sonesb = en(nc.sbuf_tensor("sonesb", [1, 128], BF16))
        XNTILE, WQKB, QE1, QE2, PACKC, PACKS = 0, 2048, 4096, 5120, 6144, 7168

        FA = en(nc.sbuf_tensor("FA", [128, 7264], F32))
        XBUF, TRU, TRT, TRF, WST = 0, 2048, 3072, 4096, 5120
        LSB, LQ, BTAB = 6144, 7168, 7184
        FAW = 7264
        sm = en(nc.sbuf_tensor("sm", [128, 1120], F32))
        odiv32 = en(nc.sbuf_tensor("odiv32", [64, 1024], F32))
        print("SBUF bytes remaining after alloc:", nc.sbuf_bytes_remaining)
        lns = en(nc.sbuf_tensor("lns", [128, 16], F32))
        hilo = en(nc.sbuf_tensor("hilo", [128, 16], BF16))
        rrows = en(nc.sbuf_tensor("rrows", [128, 2048], BF16))
        C_PIG, C_EG, C_LX, C_PAR, C_BIAS = 0, 8, 24, 40, 96

        pb = [en(nc.psum_tensor("pb%d" % i, [128, 512], F32)) for i in range(8)]

        sems = {n: en(nc.semaphore("sem_" + n)) for n in sched.streams}
        block = en(nc.Block())

        def waits(eng, op):
            for (sn, i) in op.waits:
                eng.wait_ge(sems[sn], sched.wval(sn, i))

        def fin(instr, op):
            if op.inc:
                instr.then_inc(sems[op.stream],
                               16 if op.stream in ("sync", "gp") else 1)

        def dinc(instr, op):
            instr.then_inc(sems[op.stream], 16)
            return instr

        def xnt(ic, t0, n):
            return xnT[:, ic * N_T + t0: ic * N_T + t0 + n]

        def qe1sl(buf, r0=0, r1=128):
            if buf < 2:
                return A2S[r0:r1, QE1 + buf * 512: QE1 + (buf + 1) * 512]
            return A2S[r0:r1, XNTILE + (buf - 2) * 512: XNTILE + (buf - 1) * 512]

        def qe2sl(buf, r0=0, r1=128):
            if buf < 2:
                return A2S[r0:r1, QE2 + buf * 512: QE2 + (buf + 1) * 512]
            return A2S[r0:r1, XNTILE + 1024 + (buf - 2) * 512:
                       XNTILE + 1024 + (buf - 1) * 512]

        def emit_xnt(eng, op):
            a = op.args
            tt, half = a["tt"], a["half"]
            b2 = tt % 2
            for ic in range(half * 4, half * 4 + 4):
                dinc(eng.dma_start_transpose(
                    xnT[:, ic * N_T + tt * 128: ic * N_T + tt * 128 + 128],
                    A2S[:, XNTILE + b2 * D + ic * 128:
                        XNTILE + b2 * D + (ic + 1) * 128]), op)

        def emit_rearr(eng, op):
            a = op.args
            m, ci, dcq = a["m"], a["ci"], a["dcq"]
            reg = TRU if m == 0 else XBUF
            for bp in range(4):
                p0 = 8 * dcq + 32 * bp
                src = FA[p0:p0 + 8, reg + ci * 256: reg + (ci + 1) * 256]
                src = src.rearrange("p (g e) -> p g e", g=16)
                c0 = TRF + (4 * ci + bp) * 16
                dst = FA[:, c0:c0 + 16]
                dinc(eng.dma_start(out=dst, in_=src), op)

        def e_sync(eng, op):
            k, a = op.kind, op.args
            if k == "dma_small":
                dinc(eng.dma_start(out=sm[:], in_=dsm[:, :]), op)
                dinc(eng.dma_start(out=smaskb[:], in_=dmaskb[:, 0:128]), op)
                dinc(eng.dma_start(out=FA[:, TRF:TRF + 128], in_=dident[:, :]), op)
            elif k == "dma_x":
                tt = a["tt"]
                b = tt % 4
                dinc(eng.dma_start(out=FA[:, XBUF + b * D: XBUF + (b + 1) * D],
                                   in_=dx[tt * 128:(tt + 1) * 128, :]), op)
            elif k == "dma_x2":
                tt = a["tt"]
                b = tt % 2
                dinc(eng.dma_start(out=FA[:, XBUF + b * D: XBUF + (b + 1) * D],
                                   in_=dx[tt * 128:(tt + 1) * 128, :]), op)
            elif k == "xnt_dma":
                emit_xnt(eng, op)
            elif k == "dma_pospm":
                dinc(eng.dma_start(out=FA[:, XBUF:XBUF + N_T], in_=dpospm[:, :]), op)
            elif k == "rearr":
                emit_rearr(eng, op)
            elif k == "xnt_dma":
                emit_xnt(eng, op)
            elif k == "qe_swap":
                buf = a["buf"]
                for blk in range(4):
                    src = [32, 0, 96, 64][blk]
                    dinc(eng.dma_start(out=qe2sl(buf, blk * 32, (blk + 1) * 32),
                                       in_=qe1sl(buf, src, src + 32)), op)
            elif k == "ldiv_d1":
                head = a["head"]
                src = FA[0:1, LSB + head * 512: LSB + (head + 1) * 512]
                dst = FA[:, LQ + head * 4: LQ + (head + 1) * 4]
                dinc(eng.dma_start(out=dst, in_=src), op)
            elif k == "ldiv_d2":
                head = a["head"]
                dinc(eng.dma_start(
                    out=rrows[0:1, head * 1024: head * 1024 + 512],
                    in_=hilo[:, head * 4: head * 4 + 4]), op)
                dinc(eng.dma_start(
                    out=rrows[0:1, head * 1024 + 512: (head + 1) * 1024],
                    in_=hilo[:, 8 + head * 4: 8 + (head + 1) * 4]), op)
            elif k == "dma_g":
                dinc(eng.dma_start(out=FA[0:4, XBUF:XBUF + 2048], in_=dg1[:, :]), op)
                dinc(eng.dma_start(out=FA[0:4, TRF + 512:TRF + 1024], in_=dg2[:, :]), op)
            else:
                raise KeyError(k)

        def e_gp(eng, op):
            k, a = op.kind, op.args
            if k == "dma_wv":
                ic = a["ic"]
                dinc(eng.dma_start(out=FA[:, WST:WST + 1024],
                                   in_=dwqkv[ic * 128:(ic + 1) * 128, 2048:3072]), op)
            elif k == "dma_wqk":
                sec, hp = a["sec"], a["hp"]
                c0 = sec * D + hp * 128
                src = bass.AP(dwqkv, c0, [[3 * D, 128], [128 * 3 * D, 8], [1, 128]])
                dinc(eng.dma_start(out=FA[:, WST:WST + 1024], in_=src), op)
            elif k == "dma_wo":
                dc = a["dc"]
                dinc(eng.dma_start(out=FA[:, WST:WST + 1024],
                                   in_=dwo[dc * 128:(dc + 1) * 128, :]), op)
            elif k == "qe_swap":
                buf = a["buf"]
                for blk in range(4):
                    src = [32, 0, 96, 64][blk]
                    dinc(eng.dma_start(out=qe2sl(buf, blk * 32, (blk + 1) * 32),
                                       in_=qe1sl(buf, src, src + 32)), op)
            elif k == "rearr":
                emit_rearr(eng, op)
            elif k == "xnt_dma":
                emit_xnt(eng, op)
            elif k == "dma_out":
                tt = a["tt"]
                dinc(eng.dma_start(out=dout[tt * 128:(tt + 1) * 128, :],
                                   in_=FA[:, TRU + (tt % 2) * D: TRU + (tt % 2 + 1) * D]), op)
            elif k == "final":
                pass
            else:
                raise KeyError(k)

        def e_act(eng, op):
            k, a = op.kind, op.args
            if k == "gates":
                eng.activation(out=sm[:, C_PIG:C_PIG + 8], in_=sm[:, C_PIG:C_PIG + 8],
                               func=AF.Sigmoid)
                eng.activation(out=sm[:, C_EG:C_EG + 16], in_=sm[:, C_EG:C_EG + 16],
                               func=AF.Sigmoid)
                fin(eng.activation(out=sm[:, C_LX:C_LX + 16], in_=sm[:, C_LX:C_LX + 16],
                                   func=AF.Exp, scale=-1.0), op)
            elif k == "xnt_dma":
                emit_xnt(eng, op)
            elif k == "ln_sqrt":
                b = (a["tt"] % 2) * 8
                fin(eng.activation(out=lns[:, b + 4:b + 5], in_=lns[:, b + 3:b + 4],
                                   func=AF.Sqrt, bias=sm[:, 95:96]), op)
            elif k == "trig_u":
                hp, half = a["hp"], a["half"]
                fin(eng.activation(out=FA[:, TRU:TRU + 1024],
                                   in_=FA[:, XBUF + half * 1024: XBUF + (half + 1) * 1024],
                                   func=AF.Copy, scale=sm[:, C_PIG + hp:C_PIG + hp + 1]), op)
            elif k == "trig_sin":
                out = (A2S[:, PACKS:PACKS + 1024] if a["half"] == 0
                       else Wbig[:, 0:1024])
                fin(eng.activation(out=out, in_=FA[:, TRF:TRF + 1024],
                                   func=AF.Sin, scale=2 * np.pi), op)
            elif k == "trig_cos":
                out = (A2S[:, PACKC:PACKC + 1024] if a["half"] == 0
                       else Wbig[:, 1024:2048])
                fin(eng.activation(out=out, in_=FA[:, TRT:TRT + 1024],
                                   func=AF.Sin, scale=2 * np.pi), op)
            elif k == "qe_ev":
                bank, buf = a["bank"], a["buf"]
                fin(eng.copy(out=qe1sl(buf), in_=pb[4 + bank][:]), op)
            elif k == "p_exp":
                bank, slot, n = a["bank"], a["slot"], a["n"]
                col = BTAB + a["head"] * 40 + QGKC_IDX[(a["qg"], a["kc"])]
                fin(eng.activation(out=Pbuf[:, slot * 512: slot * 512 + n],
                                   in_=pb[bank][:, 0:n], func=AF.Exp, scale=0.125,
                                   bias=FA[:, col:col + 1]), op)
            else:
                raise KeyError(k)

        def e_dve(eng, op):
            k, a = op.kind, op.args
            if k == "btab_a":
                fin(eng.tensor_mul(out=sm[:, C_EG:C_EG + 16], in0=sm[:, C_EG:C_EG + 16],
                                   in1=sm[:, C_LX:C_LX + 16]), op)
            elif k == "btab_hp":
                hp = a["hp"]
                for head in range(2):
                    h = 2 * hp + head
                    i = eng.tensor_scalar_mul(out=FA[:, BTAB + head * 40:
                                                     BTAB + (head + 1) * 40],
                                              in0=sm[:, C_PAR:C_PAR + 40],
                                              scalar1=sm[:, C_EG + h:C_EG + h + 1])
                fin(i, op)
            elif k == "v_ev":
                tt = a["tt"]
                p0, p1 = (pb[0], pb[1]) if tt % 2 == 0 else (pb[2], pb[3])
                eng.tensor_copy(out=bass.AP(Vsb, tt * NH * 65,
                                            [[16640, 128], [65, 8], [1, 64]]),
                                in_=p0[:])
                fin(eng.tensor_copy(out=bass.AP(Vsb, tt * NH * 65 + 8 * 65,
                                                [[16640, 128], [65, 8], [1, 64]]),
                                    in_=p1[:]), op)
            elif k == "ln_norm":
                tt = a["tt"]
                b4, b = tt % 4, (tt % 2) * 8
                sl = FA[:, XBUF + b4 * D: XBUF + (b4 + 1) * D]
                fin(eng.tensor_scalar(out=sl, in0=sl,
                                      scalar1=lns[:, b + 5:b + 6],
                                      scalar2=lns[:, b + 6:b + 7],
                                      op0=AluOpType.mult, op1=AluOpType.add), op)
            elif k == "ln_stats":
                tt = a["tt"]
                b4, b = tt % 4, (tt % 2) * 8
                xa = FA[:, XBUF + b4 * D: XBUF + (b4 + 1) * D]
                eng.reduce_sum(out=lns[:, b:b + 1], in_=xa, axis=mybir.AxisListType.X)
                fin(eng.scalar_tensor_tensor(out=Pbuf[:, 0:1024], in0=xa,
                                             scalar=1.0, in1=xa, op0=AluOpType.mult,
                                             op1=AluOpType.mult,
                                             accum_out=lns[:, b + 1:b + 2]), op)
            elif k == "ln_fin1":
                b = (a["tt"] % 2) * 8
                c = lambda j: lns[:, b + j:b + j + 1]
                fin(eng.tensor_scalar_mul(out=c(2), in0=c(0), scalar1=1.0 / D), op)
            elif k == "ln_fin1b":
                b = (a["tt"] % 2) * 8
                c = lambda j: lns[:, b + j:b + j + 1]
                fin(eng.scalar_tensor_tensor(out=c(7), in0=c(2), scalar=0.0, in1=c(2),
                                             op0=AluOpType.add, op1=AluOpType.mult), op)
            elif k == "ln_fin1c":
                b = (a["tt"] % 2) * 8
                c = lambda j: lns[:, b + j:b + j + 1]
                fin(eng.scalar_tensor_tensor(out=c(3), in0=c(1), scalar=1.0 / D,
                                             in1=c(7), op0=AluOpType.mult,
                                             op1=AluOpType.subtract), op)
            elif k == "ln_fin2":
                b = (a["tt"] % 2) * 8
                c = lambda j: lns[:, b + j:b + j + 1]
                fin(eng.reciprocal(out=c(5), in_=c(4)), op)
            elif k == "ln_fin2b":
                b = (a["tt"] % 2) * 8
                c = lambda j: lns[:, b + j:b + j + 1]
                fin(eng.scalar_tensor_tensor(out=c(6), in0=c(2), scalar=-1.0, in1=c(5),
                                             op0=AluOpType.mult, op1=AluOpType.mult), op)
            elif k == "xnt_ev":
                tt, grp = a["tt"], a["grp"]
                bank = pb[6 + grp]
                i = None
                for j in range(4):
                    ic = grp * 4 + j
                    i = eng.tensor_copy(out=xnT[:, ic * N_T + tt * 128:
                                                ic * N_T + tt * 128 + 128],
                                        in_=bank[:, j * 128:(j + 1) * 128])
                fin(i, op)
            elif k == "wv_cast":
                ic = a["ic"]
                fin(eng.tensor_copy(out=Wbig[:, ic * 1024:(ic + 1) * 1024],
                                    in_=FA[:, WST:WST + 1024]), op)
            elif k == "consts_set":
                eng.memset(sonesb[:], 1.0)
                for tt in range(NTT):
                    eng.memset(bass.AP(Vsb, tt * NH * 65 + 64,
                                       [[16640, 128], [65, NH]]), 1.0)
                fin(eng.tensor_copy(out=sbiasb[:], in_=sm[0:1, C_BIAS:C_BIAS + D]), op)
            elif k == "wqk_cast":
                slot = a["slot"]
                if slot < 2:
                    dst = A2S[:, WQKB + slot * 1024: WQKB + (slot + 1) * 1024]
                else:
                    dst = Wbig[:, 2048 + (slot - 2) * 1024:
                               2048 + (slot - 1) * 1024]
                fin(eng.tensor_copy(out=dst, in_=FA[:, WST:WST + 1024]), op)
            elif k == "wo_cast":
                dc = a["dc"]
                fin(eng.tensor_copy(out=A2S[:, dc * 1024:(dc + 1) * 1024],
                                    in_=FA[:, WST:WST + 1024]), op)
            elif k == "trig_rt":
                fin(eng.tensor_scalar(out=FA[:, TRT:TRT + 1024],
                                      in0=FA[:, TRU:TRU + 1024],
                                      scalar1=MAGIC, scalar2=-MAGIC,
                                      op0=AluOpType.add, op1=AluOpType.add), op)
            elif k == "trig_frs":
                fin(eng.scalar_tensor_tensor(out=FA[:, TRF:TRF + 1024],
                                             in0=FA[:, TRU:TRU + 1024], scalar=0.0,
                                             in1=FA[:, TRT:TRT + 1024],
                                             op0=AluOpType.add,
                                             op1=AluOpType.subtract), op)
            elif k == "trig_u25":
                fin(eng.tensor_scalar_add(out=FA[:, TRU:TRU + 1024],
                                          in0=FA[:, TRU:TRU + 1024], scalar1=0.25), op)
            elif k == "trig_rt2":
                fin(eng.tensor_scalar(out=FA[:, TRT:TRT + 1024],
                                      in0=FA[:, TRU:TRU + 1024],
                                      scalar1=MAGIC, scalar2=-MAGIC,
                                      op0=AluOpType.add, op1=AluOpType.add), op)
            elif k == "trig_frc":
                fin(eng.scalar_tensor_tensor(out=FA[:, TRT:TRT + 1024],
                                             in0=FA[:, TRU:TRU + 1024], scalar=0.0,
                                             in1=FA[:, TRT:TRT + 1024],
                                             op0=AluOpType.add,
                                             op1=AluOpType.subtract), op)
            elif k == "rot_c":
                buf, tg = a["buf"], a["tg"]
                t2 = (tg % 2) * 512
                c_sl = (A2S[:, PACKC + t2:PACKC + t2 + 512] if tg < 2
                        else Wbig[:, 1024 + t2:1024 + t2 + 512])
                fin(eng.tensor_mul(out=qe1sl(buf), in0=qe1sl(buf), in1=c_sl), op)
            elif k == "rot_s":
                buf, tg = a["buf"], a["tg"]
                t2 = (tg % 2) * 512
                s_sl = (A2S[:, PACKS + t2:PACKS + t2 + 512] if tg < 2
                        else Wbig[:, t2:t2 + 512])
                fin(eng.tensor_mul(out=qe2sl(buf), in0=qe2sl(buf), in1=s_sl), op)
            elif k == "rot_add":
                sec, hp, tg, buf = a["sec"], a["hp"], a["tg"], a["buf"]
                dst = (qT if sec == 0 else kT)[:, hp * N_T + tg * 512:
                                              hp * N_T + (tg + 1) * 512]
                fin(eng.tensor_add(out=dst, in0=qe1sl(buf), in1=qe2sl(buf)), op)
            elif k == "mprime_ev":
                m, ch = a["m"], a["ch"]
                reg = TRU if m == 0 else XBUF
                fin(eng.tensor_copy(out=FA[:, reg + ch * 256: reg + (ch + 1) * 256],
                                    in_=pb[a["bank"]][:, 0:256]), op)
            elif k == "wct_cast":
                m, dcq, which = a["m"], a["dcq"], a["which"]
                src = FA[:, TRF:TRF + 512]
                if m == 0 and which == 0:
                    fin(eng.tensor_copy(out=Wbig[:, dcq * 1024: dcq * 1024 + 512],
                                        in_=src), op)
                elif m == 0 and which == 1:
                    fin(eng.tensor_copy(
                        out=Wbig[:, (4 + dcq) * 1024 + 512:(4 + dcq) * 1024 + 1024],
                        in_=src), op)
                elif m == 1 and which == 0:
                    fin(eng.tensor_copy(out=Wbig[:, dcq * 1024 + 512: dcq * 1024 + 1024],
                                        in_=src), op)
                else:
                    fin(eng.tensor_scalar_mul(
                        out=Wbig[:, (4 + dcq) * 1024:(4 + dcq) * 1024 + 512],
                        in0=src, scalar1=-1.0), op)
            elif k == "p_mask":
                slot = a["slot"]
                fin(eng.tensor_mul(out=Pbuf[:, slot * 512: slot * 512 + 128],
                                   in0=Pbuf[:, slot * 512: slot * 512 + 128],
                                   in1=smaskb[:, 0:128]), op)
            elif k == "o_ev":
                head, par = a["head"], a["par"]
                ob = pb[3 + 2 * head + par]
                fin(eng.tensor_copy(out=odiv32[0:64, head * 512:(head + 1) * 512],
                                    in_=ob[0:64, :]), op)
            elif k == "l_ev":
                head, par = a["head"], a["par"]
                ob = pb[3 + 2 * head + par]
                fin(eng.tensor_copy(out=FA[0:1, LSB + head * 512:
                                           LSB + (head + 1) * 512],
                                    in_=ob[64:65, 0:512]), op)
            elif k == "recl_hi":
                head = a["head"]
                fin(eng.tensor_copy(out=hilo[:, head * 4: head * 4 + 4],
                                    in_=FA[:, LQ + head * 4: LQ + (head + 1) * 4]), op)
            elif k == "recl_hif":
                head = a["head"]
                fin(eng.tensor_copy(out=FA[:, LQ + 8 + head * 4:
                                           LQ + 8 + (head + 1) * 4],
                                    in_=hilo[:, head * 4: head * 4 + 4]), op)
            elif k == "recl_lo":
                head = a["head"]
                fin(eng.scalar_tensor_tensor(
                    out=hilo[:, 8 + head * 4: 8 + (head + 1) * 4],
                    in0=FA[:, LQ + head * 4: LQ + (head + 1) * 4], scalar=0.0,
                    in1=FA[:, LQ + 8 + head * 4: LQ + 8 + (head + 1) * 4],
                    op0=AluOpType.add, op1=AluOpType.subtract), op)
            elif k == "recl":
                head = a["head"]
                fin(eng.reciprocal(out=FA[:, LQ + head * 4: LQ + (head + 1) * 4],
                                   in_=FA[:, LQ + head * 4: LQ + (head + 1) * 4]), op)
            elif k == "div_mult":
                hp, qg, head = a["hp"], a["qg"], a["head"]
                r0 = head * 64
                dst = qT[r0:r0 + 64, hp * N_T + qg * 512: hp * N_T + (qg + 1) * 512]
                fin(eng.tensor_mul(out=dst,
                                   in0=odiv32[0:64, head * 512:(head + 1) * 512],
                                   in1=pb[7][0:64, :]), op)
            elif k == "res_xy":
                tt, oh = a["tt"], a["oh"]
                b = tt % 2
                sl = slice(b * D + oh * 512, b * D + (oh + 1) * 512)
                fin(eng.tensor_add(out=FA[:, TRU + sl.start: TRU + sl.stop],
                                   in0=FA[:, XBUF + sl.start: XBUF + sl.stop],
                                   in1=pb[oh][:, :]), op)
            elif k == "res_f":
                tt, oh = a["tt"], a["oh"]
                b = tt % 2
                sl = slice(b * D + oh * 512, b * D + (oh + 1) * 512)
                fin(eng.tensor_add(out=FA[:, TRU + sl.start: TRU + sl.stop],
                                   in0=FA[:, TRU + sl.start: TRU + sl.stop],
                                   in1=pb[2 + oh][:, :]), op)
            else:
                raise KeyError(k)

        def e_pe(eng, op):
            k, a = op.kind, op.args
            if k == "v_mm":
                tt, ic, oh = a["tt"], a["ic"], a["oh"]
                p = pb[oh] if tt % 2 == 0 else pb[2 + oh]
                fin(eng.matmul(p[:], xnt(ic, tt * 128, 128),
                               Wbig[:, ic * 1024 + oh * 512: ic * 1024 + (oh + 1) * 512],
                               start=a["start"], stop=a["stop"]), op)
            elif k == "qk_mm":
                tg, ic, bank, slot = a["tg"], a["ic"], a["bank"], a["slot"]
                if slot < 2:
                    wsl = A2S[:, WQKB + slot * 1024 + ic * 128:
                              WQKB + slot * 1024 + (ic + 1) * 128]
                else:
                    wsl = Wbig[:, 2048 + (slot - 2) * 1024 + ic * 128:
                               2048 + (slot - 2) * 1024 + (ic + 1) * 128]
                fin(eng.matmul(pb[4 + bank][:], wsl, xnt(ic, tg * 512, 512),
                               start=a["start"], stop=a["stop"]), op)
            elif k == "xnt_tp":
                tt, grp = a["tt"], a["grp"]
                b4 = tt % 4
                bank = pb[6 + grp]
                i = None
                for j in range(4):
                    ic = grp * 4 + j
                    i = eng.transpose(bank[:, j * 128:(j + 1) * 128],
                                      FA[:, XBUF + b4 * D + ic * 128:
                                         XBUF + b4 * D + (ic + 1) * 128],
                                      FA[:, TRF:TRF + 128])
                fin(i, op)
            elif k == "mprime_mm":
                m, ch = a["m"], a["ch"]
                fin(eng.matmul(pb[a["bank"]][:, 0:256],
                               FA[0:4, XBUF + m * 1024 + ch * 128:
                                  XBUF + m * 1024 + (ch + 1) * 128],
                               FA[0:4, TRF + 512 + m * 256: TRF + 512 + (m + 1) * 256],
                               start=True, stop=True), op)
            elif k == "s_mm":
                hp, qg, kc, head, bank = a["hp"], a["qg"], a["kc"], a["head"], a["bank"]
                n, off = a["n"], a["off"]
                r0 = head * 64
                fin(eng.matmul(pb[bank][:, 0:n],
                               kT[r0:r0 + 64, hp * N_T + kc * 128:
                                  hp * N_T + (kc + 1) * 128],
                               qT[r0:r0 + 64, hp * N_T + qg * 512 + off:
                                  hp * N_T + (qg + 1) * 512],
                               start=True, stop=True), op)
            elif k == "pv_mm":
                hp, kc, head, par = a["hp"], a["kc"], a["head"], a["par"]
                slot, n, off = a["slot"], a["n"], a["off"]
                h = 2 * hp + head
                fin(eng.matmul(pb[3 + 2 * head + par][0:65, off:512],
                               Vsb[:, kc * NH * 65 + h * 65:
                                   kc * NH * 65 + (h + 1) * 65],
                               Pbuf[:, slot * 512: slot * 512 + n],
                               start=a["start"], stop=a["stop"]), op)
            elif k == "rb_bcast":
                head = a["head"]
                eng.matmul(pb[7][0:64, :], sonesb[0:1, 0:64],
                           rrows[0:1, head * 1024: head * 1024 + 512],
                           start=True, stop=False)
                fin(eng.matmul(pb[7][0:64, :], sonesb[0:1, 0:64],
                               rrows[0:1, head * 1024 + 512: (head + 1) * 1024],
                               start=False, stop=True), op)
            elif k == "wo_mm":
                tt, oh, dc = a["tt"], a["oh"], a["dc"]
                fin(eng.matmul(pb[oh][:],
                               qT[:, dc * N_T + tt * 128: dc * N_T + (tt + 1) * 128],
                               A2S[:, dc * 1024 + oh * 512: dc * 1024 + (oh + 1) * 512],
                               start=a["start"], stop=a["stop"]), op)
            elif k == "ffn_pre":
                oh = a["oh"]
                fin(eng.matmul(pb[2 + oh][:], sonesb[0:1, :],
                               sbiasb[0:1, oh * 512:(oh + 1) * 512],
                               start=True, stop=False), op)
            elif k == "ffn_mm":
                tt, oh, ic = a["tt"], a["oh"], a["ic"]
                fin(eng.matmul(pb[2 + oh][:], xnt(ic, tt * 128, 128),
                               Wbig[:, ic * 1024 + oh * 512: ic * 1024 + (oh + 1) * 512],
                               start=False, stop=a["stop"]), op)
            else:
                raise KeyError(k)

        @block.sync
        def _(eng):
            for op in sched.streams["sync"]:
                waits(eng, op)
                e_sync(eng, op)

        @block.gpsimd
        def _(eng):
            for op in sched.streams["gp"]:
                waits(eng, op)
                e_gp(eng, op)

        @block.scalar
        def _(eng):
            for op in sched.streams["act"]:
                waits(eng, op)
                e_act(eng, op)

        @block.vector
        def _(eng):
            for op in sched.streams["dve"]:
                waits(eng, op)
                e_dve(eng, op)

        @block.tensor
        def _(eng):
            for op in sched.streams["pe"]:
                waits(eng, op)
                e_pe(eng, op)

    return nc


_NC_CACHE = None


def _get_nc():
    global _NC_CACHE
    if _NC_CACHE is None:
        _NC_CACHE = build_nc()
    return _NC_CACHE


def _host_inputs(L0_attn_x, L0_attn_Wqkv, L0_attn_Wo, L0_attn_log_xi,
                 L0_attn_pi_gate_logit, L0_attn_e_gate_logit,
                 L0_ffn_G1_R, L0_ffn_G2_R, L0_ffn_G1_I, L0_ffn_G2_I,
                 L0_ffn_bias_real, L0_ffn_bias_imag):
    import ml_dtypes
    x = np.asarray(L0_attn_x, dtype=np.float32)
    Wqkv = np.asarray(L0_attn_Wqkv, dtype=np.float32)
    Wo = np.asarray(L0_attn_Wo, dtype=np.float32)

    idx = np.empty(D, dtype=np.int64)
    for h in range(NH):
        for par in range(2):
            for j in range(32):
                idx[h * 64 + par * 32 + j] = h * 64 + 2 * j + par
    WqkvT = np.ascontiguousarray(Wqkv.T)
    perm = np.concatenate([idx, D + idx, 2 * D + np.arange(D)])
    WqkvT = np.ascontiguousarray(WqkvT[:, perm])
    WoT = np.ascontiguousarray(Wo.T)

    G1R = np.asarray(L0_ffn_G1_R, dtype=np.float32)[0]
    G2R = np.asarray(L0_ffn_G2_R, dtype=np.float32)[:, :, 0, :]
    G1I = np.asarray(L0_ffn_G1_I, dtype=np.float32)[0]
    G2I = np.asarray(L0_ffn_G2_I, dtype=np.float32)[:, :, 0, :]
    g1 = np.concatenate(
        [np.ascontiguousarray(G1R.transpose(1, 0, 2)).reshape(4, 1024),
         np.ascontiguousarray(G1I.transpose(1, 0, 2)).reshape(4, 1024)], axis=1)
    g2 = np.concatenate(
        [np.ascontiguousarray(G2R.transpose(0, 2, 1)).reshape(4, 256),
         np.ascontiguousarray(G2I.transpose(0, 2, 1)).reshape(4, 256)], axis=1)

    sm = np.zeros((128, 1120), dtype=np.float32)
    pig = np.asarray(L0_attn_pi_gate_logit, dtype=np.float32)
    eg = np.asarray(L0_attn_e_gate_logit, dtype=np.float32)
    lx = np.asarray(L0_attn_log_xi, dtype=np.float32)
    for hp in range(NHP):
        sm[0:64, hp] = pig[2 * hp]
        sm[64:128, hp] = pig[2 * hp + 1]
    sm[:, 8:24] = eg[None, :]
    sm[:, 24:40] = lx[None, :]
    p = np.arange(128, dtype=np.float32)
    for i, (qg, kc) in enumerate(QGKC):
        sm[:, 40 + i] = p + 128 * kc - 512 * qg - 511
    sm[:, 95] = 1e-5
    sm[0, 96:96 + D] = np.concatenate(
        [np.asarray(L0_ffn_bias_real, np.float32),
         np.asarray(L0_ffn_bias_imag, np.float32)])

    tri = np.triu(np.ones((128, 128), np.float32))
    maskb = np.concatenate([tri, np.eye(128, dtype=np.float32)],
                           axis=1).astype(ml_dtypes.bfloat16)

    j = np.arange(32, dtype=np.float64)
    f = np.pi ** (1.0 - j / 32.0)
    t = np.arange(N_T, dtype=np.float64)
    base = np.outer(f, t) / (2 * np.pi)
    pospm = np.zeros((128, N_T), dtype=np.float32)
    for H in range(2):
        pospm[H * 64 + 0:H * 64 + 32] = -base
        pospm[H * 64 + 32:H * 64 + 64] = base
    ident = np.eye(128, dtype=np.float32)
    return x, WqkvT, WoT, g1, g2, sm, maskb, pospm, ident


def kernel(**inputs):
    x, WqkvT, WoT, g1, g2, sm, maskb, pospm, ident = _host_inputs(**inputs)
    nc = _get_nc()
    in_maps = [dict(x=np.ascontiguousarray(x[c]), wqkvT=WqkvT, woT=WoT, g1=g1,
                    g2=g2, smalls=sm, maskb=maskb, pospm=pospm, ident=ident)
               for c in range(8)]
    res = run_bass_kernel_spmd(nc, in_maps, core_ids=list(range(8)))
    return np.stack([np.asarray(r["out"]) for r in res.results],
                    axis=0).astype(np.float32)


# version 33
# speedup vs baseline: 1.0160x; 1.0058x over previous
import numpy as np

import concourse.bass as bass
import concourse.mybir as mybir
from concourse.bass_utils import run_bass_kernel_spmd
from concourse.alu_op_type import AluOpType

F32 = mybir.dt.float32
BF16 = mybir.dt.bfloat16
AF = mybir.ActivationFunctionType
MAGIC = 12582912.0

N_T = 2048
D = 1024
NH = 16
NTT = 16
NQG = 4
NHP = 8

QGKC = [(qg, kc) for qg in range(NQG) for kc in range(4 * qg + 4)]
QGKC_IDX = {p: i for i, p in enumerate(QGKC)}


def n_cols(qg, kc):
    off = max(0, kc * 128 - qg * 512)
    return 512 - off, off


class Op:
    __slots__ = ["kind", "args", "waits", "inc", "stream"]

    def __init__(self, kind, args):
        self.kind, self.args, self.waits, self.inc, self.stream = kind, args, [], False, None


class Sched:
    def __init__(self):
        self.streams = {n: [] for n in ["pe", "act", "dve", "sync", "gp"]}

    def emit(self, stream, kind, waits=(), **args):
        op = Op(kind, args)
        op.stream = stream
        for w in waits:
            if w is not None:
                op.waits.append(w)
        self.streams[stream].append(op)
        return (stream, len(self.streams[stream]) - 1)

    NDMA = {"dma_small": 3, "dma_pospm": 1, "dma_x": 1, "dma_x2": 1,
            "dma_g": 2, "dma_wv": 1, "dma_wqk": 1, "dma_wo": 1,
            "qe_swap": 4, "rearr": 4, "dma_out": 1, "final": 0,
            "xnt_dma": 4, "ldiv_d1": 1, "ldiv_d2": 2}

    def finalize(self):
        for ops in self.streams.values():
            for op in ops:
                for (sn, i) in op.waits:
                    self.streams[sn][i].inc = True
        self.incval = {}
        for name, ops in self.streams.items():
            c, vals = 0, []
            for op in ops:
                if name in ("sync", "gp") or op.kind in self.NDMA:
                    c += 16 * self.NDMA[op.kind]
                    op.inc = True
                elif op.inc:
                    c += 1
                vals.append(c)
            self.incval[name] = vals

    def wval(self, sn, i):
        return self.incval[sn][i]


def build_schedule():
    s = Sched()
    E = s.emit

    ld_small = E("sync", "dma_small")
    sig = E("act", "gates", waits=[ld_small])
    btab_a = E("dve", "btab_a", waits=[sig])

    ln_norm = [None] * NTT
    xnt_ev = [None] * NTT
    tp_last = [None] * NTT
    sqs = [None] * NTT
    tp_ops = [[None, None] for _ in range(NTT)]
    ev_ops = [[None, None] for _ in range(NTT)]
    stage_chain = None
    wv_cast = []
    wqk_cast = {}

    def emit_back(t):
        f2 = E("dve", "ln_fin2", waits=[sqs[t]], tt=t)
        f2b = E("dve", "ln_fin2b", waits=[f2], tt=t)
        ln_norm[t] = E("dve", "ln_norm", waits=[f2b], tt=t)
        if t >= 1:
            for grp in range(2):
                ev_ops[t - 1][grp] = E("dve", "xnt_ev",
                                       waits=[tp_ops[t - 1][grp]],
                                       tt=t - 1, grp=grp)
            xnt_ev[t - 1] = (ev_ops[t - 1][0], ev_ops[t - 1][1])
        for grp in range(2):
            tw = [ln_norm[t]]
            if t >= 1:
                tw.append(ev_ops[t - 1][grp])
            else:
                tw.append(ld_small)
            tp_ops[t][grp] = E("pe", "xnt_tp", waits=tw, tt=t, grp=grp)
        tp_last[t] = tp_ops[t][1]

    for tt in range(NTT):
        w = [tp_last[tt - 4]] if tt >= 4 else []
        xd = E("sync", "dma_x", waits=w, tt=tt)
        st0 = E("dve", "ln_stats", waits=[xd], tt=tt)
        f1 = E("dve", "ln_fin1", waits=[st0], tt=tt)
        f1b = E("dve", "ln_fin1b", waits=[f1], tt=tt)
        f1c = E("dve", "ln_fin1c", waits=[f1b], tt=tt)
        sqs[tt] = E("act", "ln_sqrt", waits=[f1c], tt=tt)
        if tt < 8:
            d = E("gp", "dma_wv", waits=[stage_chain], ic=tt)
            stage_chain = E("dve", "wv_cast", waits=[d], ic=tt)
            wv_cast.append(stage_chain)
        elif tt < 10:
            sec = tt - 8
            d = E("gp", "dma_wqk", waits=[stage_chain], sec=sec, hp=0)
            stage_chain = E("dve", "wqk_cast", waits=[d], sec=sec, hp=0, slot=sec)
            wqk_cast[(sec, 0)] = stage_chain
        if tt >= 1:
            emit_back(tt - 1)
    emit_back(NTT - 1)
    for grp in range(2):
        ev_ops[NTT - 1][grp] = E("dve", "xnt_ev",
                                 waits=[tp_ops[NTT - 1][grp]],
                                 tt=NTT - 1, grp=grp)
    xnt_ev[NTT - 1] = (ev_ops[NTT - 1][0], ev_ops[NTT - 1][1])
    tp_last[NTT - 1] = tp_ops[NTT - 1][1]

    ones_set = E("dve", "consts_set")
    v_ev = [None] * NTT
    for tt in range(NTT):
        last = None
        for ic in range(8):
            for oh in range(2):
                w = []
                if ic == 0 and oh == 0:
                    w = list(xnt_ev[tt]) + [wv_cast[-1]]
                    if tt >= 2:
                        w.append(v_ev[tt - 2])
                last = E("pe", "v_mm", waits=w, tt=tt, ic=ic, oh=oh,
                         start=(ic == 0), stop=(ic == 7))
        v_ev[tt] = E("dve", "v_ev", waits=[last, ones_set], tt=tt)

    ld_pospm = E("sync", "dma_pospm",
                 waits=[ln_norm[NTT - 1], tp_last[NTT - 1]])
    last_qk_sect = {}
    rot = {}
    prev_sin = prev_cos = prev_frc = None
    pack_rot_guard = {0: None, 1: None}
    qe_by_bank = [None, None, ev_ops[NTT - 1][0], ev_ops[NTT - 1][1]]
    rot_by_buf = [None, None, None, None]
    scos_by_half = {}
    g = 0
    halves = [(hp, half) for hp in range(NHP) for half in range(2)]

    def trig_full(hp, half):
        nonlocal prev_sin, prev_cos, prev_frc
        u = E("act", "trig_u", waits=[ld_pospm, sig, prev_frc], hp=hp, half=half)
        rt = E("dve", "trig_rt", waits=[u, prev_cos], hp=hp, half=half)
        fs = E("dve", "trig_frs", waits=[prev_sin, rt], hp=hp, half=half)
        sw_sin = [fs, pack_rot_guard[half]]
        if hp == 0 and half == 1:
            sw_sin += [v_ev[NTT - 1], v_ev[NTT - 2]]
        ssin = E("act", "trig_sin", waits=sw_sin, hp=hp, half=half)
        u25 = E("dve", "trig_u25", waits=[fs], hp=hp, half=half)
        rt2 = E("dve", "trig_rt2", waits=[u25], hp=hp, half=half)
        frc = E("dve", "trig_frc", waits=[ssin, rt2], hp=hp, half=half)
        scos = E("act", "trig_cos", waits=[frc], hp=hp, half=half)
        prev_sin, prev_cos, prev_frc = ssin, scos, frc
        scos_by_half[(hp, half)] = scos

    trig_full(*halves[0])
    for i, (hp, half) in enumerate(halves):
        scos = scos_by_half[(hp, half)]
        last_rot_this_half = None
        for sec in range(2):
            slot = sec if hp == 0 else 2 + ((hp - 1) * 2 + sec) % 6
            if half == 0 and hp > 0:
                d = E("gp", "dma_wqk", waits=[stage_chain], sec=sec, hp=hp)
                stage_chain = E("dve", "wqk_cast",
                                waits=[d, last_qk_sect.get((sec, hp - 3))],
                                sec=sec, hp=hp, slot=slot)
                wqk_cast[(sec, hp)] = stage_chain
            for tg in (2 * half, 2 * half + 1):
                bank = g % 4
                buf = g % 4
                w = ([wqk_cast[(sec, hp)], qe_by_bank[bank]]
                     + list(xnt_ev[4 * tg + 3]))
                last = None
                for ic in range(8):
                    last = E("pe", "qk_mm", waits=(w if ic == 0 else []),
                             sec=sec, hp=hp, tg=tg, ic=ic, bank=bank,
                             slot=slot, start=(ic == 0), stop=(ic == 7))
                last_qk_sect[(sec, hp)] = last
                qe = E("act", "qe_ev", waits=[last, rot_by_buf[buf]],
                       bank=bank, buf=buf)
                qe_by_bank[bank] = qe
                sw = E("sync", "qe_swap", waits=[qe], buf=buf)
                rc = E("dve", "rot_c", waits=[qe, scos, sw], tg=tg, buf=buf)
                rs = E("dve", "rot_s", waits=[sw], tg=tg, buf=buf)
                r3 = E("dve", "rot_add", waits=[rs, rc], sec=sec, hp=hp, tg=tg, buf=buf)
                rot[(sec, hp, tg)] = r3
                rot_by_buf[buf] = r3
                last_rot_this_half = r3
                g += 1
        pack_rot_guard[half] = last_rot_this_half
        if i + 1 < len(halves):
            trig_full(*halves[i + 1])
    a2_close = prev_cos

    ld_g = E("sync", "dma_g", waits=[a2_close, rot_by_buf[0], rot_by_buf[1],
                                     rot_by_buf[2], rot_by_buf[3]])
    mpev = []
    rearr_all = []
    wct_cast = []
    mpi = 0
    for m in range(2):
        for ch in range(8):
            bank = 4 + 2 * (mpi % 2)
            w = [ld_g]
            if mpi >= 2:
                w.append(mpev[mpi - 2])
            elif mpi == 0:
                w.append(qe_by_bank[0])
            else:
                w.append(qe_by_bank[2])
            mm = E("pe", "mprime_mm", waits=w, m=m, ch=ch, bank=bank)
            mpev.append(E("dve", "mprime_ev", waits=[mm, a2_close], m=m, ch=ch,
                          bank=bank))
            mpi += 1
        for dcq in range(4):
            rr = None
            for ci in range(8):
                rr = E("sync" if ci % 2 else "gp", "rearr",
                       waits=[mpev[m * 8 + ci],
                              (wct_cast[-1] if wct_cast else None)],
                       m=m, ci=ci, dcq=dcq)
                rearr_all.append(rr)
            wct_cast.append(E("dve", "wct_cast",
                              waits=[rearr_all[-1], rearr_all[-2],
                                     v_ev[NTT - 1], v_ev[NTT - 2]],
                              m=m, dcq=dcq, which=0))
            wct_cast.append(E("dve", "wct_cast", waits=[], m=m, dcq=dcq, which=1))
    wct_done = wct_cast[-1]

    wo_cast = []
    for dc in range(8):
        d = E("gp", "dma_wo", waits=[stage_chain if dc == 0 else wo_cast[-1],
                                     a2_close if dc == 0 else None], dc=dc)
        cw = [d]
        if dc == 0:
            cw += [a2_close, rot_by_buf[0], rot_by_buf[1], rot_by_buf[2],
                   rot_by_buf[3]] + list(xnt_ev[15]) + list(xnt_ev[14])
        wo_cast.append(E("dve", "wo_cast", waits=cw, dc=dc))

    s_hist = []
    exp_hist = []
    pv_done = []
    div_hist = []
    prev_div = {}
    prev_lev = {}
    prev_d1 = {}
    prev_d2 = {}
    prev_oev = {}
    first_pv_guard = {(0, 0): v_ev[NTT - 1], (0, 1): qe_by_bank[1],
                      (1, 0): qe_by_bank[2], (1, 1): mpev[-1]}
    rb_first_guard = qe_by_bank[3]
    pending = []
    for hp in range(NHP):
        btab = E("dve", "btab_hp",
                 waits=[btab_a, exp_hist[-1] if exp_hist else None], hp=hp)
        for qg in range(NQG):
            w = 4 * qg + 4
            gates = {}

            def emit_s(kc):
                for head in range(2):
                    n, off = n_cols(qg, kc)
                    si = len(s_hist)
                    waits = []
                    if kc == 0 and head == 0:
                        waits = [btab, rot[(1, hp, qg)], rot[(0, hp, qg)]]
                        if si < 3:
                            waits += [v_ev[NTT - 1], v_ev[NTT - 2]]
                    if si >= 3:
                        waits.append(exp_hist[si - 3])
                    smm = E("pe", "s_mm", waits=waits, hp=hp, qg=qg, kc=kc,
                            head=head, bank=si % 3, n=n, off=off)
                    s_hist.append(smm)
                    ei = len(exp_hist)
                    ew = [smm]
                    if ei >= 4:
                        ew.append(pv_done[ei - 4])
                    ex = E("act", "p_exp", waits=ew, hp=hp, qg=qg, kc=kc,
                           head=head, bank=si % 3, slot=ei % 4, n=n)
                    exp_hist.append(ex)
                    gate = ex
                    if kc >= 4 * qg:
                        gate = E("dve", "p_mask", waits=[ex], slot=ei % 4)
                    gates[(kc, head)] = (gate, ei % 4)

            def emit_pv(kc):
                n, off = n_cols(qg, kc)
                par = qg % 2
                for head in range(2):
                    gate, slot = gates[(kc, head)]
                    waits_ = [gate]
                    if kc == 0:
                        waits_.append(prev_div.get((head, par),
                                                   first_pv_guard[(head, par)]))
                    pv = E("pe", "pv_mm", waits=waits_, hp=hp, qg=qg, kc=kc,
                           head=head, par=par, slot=slot, n=n, off=off,
                           start=(kc == 0), stop=(kc == w - 1))
                    pv_done.append(pv)

            emit_s(0)
            emit_s(1)
            for kc in range(w):
                if kc == min(6, w - 1):
                    while pending:
                        args = pending.pop(0)
                        bc = E("pe", "rb_bcast",
                               waits=[args["d2"],
                                      (div_hist[-1] if div_hist else mpev[-1]),
                                      (None if div_hist else rb_first_guard)],
                               head=args["head"])
                        dv = E("dve", "div_mult", waits=[bc, args["oev"]],
                               hp=args["hp"], qg=args["qg"], head=args["head"],
                               par=args["par"])
                        div_hist.append(dv)
                        prev_div[(args["head"], args["par"])] = dv
                emit_pv(kc)
                if kc + 2 < w:
                    emit_s(kc + 2)
            par = qg % 2
            for head in range(2):
                lastA = pv_done[-1]
                dh = div_hist[-1] if div_hist else None
                lev = E("dve", "l_ev", waits=[lastA, prev_d1.get(head), dh],
                        head=head, par=par)
                oev = E("dve", "o_ev", waits=[lastA, dh], head=head, par=par)
                d1 = E("sync", "ldiv_d1", waits=[lev, prev_d2.get(head)], head=head)
                rc = E("dve", "recl", waits=[d1], head=head)
                rhi = E("dve", "recl_hi", waits=[rc], head=head)
                rhf = E("dve", "recl_hif", waits=[rhi], head=head)
                rlo = E("dve", "recl_lo", waits=[rhf], head=head)
                d2 = E("sync", "ldiv_d2", waits=[rlo, dh], head=head)
                pending.append({"d2": d2, "oev": oev, "hp": hp, "qg": qg,
                                "head": head, "par": par})
                prev_d1[head] = d1
                prev_d2[head] = d2
    while pending:
        args = pending.pop(0)
        bc = E("pe", "rb_bcast",
               waits=[args["d2"], (div_hist[-1] if div_hist else mpev[-1])],
               head=args["head"])
        dv = E("dve", "div_mult", waits=[bc, args["oev"]], hp=args["hp"],
               qg=args["qg"], head=args["head"], par=args["par"])
        div_hist.append(dv)
        prev_div[(args["head"], args["par"])] = dv
    div_all = div_hist[-1]

    res = {}
    outs = []
    for tt in range(NTT):
        w = [wct_done]
        if tt >= 2:
            w += [res[(tt - 2, 0, "xy")], res[(tt - 2, 1, "xy")]]
        x2 = E("sync", "dma_x2", waits=w, tt=tt)
        for oh in range(2):
            wlast = None
            for dc in range(8):
                ww = []
                if dc == 0:
                    ww = [wo_cast[-1], div_all]
                    if tt >= 1:
                        ww.append(res[(tt - 1, oh, "xy")])
                wlast = E("pe", "wo_mm", waits=ww, tt=tt, oh=oh, dc=dc,
                          start=(dc == 0), stop=(dc == 7))
            fw = [wct_done, div_all]
            if tt >= 1:
                fw.append(res[(tt - 1, oh, "f")])
            flast = E("pe", "ffn_pre", waits=fw, tt=tt, oh=oh)
            for ic in range(8):
                flast = E("pe", "ffn_mm", waits=[], tt=tt, oh=oh, ic=ic,
                          stop=(ic == 7))
            rw = [x2, wlast]
            if tt >= 2:
                rw.append(outs[tt - 2])
            r1 = E("dve", "res_xy", waits=rw, tt=tt, oh=oh)
            r2 = E("dve", "res_f", waits=[flast, r1], tt=tt, oh=oh)
            res[(tt, oh, "xy")] = r1
            res[(tt, oh, "f")] = r2
        outs.append(E("gp", "dma_out", waits=[res[(tt, 0, "f")], res[(tt, 1, "f")]],
                      tt=tt))
    E("gp", "final", waits=[outs[-1]])
    s.finalize()
    return s


def build_nc():
    sched = build_schedule()
    nc = bass.Bass("TRN2", target_bir_lowering=False)

    dx = nc.dram_tensor("x", [N_T, D], F32, kind="ExternalInput")
    dwqkv = nc.dram_tensor("wqkvT", [D, 3 * D], F32, kind="ExternalInput")
    dwo = nc.dram_tensor("woT", [D, D], F32, kind="ExternalInput")
    dg1 = nc.dram_tensor("g1", [4, 2048], F32, kind="ExternalInput")
    dg2 = nc.dram_tensor("g2", [4, 512], F32, kind="ExternalInput")
    dsm = nc.dram_tensor("smalls", [128, 1120], F32, kind="ExternalInput")
    dmaskb = nc.dram_tensor("maskb", [128, 256], BF16, kind="ExternalInput")
    dpospm = nc.dram_tensor("pospm", [128, N_T], F32, kind="ExternalInput")
    dident = nc.dram_tensor("ident", [128, 128], F32, kind="ExternalInput")
    dout = nc.dram_tensor("out", [N_T, D], F32, kind="ExternalOutput")

    from contextlib import ExitStack
    with ExitStack() as stk:
        en = stk.enter_context
        xnT = en(nc.sbuf_tensor("xnT", [128, 8 * N_T], BF16))
        qT = en(nc.sbuf_tensor("qT", [128, 8 * N_T], BF16))
        kT = en(nc.sbuf_tensor("kT", [128, 8 * N_T], BF16))
        Vsb = en(nc.sbuf_tensor("Vsb", [128, NTT * NH * 65], BF16))
        Wbig = en(nc.sbuf_tensor("Wbig", [128, 8192], BF16))
        Pbuf = en(nc.sbuf_tensor("Pbuf", [128, 4 * 512], BF16))
        A2S = en(nc.sbuf_tensor("A2S", [128, 8192], BF16))
        sbiasb = en(nc.sbuf_tensor("sbiasb", [1, D], BF16))
        smaskb = en(nc.sbuf_tensor("smaskb", [128, 128], BF16))
        sonesb = en(nc.sbuf_tensor("sonesb", [2, 128], BF16))
        XNTILE, WQKB, QE1, QE2, PACKC, PACKS = 0, 2048, 4096, 5120, 6144, 7168

        FA = en(nc.sbuf_tensor("FA", [128, 7264], F32))
        XBUF, TRU, TRT, TRF, WST = 0, 2048, 3072, 4096, 5120
        LSB, LQ, BTAB = 6144, 7168, 7184
        FAW = 7264
        sm = en(nc.sbuf_tensor("sm", [128, 1120], F32))
        odiv32 = en(nc.sbuf_tensor("odiv32", [64, 1024], F32))
        print("SBUF bytes remaining after alloc:", nc.sbuf_bytes_remaining)
        lns = en(nc.sbuf_tensor("lns", [128, 16], F32))
        hilo = en(nc.sbuf_tensor("hilo", [128, 16], BF16))
        rrows = en(nc.sbuf_tensor("rrows", [128, 2048], BF16))
        C_PIG, C_EG, C_LX, C_PAR, C_BIAS = 0, 8, 24, 40, 96

        pb = [en(nc.psum_tensor("pb%d" % i, [128, 512], F32)) for i in range(8)]

        sems = {n: en(nc.semaphore("sem_" + n)) for n in sched.streams}
        block = en(nc.Block())

        def waits(eng, op):
            for (sn, i) in op.waits:
                eng.wait_ge(sems[sn], sched.wval(sn, i))

        def fin(instr, op):
            if op.inc:
                instr.then_inc(sems[op.stream],
                               16 if op.stream in ("sync", "gp") else 1)

        def dinc(instr, op):
            instr.then_inc(sems[op.stream], 16)
            return instr

        def xnt(ic, t0, n):
            return xnT[:, ic * N_T + t0: ic * N_T + t0 + n]

        def qe1sl(buf, r0=0, r1=128):
            if buf < 2:
                return A2S[r0:r1, QE1 + buf * 512: QE1 + (buf + 1) * 512]
            return A2S[r0:r1, XNTILE + (buf - 2) * 512: XNTILE + (buf - 1) * 512]

        def qe2sl(buf, r0=0, r1=128):
            if buf < 2:
                return A2S[r0:r1, QE2 + buf * 512: QE2 + (buf + 1) * 512]
            return A2S[r0:r1, XNTILE + 1024 + (buf - 2) * 512:
                       XNTILE + 1024 + (buf - 1) * 512]

        def emit_xnt(eng, op):
            a = op.args
            tt, half = a["tt"], a["half"]
            b2 = tt % 2
            for ic in range(half * 4, half * 4 + 4):
                dinc(eng.dma_start_transpose(
                    xnT[:, ic * N_T + tt * 128: ic * N_T + tt * 128 + 128],
                    A2S[:, XNTILE + b2 * D + ic * 128:
                        XNTILE + b2 * D + (ic + 1) * 128]), op)

        def emit_rearr(eng, op):
            a = op.args
            m, ci, dcq = a["m"], a["ci"], a["dcq"]
            reg = TRU if m == 0 else XBUF
            for bp in range(4):
                p0 = 8 * dcq + 32 * bp
                src = FA[p0:p0 + 8, reg + ci * 256: reg + (ci + 1) * 256]
                src = src.rearrange("p (g e) -> p g e", g=16)
                c0 = TRF + (4 * ci + bp) * 16
                dst = FA[:, c0:c0 + 16]
                dinc(eng.dma_start(out=dst, in_=src), op)

        def e_sync(eng, op):
            k, a = op.kind, op.args
            if k == "dma_small":
                dinc(eng.dma_start(out=sm[:], in_=dsm[:, :]), op)
                dinc(eng.dma_start(out=smaskb[:], in_=dmaskb[:, 0:128]), op)
                dinc(eng.dma_start(out=FA[:, TRF:TRF + 128], in_=dident[:, :]), op)
            elif k == "dma_x":
                tt = a["tt"]
                b = tt % 4
                dinc(eng.dma_start(out=FA[:, XBUF + b * D: XBUF + (b + 1) * D],
                                   in_=dx[tt * 128:(tt + 1) * 128, :]), op)
            elif k == "dma_x2":
                tt = a["tt"]
                b = tt % 2
                dinc(eng.dma_start(out=FA[:, XBUF + b * D: XBUF + (b + 1) * D],
                                   in_=dx[tt * 128:(tt + 1) * 128, :]), op)
            elif k == "xnt_dma":
                emit_xnt(eng, op)
            elif k == "dma_pospm":
                dinc(eng.dma_start(out=FA[:, XBUF:XBUF + N_T], in_=dpospm[:, :]), op)
            elif k == "rearr":
                emit_rearr(eng, op)
            elif k == "xnt_dma":
                emit_xnt(eng, op)
            elif k == "qe_swap":
                buf = a["buf"]
                for blk in range(4):
                    src = [32, 0, 96, 64][blk]
                    dinc(eng.dma_start(out=qe2sl(buf, blk * 32, (blk + 1) * 32),
                                       in_=qe1sl(buf, src, src + 32)), op)
            elif k == "ldiv_d1":
                head = a["head"]
                src = FA[0:1, LSB + head * 512: LSB + (head + 1) * 512]
                dst = FA[:, LQ + head * 4: LQ + (head + 1) * 4]
                dinc(eng.dma_start(out=dst, in_=src), op)
            elif k == "ldiv_d2":
                head = a["head"]
                dinc(eng.dma_start(
                    out=rrows[0:1, head * 1024: head * 1024 + 512],
                    in_=hilo[:, head * 4: head * 4 + 4]), op)
                dinc(eng.dma_start(
                    out=rrows[1:2, head * 1024: head * 1024 + 512],
                    in_=hilo[:, 8 + head * 4: 8 + (head + 1) * 4]), op)
            elif k == "dma_g":
                dinc(eng.dma_start(out=FA[0:4, XBUF:XBUF + 2048], in_=dg1[:, :]), op)
                dinc(eng.dma_start(out=FA[0:4, TRF + 512:TRF + 1024], in_=dg2[:, :]), op)
            else:
                raise KeyError(k)

        def e_gp(eng, op):
            k, a = op.kind, op.args
            if k == "dma_wv":
                ic = a["ic"]
                dinc(eng.dma_start(out=FA[:, WST:WST + 1024],
                                   in_=dwqkv[ic * 128:(ic + 1) * 128, 2048:3072]), op)
            elif k == "dma_wqk":
                sec, hp = a["sec"], a["hp"]
                c0 = sec * D + hp * 128
                src = bass.AP(dwqkv, c0, [[3 * D, 128], [128 * 3 * D, 8], [1, 128]])
                dinc(eng.dma_start(out=FA[:, WST:WST + 1024], in_=src), op)
            elif k == "dma_wo":
                dc = a["dc"]
                dinc(eng.dma_start(out=FA[:, WST:WST + 1024],
                                   in_=dwo[dc * 128:(dc + 1) * 128, :]), op)
            elif k == "qe_swap":
                buf = a["buf"]
                for blk in range(4):
                    src = [32, 0, 96, 64][blk]
                    dinc(eng.dma_start(out=qe2sl(buf, blk * 32, (blk + 1) * 32),
                                       in_=qe1sl(buf, src, src + 32)), op)
            elif k == "rearr":
                emit_rearr(eng, op)
            elif k == "xnt_dma":
                emit_xnt(eng, op)
            elif k == "dma_out":
                tt = a["tt"]
                dinc(eng.dma_start(out=dout[tt * 128:(tt + 1) * 128, :],
                                   in_=FA[:, TRU + (tt % 2) * D: TRU + (tt % 2 + 1) * D]), op)
            elif k == "final":
                pass
            else:
                raise KeyError(k)

        def e_act(eng, op):
            k, a = op.kind, op.args
            if k == "gates":
                eng.activation(out=sm[:, C_PIG:C_PIG + 8], in_=sm[:, C_PIG:C_PIG + 8],
                               func=AF.Sigmoid)
                eng.activation(out=sm[:, C_EG:C_EG + 16], in_=sm[:, C_EG:C_EG + 16],
                               func=AF.Sigmoid)
                fin(eng.activation(out=sm[:, C_LX:C_LX + 16], in_=sm[:, C_LX:C_LX + 16],
                                   func=AF.Exp, scale=-1.0), op)
            elif k == "xnt_dma":
                emit_xnt(eng, op)
            elif k == "ln_sqrt":
                b = (a["tt"] % 2) * 8
                fin(eng.activation(out=lns[:, b + 4:b + 5], in_=lns[:, b + 3:b + 4],
                                   func=AF.Sqrt, bias=sm[:, 95:96]), op)
            elif k == "trig_u":
                hp, half = a["hp"], a["half"]
                fin(eng.activation(out=FA[:, TRU:TRU + 1024],
                                   in_=FA[:, XBUF + half * 1024: XBUF + (half + 1) * 1024],
                                   func=AF.Copy, scale=sm[:, C_PIG + hp:C_PIG + hp + 1]), op)
            elif k == "trig_sin":
                out = (A2S[:, PACKS:PACKS + 1024] if a["half"] == 0
                       else Wbig[:, 0:1024])
                fin(eng.activation(out=out, in_=FA[:, TRF:TRF + 1024],
                                   func=AF.Sin, scale=2 * np.pi), op)
            elif k == "trig_cos":
                out = (A2S[:, PACKC:PACKC + 1024] if a["half"] == 0
                       else Wbig[:, 1024:2048])
                fin(eng.activation(out=out, in_=FA[:, TRT:TRT + 1024],
                                   func=AF.Sin, scale=2 * np.pi), op)
            elif k == "qe_ev":
                bank, buf = a["bank"], a["buf"]
                fin(eng.copy(out=qe1sl(buf), in_=pb[4 + bank][:]), op)
            elif k == "p_exp":
                bank, slot, n = a["bank"], a["slot"], a["n"]
                col = BTAB + a["head"] * 40 + QGKC_IDX[(a["qg"], a["kc"])]
                fin(eng.activation(out=Pbuf[:, slot * 512: slot * 512 + n],
                                   in_=pb[bank][:, 0:n], func=AF.Exp, scale=0.125,
                                   bias=FA[:, col:col + 1]), op)
            else:
                raise KeyError(k)

        def e_dve(eng, op):
            k, a = op.kind, op.args
            if k == "btab_a":
                fin(eng.tensor_mul(out=sm[:, C_EG:C_EG + 16], in0=sm[:, C_EG:C_EG + 16],
                                   in1=sm[:, C_LX:C_LX + 16]), op)
            elif k == "btab_hp":
                hp = a["hp"]
                for head in range(2):
                    h = 2 * hp + head
                    i = eng.tensor_scalar_mul(out=FA[:, BTAB + head * 40:
                                                     BTAB + (head + 1) * 40],
                                              in0=sm[:, C_PAR:C_PAR + 40],
                                              scalar1=sm[:, C_EG + h:C_EG + h + 1])
                fin(i, op)
            elif k == "v_ev":
                tt = a["tt"]
                p0, p1 = (pb[0], pb[1]) if tt % 2 == 0 else (pb[2], pb[3])
                eng.tensor_copy(out=bass.AP(Vsb, tt * NH * 65,
                                            [[16640, 128], [65, 8], [1, 64]]),
                                in_=p0[:])
                fin(eng.tensor_copy(out=bass.AP(Vsb, tt * NH * 65 + 8 * 65,
                                                [[16640, 128], [65, 8], [1, 64]]),
                                    in_=p1[:]), op)
            elif k == "ln_norm":
                tt = a["tt"]
                b4, b = tt % 4, (tt % 2) * 8
                sl = FA[:, XBUF + b4 * D: XBUF + (b4 + 1) * D]
                fin(eng.tensor_scalar(out=sl, in0=sl,
                                      scalar1=lns[:, b + 5:b + 6],
                                      scalar2=lns[:, b + 6:b + 7],
                                      op0=AluOpType.mult, op1=AluOpType.add), op)
            elif k == "ln_stats":
                tt = a["tt"]
                b4, b = tt % 4, (tt % 2) * 8
                xa = FA[:, XBUF + b4 * D: XBUF + (b4 + 1) * D]
                eng.reduce_sum(out=lns[:, b:b + 1], in_=xa, axis=mybir.AxisListType.X)
                fin(eng.scalar_tensor_tensor(out=Pbuf[:, 0:1024], in0=xa,
                                             scalar=1.0, in1=xa, op0=AluOpType.mult,
                                             op1=AluOpType.mult,
                                             accum_out=lns[:, b + 1:b + 2]), op)
            elif k == "ln_fin1":
                b = (a["tt"] % 2) * 8
                c = lambda j: lns[:, b + j:b + j + 1]
                fin(eng.tensor_scalar_mul(out=c(2), in0=c(0), scalar1=1.0 / D), op)
            elif k == "ln_fin1b":
                b = (a["tt"] % 2) * 8
                c = lambda j: lns[:, b + j:b + j + 1]
                fin(eng.scalar_tensor_tensor(out=c(7), in0=c(2), scalar=0.0, in1=c(2),
                                             op0=AluOpType.add, op1=AluOpType.mult), op)
            elif k == "ln_fin1c":
                b = (a["tt"] % 2) * 8
                c = lambda j: lns[:, b + j:b + j + 1]
                fin(eng.scalar_tensor_tensor(out=c(3), in0=c(1), scalar=1.0 / D,
                                             in1=c(7), op0=AluOpType.mult,
                                             op1=AluOpType.subtract), op)
            elif k == "ln_fin2":
                b = (a["tt"] % 2) * 8
                c = lambda j: lns[:, b + j:b + j + 1]
                fin(eng.reciprocal(out=c(5), in_=c(4)), op)
            elif k == "ln_fin2b":
                b = (a["tt"] % 2) * 8
                c = lambda j: lns[:, b + j:b + j + 1]
                fin(eng.scalar_tensor_tensor(out=c(6), in0=c(2), scalar=-1.0, in1=c(5),
                                             op0=AluOpType.mult, op1=AluOpType.mult), op)
            elif k == "xnt_ev":
                tt, grp = a["tt"], a["grp"]
                bank = pb[6 + grp]
                i = None
                for j in range(4):
                    ic = grp * 4 + j
                    i = eng.tensor_copy(out=xnT[:, ic * N_T + tt * 128:
                                                ic * N_T + tt * 128 + 128],
                                        in_=bank[:, j * 128:(j + 1) * 128])
                fin(i, op)
            elif k == "wv_cast":
                ic = a["ic"]
                fin(eng.tensor_copy(out=Wbig[:, ic * 1024:(ic + 1) * 1024],
                                    in_=FA[:, WST:WST + 1024]), op)
            elif k == "consts_set":
                eng.memset(sonesb[:], 1.0)
                for tt in range(NTT):
                    eng.memset(bass.AP(Vsb, tt * NH * 65 + 64,
                                       [[16640, 128], [65, NH]]), 1.0)
                fin(eng.tensor_copy(out=sbiasb[:], in_=sm[0:1, C_BIAS:C_BIAS + D]), op)
            elif k == "wqk_cast":
                slot = a["slot"]
                if slot < 2:
                    dst = A2S[:, WQKB + slot * 1024: WQKB + (slot + 1) * 1024]
                else:
                    dst = Wbig[:, 2048 + (slot - 2) * 1024:
                               2048 + (slot - 1) * 1024]
                fin(eng.tensor_copy(out=dst, in_=FA[:, WST:WST + 1024]), op)
            elif k == "wo_cast":
                dc = a["dc"]
                fin(eng.tensor_copy(out=A2S[:, dc * 1024:(dc + 1) * 1024],
                                    in_=FA[:, WST:WST + 1024]), op)
            elif k == "trig_rt":
                fin(eng.tensor_scalar(out=FA[:, TRT:TRT + 1024],
                                      in0=FA[:, TRU:TRU + 1024],
                                      scalar1=MAGIC, scalar2=-MAGIC,
                                      op0=AluOpType.add, op1=AluOpType.add), op)
            elif k == "trig_frs":
                fin(eng.scalar_tensor_tensor(out=FA[:, TRF:TRF + 1024],
                                             in0=FA[:, TRU:TRU + 1024], scalar=0.0,
                                             in1=FA[:, TRT:TRT + 1024],
                                             op0=AluOpType.add,
                                             op1=AluOpType.subtract), op)
            elif k == "trig_u25":
                fin(eng.tensor_scalar_add(out=FA[:, TRU:TRU + 1024],
                                          in0=FA[:, TRU:TRU + 1024], scalar1=0.25), op)
            elif k == "trig_rt2":
                fin(eng.tensor_scalar(out=FA[:, TRT:TRT + 1024],
                                      in0=FA[:, TRU:TRU + 1024],
                                      scalar1=MAGIC, scalar2=-MAGIC,
                                      op0=AluOpType.add, op1=AluOpType.add), op)
            elif k == "trig_frc":
                fin(eng.scalar_tensor_tensor(out=FA[:, TRT:TRT + 1024],
                                             in0=FA[:, TRU:TRU + 1024], scalar=0.0,
                                             in1=FA[:, TRT:TRT + 1024],
                                             op0=AluOpType.add,
                                             op1=AluOpType.subtract), op)
            elif k == "rot_c":
                buf, tg = a["buf"], a["tg"]
                t2 = (tg % 2) * 512
                c_sl = (A2S[:, PACKC + t2:PACKC + t2 + 512] if tg < 2
                        else Wbig[:, 1024 + t2:1024 + t2 + 512])
                fin(eng.tensor_mul(out=qe1sl(buf), in0=qe1sl(buf), in1=c_sl), op)
            elif k == "rot_s":
                buf, tg = a["buf"], a["tg"]
                t2 = (tg % 2) * 512
                s_sl = (A2S[:, PACKS + t2:PACKS + t2 + 512] if tg < 2
                        else Wbig[:, t2:t2 + 512])
                fin(eng.tensor_mul(out=qe2sl(buf), in0=qe2sl(buf), in1=s_sl), op)
            elif k == "rot_add":
                sec, hp, tg, buf = a["sec"], a["hp"], a["tg"], a["buf"]
                dst = (qT if sec == 0 else kT)[:, hp * N_T + tg * 512:
                                              hp * N_T + (tg + 1) * 512]
                fin(eng.tensor_add(out=dst, in0=qe1sl(buf), in1=qe2sl(buf)), op)
            elif k == "mprime_ev":
                m, ch = a["m"], a["ch"]
                reg = TRU if m == 0 else XBUF
                fin(eng.tensor_copy(out=FA[:, reg + ch * 256: reg + (ch + 1) * 256],
                                    in_=pb[a["bank"]][:, 0:256]), op)
            elif k == "wct_cast":
                m, dcq, which = a["m"], a["dcq"], a["which"]
                src = FA[:, TRF:TRF + 512]
                if m == 0 and which == 0:
                    fin(eng.tensor_copy(out=Wbig[:, dcq * 1024: dcq * 1024 + 512],
                                        in_=src), op)
                elif m == 0 and which == 1:
                    fin(eng.tensor_copy(
                        out=Wbig[:, (4 + dcq) * 1024 + 512:(4 + dcq) * 1024 + 1024],
                        in_=src), op)
                elif m == 1 and which == 0:
                    fin(eng.tensor_copy(out=Wbig[:, dcq * 1024 + 512: dcq * 1024 + 1024],
                                        in_=src), op)
                else:
                    fin(eng.tensor_scalar_mul(
                        out=Wbig[:, (4 + dcq) * 1024:(4 + dcq) * 1024 + 512],
                        in0=src, scalar1=-1.0), op)
            elif k == "p_mask":
                slot = a["slot"]
                fin(eng.tensor_mul(out=Pbuf[:, slot * 512: slot * 512 + 128],
                                   in0=Pbuf[:, slot * 512: slot * 512 + 128],
                                   in1=smaskb[:, 0:128]), op)
            elif k == "o_ev":
                head, par = a["head"], a["par"]
                ob = pb[3 + 2 * head + par]
                fin(eng.tensor_copy(out=odiv32[0:64, head * 512:(head + 1) * 512],
                                    in_=ob[0:64, :]), op)
            elif k == "l_ev":
                head, par = a["head"], a["par"]
                ob = pb[3 + 2 * head + par]
                fin(eng.tensor_copy(out=FA[0:1, LSB + head * 512:
                                           LSB + (head + 1) * 512],
                                    in_=ob[64:65, 0:512]), op)
            elif k == "recl_hi":
                head = a["head"]
                fin(eng.tensor_copy(out=hilo[:, head * 4: head * 4 + 4],
                                    in_=FA[:, LQ + head * 4: LQ + (head + 1) * 4]), op)
            elif k == "recl_hif":
                head = a["head"]
                fin(eng.tensor_copy(out=FA[:, LQ + 8 + head * 4:
                                           LQ + 8 + (head + 1) * 4],
                                    in_=hilo[:, head * 4: head * 4 + 4]), op)
            elif k == "recl_lo":
                head = a["head"]
                fin(eng.scalar_tensor_tensor(
                    out=hilo[:, 8 + head * 4: 8 + (head + 1) * 4],
                    in0=FA[:, LQ + head * 4: LQ + (head + 1) * 4], scalar=0.0,
                    in1=FA[:, LQ + 8 + head * 4: LQ + 8 + (head + 1) * 4],
                    op0=AluOpType.add, op1=AluOpType.subtract), op)
            elif k == "recl":
                head = a["head"]
                fin(eng.reciprocal(out=FA[:, LQ + head * 4: LQ + (head + 1) * 4],
                                   in_=FA[:, LQ + head * 4: LQ + (head + 1) * 4]), op)
            elif k == "div_mult":
                hp, qg, head = a["hp"], a["qg"], a["head"]
                r0 = head * 64
                dst = qT[r0:r0 + 64, hp * N_T + qg * 512: hp * N_T + (qg + 1) * 512]
                fin(eng.tensor_mul(out=dst,
                                   in0=odiv32[0:64, head * 512:(head + 1) * 512],
                                   in1=pb[7][0:64, :]), op)
            elif k == "res_xy":
                tt, oh = a["tt"], a["oh"]
                b = tt % 2
                sl = slice(b * D + oh * 512, b * D + (oh + 1) * 512)
                fin(eng.tensor_add(out=FA[:, TRU + sl.start: TRU + sl.stop],
                                   in0=FA[:, XBUF + sl.start: XBUF + sl.stop],
                                   in1=pb[oh][:, :]), op)
            elif k == "res_f":
                tt, oh = a["tt"], a["oh"]
                b = tt % 2
                sl = slice(b * D + oh * 512, b * D + (oh + 1) * 512)
                fin(eng.tensor_add(out=FA[:, TRU + sl.start: TRU + sl.stop],
                                   in0=FA[:, TRU + sl.start: TRU + sl.stop],
                                   in1=pb[2 + oh][:, :]), op)
            else:
                raise KeyError(k)

        def e_pe(eng, op):
            k, a = op.kind, op.args
            if k == "v_mm":
                tt, ic, oh = a["tt"], a["ic"], a["oh"]
                p = pb[oh] if tt % 2 == 0 else pb[2 + oh]
                fin(eng.matmul(p[:], xnt(ic, tt * 128, 128),
                               Wbig[:, ic * 1024 + oh * 512: ic * 1024 + (oh + 1) * 512],
                               start=a["start"], stop=a["stop"]), op)
            elif k == "qk_mm":
                tg, ic, bank, slot = a["tg"], a["ic"], a["bank"], a["slot"]
                if slot < 2:
                    wsl = A2S[:, WQKB + slot * 1024 + ic * 128:
                              WQKB + slot * 1024 + (ic + 1) * 128]
                else:
                    wsl = Wbig[:, 2048 + (slot - 2) * 1024 + ic * 128:
                               2048 + (slot - 2) * 1024 + (ic + 1) * 128]
                fin(eng.matmul(pb[4 + bank][:], wsl, xnt(ic, tg * 512, 512),
                               start=a["start"], stop=a["stop"]), op)
            elif k == "xnt_tp":
                tt, grp = a["tt"], a["grp"]
                b4 = tt % 4
                bank = pb[6 + grp]
                i = None
                for j in range(4):
                    ic = grp * 4 + j
                    i = eng.transpose(bank[:, j * 128:(j + 1) * 128],
                                      FA[:, XBUF + b4 * D + ic * 128:
                                         XBUF + b4 * D + (ic + 1) * 128],
                                      FA[:, TRF:TRF + 128])
                fin(i, op)
            elif k == "mprime_mm":
                m, ch = a["m"], a["ch"]
                fin(eng.matmul(pb[a["bank"]][:, 0:256],
                               FA[0:4, XBUF + m * 1024 + ch * 128:
                                  XBUF + m * 1024 + (ch + 1) * 128],
                               FA[0:4, TRF + 512 + m * 256: TRF + 512 + (m + 1) * 256],
                               start=True, stop=True), op)
            elif k == "s_mm":
                hp, qg, kc, head, bank = a["hp"], a["qg"], a["kc"], a["head"], a["bank"]
                n, off = a["n"], a["off"]
                r0 = head * 64
                fin(eng.matmul(pb[bank][:, 0:n],
                               kT[r0:r0 + 64, hp * N_T + kc * 128:
                                  hp * N_T + (kc + 1) * 128],
                               qT[r0:r0 + 64, hp * N_T + qg * 512 + off:
                                  hp * N_T + (qg + 1) * 512],
                               start=True, stop=True), op)
            elif k == "pv_mm":
                hp, kc, head, par = a["hp"], a["kc"], a["head"], a["par"]
                slot, n, off = a["slot"], a["n"], a["off"]
                h = 2 * hp + head
                fin(eng.matmul(pb[3 + 2 * head + par][0:65, off:512],
                               Vsb[:, kc * NH * 65 + h * 65:
                                   kc * NH * 65 + (h + 1) * 65],
                               Pbuf[:, slot * 512: slot * 512 + n],
                               start=a["start"], stop=a["stop"]), op)
            elif k == "rb_bcast":
                head = a["head"]
                fin(eng.matmul(pb[7][0:64, :], sonesb[0:2, 0:64],
                               rrows[0:2, head * 1024: head * 1024 + 512],
                               start=True, stop=True), op)
            elif k == "wo_mm":
                tt, oh, dc = a["tt"], a["oh"], a["dc"]
                fin(eng.matmul(pb[oh][:],
                               qT[:, dc * N_T + tt * 128: dc * N_T + (tt + 1) * 128],
                               A2S[:, dc * 1024 + oh * 512: dc * 1024 + (oh + 1) * 512],
                               start=a["start"], stop=a["stop"]), op)
            elif k == "ffn_pre":
                oh = a["oh"]
                fin(eng.matmul(pb[2 + oh][:], sonesb[0:1, :],
                               sbiasb[0:1, oh * 512:(oh + 1) * 512],
                               start=True, stop=False), op)
            elif k == "ffn_mm":
                tt, oh, ic = a["tt"], a["oh"], a["ic"]
                fin(eng.matmul(pb[2 + oh][:], xnt(ic, tt * 128, 128),
                               Wbig[:, ic * 1024 + oh * 512: ic * 1024 + (oh + 1) * 512],
                               start=False, stop=a["stop"]), op)
            else:
                raise KeyError(k)

        @block.sync
        def _(eng):
            for op in sched.streams["sync"]:
                waits(eng, op)
                e_sync(eng, op)

        @block.gpsimd
        def _(eng):
            for op in sched.streams["gp"]:
                waits(eng, op)
                e_gp(eng, op)

        @block.scalar
        def _(eng):
            for op in sched.streams["act"]:
                waits(eng, op)
                e_act(eng, op)

        @block.vector
        def _(eng):
            for op in sched.streams["dve"]:
                waits(eng, op)
                e_dve(eng, op)

        @block.tensor
        def _(eng):
            for op in sched.streams["pe"]:
                waits(eng, op)
                e_pe(eng, op)

    return nc


_NC_CACHE = None


def _get_nc():
    global _NC_CACHE
    if _NC_CACHE is None:
        _NC_CACHE = build_nc()
    return _NC_CACHE


def _host_inputs(L0_attn_x, L0_attn_Wqkv, L0_attn_Wo, L0_attn_log_xi,
                 L0_attn_pi_gate_logit, L0_attn_e_gate_logit,
                 L0_ffn_G1_R, L0_ffn_G2_R, L0_ffn_G1_I, L0_ffn_G2_I,
                 L0_ffn_bias_real, L0_ffn_bias_imag):
    import ml_dtypes
    x = np.asarray(L0_attn_x, dtype=np.float32)
    Wqkv = np.asarray(L0_attn_Wqkv, dtype=np.float32)
    Wo = np.asarray(L0_attn_Wo, dtype=np.float32)

    idx = np.empty(D, dtype=np.int64)
    for h in range(NH):
        for par in range(2):
            for j in range(32):
                idx[h * 64 + par * 32 + j] = h * 64 + 2 * j + par
    WqkvT = np.ascontiguousarray(Wqkv.T)
    perm = np.concatenate([idx, D + idx, 2 * D + np.arange(D)])
    WqkvT = np.ascontiguousarray(WqkvT[:, perm])
    WoT = np.ascontiguousarray(Wo.T)

    G1R = np.asarray(L0_ffn_G1_R, dtype=np.float32)[0]
    G2R = np.asarray(L0_ffn_G2_R, dtype=np.float32)[:, :, 0, :]
    G1I = np.asarray(L0_ffn_G1_I, dtype=np.float32)[0]
    G2I = np.asarray(L0_ffn_G2_I, dtype=np.float32)[:, :, 0, :]
    g1 = np.concatenate(
        [np.ascontiguousarray(G1R.transpose(1, 0, 2)).reshape(4, 1024),
         np.ascontiguousarray(G1I.transpose(1, 0, 2)).reshape(4, 1024)], axis=1)
    g2 = np.concatenate(
        [np.ascontiguousarray(G2R.transpose(0, 2, 1)).reshape(4, 256),
         np.ascontiguousarray(G2I.transpose(0, 2, 1)).reshape(4, 256)], axis=1)

    sm = np.zeros((128, 1120), dtype=np.float32)
    pig = np.asarray(L0_attn_pi_gate_logit, dtype=np.float32)
    eg = np.asarray(L0_attn_e_gate_logit, dtype=np.float32)
    lx = np.asarray(L0_attn_log_xi, dtype=np.float32)
    for hp in range(NHP):
        sm[0:64, hp] = pig[2 * hp]
        sm[64:128, hp] = pig[2 * hp + 1]
    sm[:, 8:24] = eg[None, :]
    sm[:, 24:40] = lx[None, :]
    p = np.arange(128, dtype=np.float32)
    for i, (qg, kc) in enumerate(QGKC):
        sm[:, 40 + i] = p + 128 * kc - 512 * qg - 511
    sm[:, 95] = 1e-5
    sm[0, 96:96 + D] = np.concatenate(
        [np.asarray(L0_ffn_bias_real, np.float32),
         np.asarray(L0_ffn_bias_imag, np.float32)])

    tri = np.triu(np.ones((128, 128), np.float32))
    maskb = np.concatenate([tri, np.eye(128, dtype=np.float32)],
                           axis=1).astype(ml_dtypes.bfloat16)

    j = np.arange(32, dtype=np.float64)
    f = np.pi ** (1.0 - j / 32.0)
    t = np.arange(N_T, dtype=np.float64)
    base = np.outer(f, t) / (2 * np.pi)
    pospm = np.zeros((128, N_T), dtype=np.float32)
    for H in range(2):
        pospm[H * 64 + 0:H * 64 + 32] = -base
        pospm[H * 64 + 32:H * 64 + 64] = base
    ident = np.eye(128, dtype=np.float32)
    return x, WqkvT, WoT, g1, g2, sm, maskb, pospm, ident


def kernel(**inputs):
    x, WqkvT, WoT, g1, g2, sm, maskb, pospm, ident = _host_inputs(**inputs)
    nc = _get_nc()
    in_maps = [dict(x=np.ascontiguousarray(x[c]), wqkvT=WqkvT, woT=WoT, g1=g1,
                    g2=g2, smalls=sm, maskb=maskb, pospm=pospm, ident=ident)
               for c in range(8)]
    res = run_bass_kernel_spmd(nc, in_maps, core_ids=list(range(8)))
    return np.stack([np.asarray(r["out"]) for r in res.results],
                    axis=0).astype(np.float32)


# version 34
# speedup vs baseline: 1.0564x; 1.0397x over previous
import numpy as np

import concourse.bass as bass
import concourse.mybir as mybir
from concourse.bass_utils import run_bass_kernel_spmd
from concourse.alu_op_type import AluOpType

F32 = mybir.dt.float32
BF16 = mybir.dt.bfloat16
AF = mybir.ActivationFunctionType
MAGIC = 12582912.0

N_T = 2048
D = 1024
NH = 16
NTT = 16
NQG = 4
NHP = 8

QGKC = [(qg, kc) for qg in range(NQG) for kc in range(4 * qg + 4)]
QGKC_IDX = {p: i for i, p in enumerate(QGKC)}


def n_cols(qg, kc):
    off = max(0, kc * 128 - qg * 512)
    return 512 - off, off


class Op:
    __slots__ = ["kind", "args", "waits", "inc", "stream"]

    def __init__(self, kind, args):
        self.kind, self.args, self.waits, self.inc, self.stream = kind, args, [], False, None


class Sched:
    def __init__(self):
        self.streams = {n: [] for n in ["pe", "act", "dve", "sync", "gp"]}

    def emit(self, stream, kind, waits=(), **args):
        op = Op(kind, args)
        op.stream = stream
        for w in waits:
            if w is not None:
                op.waits.append(w)
        self.streams[stream].append(op)
        return (stream, len(self.streams[stream]) - 1)

    NDMA = {"dma_small": 3, "dma_pospm": 1, "dma_x": 1, "dma_x2": 1,
            "dma_g": 2, "dma_wv": 1, "dma_wqk": 1, "dma_wo": 1,
            "qe_swap": 4, "rearr": 4, "dma_out": 1, "final": 0,
            "xnt_dma": 4, "ldiv_d1": 1, "ldiv_d2": 2}

    def finalize(self):
        for ops in self.streams.values():
            for op in ops:
                for (sn, i) in op.waits:
                    self.streams[sn][i].inc = True
        self.incval = {}
        for name, ops in self.streams.items():
            c, vals = 0, []
            for op in ops:
                if name in ("sync", "gp") or op.kind in self.NDMA:
                    c += 16 * self.NDMA[op.kind]
                    op.inc = True
                elif op.inc:
                    c += 1
                vals.append(c)
            self.incval[name] = vals

    def wval(self, sn, i):
        return self.incval[sn][i]


def build_schedule():
    s = Sched()
    E = s.emit

    ld_small = E("sync", "dma_small")
    sig = E("act", "gates", waits=[ld_small])
    btab_a = E("dve", "btab_a", waits=[sig])

    ln_norm = [None] * NTT
    xnt_ev = [None] * NTT
    tp_last = [None] * NTT
    sqs = [None] * NTT
    tp_ops = [[None, None] for _ in range(NTT)]
    ev_ops = [[None, None] for _ in range(NTT)]
    stage_chain = None
    wv_cast = []
    wqk_cast = {}

    def emit_back(t):
        f2 = E("dve", "ln_fin2", waits=[sqs[t]], tt=t)
        f2b = E("dve", "ln_fin2b", waits=[f2], tt=t)
        ln_norm[t] = E("dve", "ln_norm", waits=[f2b], tt=t)
        if t >= 1:
            for grp in range(2):
                ev_ops[t - 1][grp] = E("dve", "xnt_ev",
                                       waits=[tp_ops[t - 1][grp]],
                                       tt=t - 1, grp=grp)
            xnt_ev[t - 1] = (ev_ops[t - 1][0], ev_ops[t - 1][1])
        for grp in range(2):
            tw = [ln_norm[t]]
            if t >= 1:
                tw.append(ev_ops[t - 1][grp])
            else:
                tw.append(ld_small)
            tp_ops[t][grp] = E("pe", "xnt_tp", waits=tw, tt=t, grp=grp)
        tp_last[t] = tp_ops[t][1]

    for tt in range(NTT):
        w = [tp_last[tt - 4]] if tt >= 4 else []
        xd = E("sync", "dma_x", waits=w, tt=tt)
        st0 = E("dve", "ln_stats", waits=[xd], tt=tt)
        f1 = E("dve", "ln_fin1", waits=[st0], tt=tt)
        f1b = E("dve", "ln_fin1b", waits=[f1], tt=tt)
        f1c = E("dve", "ln_fin1c", waits=[f1b], tt=tt)
        sqs[tt] = E("act", "ln_sqrt", waits=[f1c], tt=tt)
        if tt < 8:
            d = E("gp", "dma_wv", waits=[stage_chain], ic=tt)
            stage_chain = E("dve", "wv_cast", waits=[d], ic=tt)
            wv_cast.append(stage_chain)
        elif tt < 10:
            sec = tt - 8
            d = E("gp", "dma_wqk", waits=[stage_chain], sec=sec, hp=0)
            stage_chain = E("dve", "wqk_cast", waits=[d], sec=sec, hp=0, slot=sec)
            wqk_cast[(sec, 0)] = stage_chain
        if tt >= 1:
            emit_back(tt - 1)
    emit_back(NTT - 1)
    for grp in range(2):
        ev_ops[NTT - 1][grp] = E("dve", "xnt_ev",
                                 waits=[tp_ops[NTT - 1][grp]],
                                 tt=NTT - 1, grp=grp)
    xnt_ev[NTT - 1] = (ev_ops[NTT - 1][0], ev_ops[NTT - 1][1])
    tp_last[NTT - 1] = tp_ops[NTT - 1][1]

    ones_set = E("dve", "consts_set")
    v_ev = [None] * NTT
    for tt in range(NTT):
        last = None
        for ic in range(8):
            for oh in range(2):
                w = []
                if ic == 0 and oh == 0:
                    w = list(xnt_ev[tt]) + [wv_cast[-1]]
                    if tt >= 2:
                        w.append(v_ev[tt - 2])
                last = E("pe", "v_mm", waits=w, tt=tt, ic=ic, oh=oh,
                         start=(ic == 0), stop=(ic == 7))
        v_ev[tt] = E("dve", "v_ev", waits=[last, ones_set], tt=tt)

    ld_pospm = E("sync", "dma_pospm",
                 waits=[ln_norm[NTT - 1], tp_last[NTT - 1]])
    last_qk_sect = {}
    rot = {}
    prev_sin = prev_cos = prev_frc = None
    pack_rot_guard = {0: None, 1: None}
    qe_by_bank = [None, None, ev_ops[NTT - 1][0], ev_ops[NTT - 1][1]]
    rot_by_buf = [None, None, None, None]
    scos_by_half = {}
    g = 0
    halves = [(hp, half) for hp in range(NHP) for half in range(2)]

    def trig_full(hp, half):
        nonlocal prev_sin, prev_cos, prev_frc
        u = E("act", "trig_u", waits=[ld_pospm, sig, prev_frc], hp=hp, half=half)
        rt = E("dve", "trig_rt", waits=[u, prev_cos], hp=hp, half=half)
        fs = E("dve", "trig_frs", waits=[prev_sin, rt], hp=hp, half=half)
        sw_sin = [fs, pack_rot_guard[half]]
        if hp == 0 and half == 1:
            sw_sin += [v_ev[NTT - 1], v_ev[NTT - 2]]
        ssin = E("act", "trig_sin", waits=sw_sin, hp=hp, half=half)
        u25 = E("dve", "trig_u25", waits=[fs], hp=hp, half=half)
        rt2 = E("dve", "trig_rt2", waits=[u25], hp=hp, half=half)
        frc = E("dve", "trig_frc", waits=[ssin, rt2], hp=hp, half=half)
        scos = E("act", "trig_cos", waits=[frc], hp=hp, half=half)
        prev_sin, prev_cos, prev_frc = ssin, scos, frc
        scos_by_half[(hp, half)] = scos

    trig_full(*halves[0])
    for i, (hp, half) in enumerate(halves):
        scos = scos_by_half[(hp, half)]
        last_rot_this_half = None
        for sec in range(2):
            slot = sec if hp == 0 else 2 + ((hp - 1) * 2 + sec) % 6
            if half == 0 and hp > 0:
                d = E("gp", "dma_wqk", waits=[stage_chain], sec=sec, hp=hp)
                stage_chain = E("dve", "wqk_cast",
                                waits=[d, last_qk_sect.get((sec, hp - 3))],
                                sec=sec, hp=hp, slot=slot)
                wqk_cast[(sec, hp)] = stage_chain
            for tg in (2 * half, 2 * half + 1):
                bank = g % 4
                buf = g % 4
                w = ([wqk_cast[(sec, hp)], qe_by_bank[bank]]
                     + list(xnt_ev[4 * tg + 3]))
                last = None
                for ic in range(8):
                    last = E("pe", "qk_mm", waits=(w if ic == 0 else []),
                             sec=sec, hp=hp, tg=tg, ic=ic, bank=bank,
                             slot=slot, start=(ic == 0), stop=(ic == 7))
                last_qk_sect[(sec, hp)] = last
                qe = E("act", "qe_ev", waits=[last, rot_by_buf[buf]],
                       bank=bank, buf=buf)
                qe_by_bank[bank] = qe
                sw = E("sync", "qe_swap", waits=[qe], buf=buf)
                rc = E("dve", "rot_c", waits=[qe, scos, sw], tg=tg, buf=buf)
                rs = E("dve", "rot_s", waits=[sw], tg=tg, buf=buf)
                r3 = E("dve", "rot_add", waits=[rs, rc], sec=sec, hp=hp, tg=tg, buf=buf)
                rot[(sec, hp, tg)] = r3
                rot_by_buf[buf] = r3
                last_rot_this_half = r3
                g += 1
        pack_rot_guard[half] = last_rot_this_half
        if i + 1 < len(halves):
            trig_full(*halves[i + 1])
    a2_close = prev_cos

    ld_g = E("sync", "dma_g", waits=[a2_close, rot_by_buf[0], rot_by_buf[1],
                                     rot_by_buf[2], rot_by_buf[3]])
    mpev_all = {}
    mpev_by_bank = {4: None, 6: None}
    oev_par1 = {0: None, 1: None}
    rearr_all = []
    wct_cast = []
    mp_state = {"i": 0}
    rearr_slots = {}
    for si, (shp, sqg) in enumerate([(h, q) for h in (3, 4, 5, 6)
                                     for q in range(4)]):
        m = si // 8
        dcq = (si % 8) // 2
        cis = (0, 1, 2, 3) if si % 2 == 0 else (4, 5, 6, 7)
        rearr_slots[(shp, sqg)] = (m, dcq, cis)

    def emit_mprime_pair():
        for t in range(2):
            mpi = mp_state["i"]
            m, ch = divmod(mpi, 8)
            bank = 4 + 2 * t
            head = t
            g0 = oev_par1[head] if oev_par1[head] is not None else qe_by_bank[3]
            mm = E("pe", "mprime_mm", waits=[ld_g, g0], m=m, ch=ch, bank=bank)
            ev = E("dve", "mprime_ev",
                   waits=[mm] + ([a2_close] if mpi == 0 else []),
                   m=m, ch=ch, bank=bank)
            mpev_all[(m, ch)] = ev
            mpev_by_bank[bank] = ev
            mp_state["i"] += 1

    def emit_rearr_batch(hp, qg):
        m, dcq, cis = rearr_slots[(hp, qg)]
        for ci in cis:
            rr = E("sync" if ci % 2 else "gp", "rearr",
                   waits=[mpev_all[(m, ci)],
                          (wct_cast[-1] if wct_cast else None)],
                   m=m, ci=ci, dcq=dcq)
            rearr_all.append(rr)
        if cis[0] == 4:
            w0 = [rearr_all[-1], rearr_all[-2]]
            if not wct_cast:
                w0 += [v_ev[NTT - 1], v_ev[NTT - 2]]
            wct_cast.append(E("dve", "wct_cast", waits=w0, m=m, dcq=dcq,
                              which=0))
            wct_cast.append(E("dve", "wct_cast", waits=[], m=m, dcq=dcq,
                              which=1))

    wo_cast = []
    for dc in range(8):
        d = E("gp", "dma_wo", waits=[stage_chain if dc == 0 else wo_cast[-1],
                                     a2_close if dc == 0 else None], dc=dc)
        cw = [d]
        if dc == 0:
            cw += [a2_close, rot_by_buf[0], rot_by_buf[1], rot_by_buf[2],
                   rot_by_buf[3]] + list(xnt_ev[15]) + list(xnt_ev[14])
        wo_cast.append(E("dve", "wo_cast", waits=cw, dc=dc))

    s_hist = []
    exp_hist = []
    pv_done = []
    div_hist = []
    prev_div = {}
    prev_lev = {}
    prev_d1 = {}
    prev_d2 = {}
    prev_oev = {}
    first_pv_guard = {(0, 0): v_ev[NTT - 1], (0, 1): qe_by_bank[1],
                      (1, 0): qe_by_bank[2], (1, 1): qe_by_bank[3]}
    rb_first_guard = qe_by_bank[3]
    pending = []
    for hp in range(NHP):
        btab = E("dve", "btab_hp",
                 waits=[btab_a, exp_hist[-1] if exp_hist else None], hp=hp)
        for qg in range(NQG):
            if 1 <= hp <= 4 and qg % 2 == 0:
                emit_mprime_pair()
            w = 4 * qg + 4
            gates = {}

            def emit_s(kc):
                for head in range(2):
                    n, off = n_cols(qg, kc)
                    si = len(s_hist)
                    waits = []
                    if kc == 0 and head == 0:
                        waits = [btab, rot[(1, hp, qg)], rot[(0, hp, qg)]]
                        if si < 3:
                            waits += [v_ev[NTT - 1], v_ev[NTT - 2]]
                    if si >= 3:
                        waits.append(exp_hist[si - 3])
                    smm = E("pe", "s_mm", waits=waits, hp=hp, qg=qg, kc=kc,
                            head=head, bank=si % 3, n=n, off=off)
                    s_hist.append(smm)
                    ei = len(exp_hist)
                    ew = [smm]
                    if ei >= 4:
                        ew.append(pv_done[ei - 4])
                    ex = E("act", "p_exp", waits=ew, hp=hp, qg=qg, kc=kc,
                           head=head, bank=si % 3, slot=ei % 4, n=n)
                    exp_hist.append(ex)
                    gate = ex
                    if kc >= 4 * qg:
                        gate = E("dve", "p_mask", waits=[ex], slot=ei % 4)
                    gates[(kc, head)] = (gate, ei % 4)

            def emit_pv(kc):
                n, off = n_cols(qg, kc)
                par = qg % 2
                for head in range(2):
                    gate, slot = gates[(kc, head)]
                    waits_ = [gate]
                    if kc == 0:
                        waits_.append(prev_div.get((head, par),
                                                   first_pv_guard[(head, par)]))
                        if par == 1:
                            waits_.append(mpev_by_bank[4 + 2 * head])
                    pv = E("pe", "pv_mm", waits=waits_, hp=hp, qg=qg, kc=kc,
                           head=head, par=par, slot=slot, n=n, off=off,
                           start=(kc == 0), stop=(kc == w - 1))
                    pv_done.append(pv)

            emit_s(0)
            emit_s(1)
            for kc in range(w):
                if kc == min(6, w - 1):
                    while pending:
                        args = pending.pop(0)
                        bc = E("pe", "rb_bcast",
                               waits=[args["d2"],
                                      (div_hist[-1] if div_hist
                                       else rb_first_guard)],
                               head=args["head"])
                        dv = E("dve", "div_mult", waits=[bc, args["oev"]],
                               hp=args["hp"], qg=args["qg"], head=args["head"],
                               par=args["par"])
                        div_hist.append(dv)
                        prev_div[(args["head"], args["par"])] = dv
                emit_pv(kc)
                if kc + 2 < w:
                    emit_s(kc + 2)
            par = qg % 2
            for head in range(2):
                lastA = pv_done[-1]
                dh = div_hist[-1] if div_hist else None
                lev = E("dve", "l_ev", waits=[lastA, prev_d1.get(head), dh],
                        head=head, par=par)
                oev = E("dve", "o_ev", waits=[lastA, dh], head=head, par=par)
                if par == 1:
                    oev_par1[head] = oev
                d1 = E("sync", "ldiv_d1", waits=[lev, prev_d2.get(head)], head=head)
                rc = E("dve", "recl", waits=[d1], head=head)
                rhi = E("dve", "recl_hi", waits=[rc], head=head)
                rhf = E("dve", "recl_hif", waits=[rhi], head=head)
                rlo = E("dve", "recl_lo", waits=[rhf], head=head)
                d2 = E("sync", "ldiv_d2", waits=[rlo, dh], head=head)
                pending.append({"d2": d2, "oev": oev, "hp": hp, "qg": qg,
                                "head": head, "par": par})
                prev_d1[head] = d1
                prev_d2[head] = d2
            if (hp, qg) in rearr_slots:
                emit_rearr_batch(hp, qg)
    while pending:
        args = pending.pop(0)
        bc = E("pe", "rb_bcast",
               waits=[args["d2"], (div_hist[-1] if div_hist else rb_first_guard)],
               head=args["head"])
        dv = E("dve", "div_mult", waits=[bc, args["oev"]], hp=args["hp"],
               qg=args["qg"], head=args["head"], par=args["par"])
        div_hist.append(dv)
        prev_div[(args["head"], args["par"])] = dv
    div_all = div_hist[-1]
    wct_done = wct_cast[-1]

    res = {}
    outs = []
    for tt in range(NTT):
        w = [wct_done]
        if tt >= 2:
            w += [res[(tt - 2, 0, "xy")], res[(tt - 2, 1, "xy")]]
        x2 = E("sync", "dma_x2", waits=w, tt=tt)
        for oh in range(2):
            wlast = None
            for dc in range(8):
                ww = []
                if dc == 0:
                    ww = [wo_cast[-1], div_all]
                    if tt >= 1:
                        ww.append(res[(tt - 1, oh, "xy")])
                wlast = E("pe", "wo_mm", waits=ww, tt=tt, oh=oh, dc=dc,
                          start=(dc == 0), stop=(dc == 7))
            fw = [wct_done, div_all]
            if tt >= 1:
                fw.append(res[(tt - 1, oh, "f")])
            flast = E("pe", "ffn_pre", waits=fw, tt=tt, oh=oh)
            for ic in range(8):
                flast = E("pe", "ffn_mm", waits=[], tt=tt, oh=oh, ic=ic,
                          stop=(ic == 7))
            rw = [x2, wlast]
            if tt >= 2:
                rw.append(outs[tt - 2])
            r1 = E("dve", "res_xy", waits=rw, tt=tt, oh=oh)
            r2 = E("dve", "res_f", waits=[flast, r1], tt=tt, oh=oh)
            res[(tt, oh, "xy")] = r1
            res[(tt, oh, "f")] = r2
        outs.append(E("gp", "dma_out", waits=[res[(tt, 0, "f")], res[(tt, 1, "f")]],
                      tt=tt))
    E("gp", "final", waits=[outs[-1]])
    s.finalize()
    return s


def build_nc():
    sched = build_schedule()
    nc = bass.Bass("TRN2", target_bir_lowering=False)

    dx = nc.dram_tensor("x", [N_T, D], F32, kind="ExternalInput")
    dwqkv = nc.dram_tensor("wqkvT", [D, 3 * D], F32, kind="ExternalInput")
    dwo = nc.dram_tensor("woT", [D, D], F32, kind="ExternalInput")
    dg1 = nc.dram_tensor("g1", [4, 2048], F32, kind="ExternalInput")
    dg2 = nc.dram_tensor("g2", [4, 512], F32, kind="ExternalInput")
    dsm = nc.dram_tensor("smalls", [128, 1120], F32, kind="ExternalInput")
    dmaskb = nc.dram_tensor("maskb", [128, 256], BF16, kind="ExternalInput")
    dpospm = nc.dram_tensor("pospm", [128, N_T], F32, kind="ExternalInput")
    dident = nc.dram_tensor("ident", [128, 128], F32, kind="ExternalInput")
    dout = nc.dram_tensor("out", [N_T, D], F32, kind="ExternalOutput")

    from contextlib import ExitStack
    with ExitStack() as stk:
        en = stk.enter_context
        xnT = en(nc.sbuf_tensor("xnT", [128, 8 * N_T], BF16))
        qT = en(nc.sbuf_tensor("qT", [128, 8 * N_T], BF16))
        kT = en(nc.sbuf_tensor("kT", [128, 8 * N_T], BF16))
        Vsb = en(nc.sbuf_tensor("Vsb", [128, NTT * NH * 65], BF16))
        Wbig = en(nc.sbuf_tensor("Wbig", [128, 8192], BF16))
        Pbuf = en(nc.sbuf_tensor("Pbuf", [128, 4 * 512], BF16))
        A2S = en(nc.sbuf_tensor("A2S", [128, 8192], BF16))
        sbiasb = en(nc.sbuf_tensor("sbiasb", [1, D], BF16))
        smaskb = en(nc.sbuf_tensor("smaskb", [128, 128], BF16))
        sonesb = en(nc.sbuf_tensor("sonesb", [2, 128], BF16))
        XNTILE, WQKB, QE1, QE2, PACKC, PACKS = 0, 2048, 4096, 5120, 6144, 7168

        FA = en(nc.sbuf_tensor("FA", [128, 7264], F32))
        XBUF, TRU, TRT, TRF, WST = 0, 2048, 3072, 4096, 5120
        LSB, LQ, BTAB = 6144, 7168, 7184
        FAW = 7264
        sm = en(nc.sbuf_tensor("sm", [128, 1120], F32))
        odiv32 = en(nc.sbuf_tensor("odiv32", [64, 1024], F32))
        print("SBUF bytes remaining after alloc:", nc.sbuf_bytes_remaining)
        lns = en(nc.sbuf_tensor("lns", [128, 16], F32))
        hilo = en(nc.sbuf_tensor("hilo", [128, 16], BF16))
        rrows = en(nc.sbuf_tensor("rrows", [128, 2048], BF16))
        C_PIG, C_EG, C_LX, C_PAR, C_BIAS = 0, 8, 24, 40, 96

        pb = [en(nc.psum_tensor("pb%d" % i, [128, 512], F32)) for i in range(8)]

        sems = {n: en(nc.semaphore("sem_" + n)) for n in sched.streams}
        block = en(nc.Block())

        def waits(eng, op):
            for (sn, i) in op.waits:
                eng.wait_ge(sems[sn], sched.wval(sn, i))

        def fin(instr, op):
            if op.inc:
                instr.then_inc(sems[op.stream],
                               16 if op.stream in ("sync", "gp") else 1)

        def dinc(instr, op):
            instr.then_inc(sems[op.stream], 16)
            return instr

        def xnt(ic, t0, n):
            return xnT[:, ic * N_T + t0: ic * N_T + t0 + n]

        def qe1sl(buf, r0=0, r1=128):
            if buf < 2:
                return A2S[r0:r1, QE1 + buf * 512: QE1 + (buf + 1) * 512]
            return A2S[r0:r1, XNTILE + (buf - 2) * 512: XNTILE + (buf - 1) * 512]

        def qe2sl(buf, r0=0, r1=128):
            if buf < 2:
                return A2S[r0:r1, QE2 + buf * 512: QE2 + (buf + 1) * 512]
            return A2S[r0:r1, XNTILE + 1024 + (buf - 2) * 512:
                       XNTILE + 1024 + (buf - 1) * 512]

        def emit_xnt(eng, op):
            a = op.args
            tt, half = a["tt"], a["half"]
            b2 = tt % 2
            for ic in range(half * 4, half * 4 + 4):
                dinc(eng.dma_start_transpose(
                    xnT[:, ic * N_T + tt * 128: ic * N_T + tt * 128 + 128],
                    A2S[:, XNTILE + b2 * D + ic * 128:
                        XNTILE + b2 * D + (ic + 1) * 128]), op)

        def emit_rearr(eng, op):
            a = op.args
            m, ci, dcq = a["m"], a["ci"], a["dcq"]
            reg = TRU if m == 0 else XBUF
            for bp in range(4):
                p0 = 8 * dcq + 32 * bp
                src = FA[p0:p0 + 8, reg + ci * 256: reg + (ci + 1) * 256]
                src = src.rearrange("p (g e) -> p g e", g=16)
                c0 = TRF + (4 * ci + bp) * 16
                dst = FA[:, c0:c0 + 16]
                dinc(eng.dma_start(out=dst, in_=src), op)

        def e_sync(eng, op):
            k, a = op.kind, op.args
            if k == "dma_small":
                dinc(eng.dma_start(out=sm[:], in_=dsm[:, :]), op)
                dinc(eng.dma_start(out=smaskb[:], in_=dmaskb[:, 0:128]), op)
                dinc(eng.dma_start(out=FA[:, TRF:TRF + 128], in_=dident[:, :]), op)
            elif k == "dma_x":
                tt = a["tt"]
                b = tt % 4
                dinc(eng.dma_start(out=FA[:, XBUF + b * D: XBUF + (b + 1) * D],
                                   in_=dx[tt * 128:(tt + 1) * 128, :]), op)
            elif k == "dma_x2":
                tt = a["tt"]
                b = tt % 2
                dinc(eng.dma_start(out=FA[:, XBUF + b * D: XBUF + (b + 1) * D],
                                   in_=dx[tt * 128:(tt + 1) * 128, :]), op)
            elif k == "xnt_dma":
                emit_xnt(eng, op)
            elif k == "dma_pospm":
                dinc(eng.dma_start(out=FA[:, XBUF:XBUF + N_T], in_=dpospm[:, :]), op)
            elif k == "rearr":
                emit_rearr(eng, op)
            elif k == "xnt_dma":
                emit_xnt(eng, op)
            elif k == "qe_swap":
                buf = a["buf"]
                for blk in range(4):
                    src = [32, 0, 96, 64][blk]
                    dinc(eng.dma_start(out=qe2sl(buf, blk * 32, (blk + 1) * 32),
                                       in_=qe1sl(buf, src, src + 32)), op)
            elif k == "ldiv_d1":
                head = a["head"]
                src = FA[0:1, LSB + head * 512: LSB + (head + 1) * 512]
                dst = FA[:, LQ + head * 4: LQ + (head + 1) * 4]
                dinc(eng.dma_start(out=dst, in_=src), op)
            elif k == "ldiv_d2":
                head = a["head"]
                dinc(eng.dma_start(
                    out=rrows[0:1, head * 1024: head * 1024 + 512],
                    in_=hilo[:, head * 4: head * 4 + 4]), op)
                dinc(eng.dma_start(
                    out=rrows[1:2, head * 1024: head * 1024 + 512],
                    in_=hilo[:, 8 + head * 4: 8 + (head + 1) * 4]), op)
            elif k == "dma_g":
                dinc(eng.dma_start(out=FA[0:4, XBUF:XBUF + 2048], in_=dg1[:, :]), op)
                dinc(eng.dma_start(out=FA[0:4, TRF + 512:TRF + 1024], in_=dg2[:, :]), op)
            else:
                raise KeyError(k)

        def e_gp(eng, op):
            k, a = op.kind, op.args
            if k == "dma_wv":
                ic = a["ic"]
                dinc(eng.dma_start(out=FA[:, WST:WST + 1024],
                                   in_=dwqkv[ic * 128:(ic + 1) * 128, 2048:3072]), op)
            elif k == "dma_wqk":
                sec, hp = a["sec"], a["hp"]
                c0 = sec * D + hp * 128
                src = bass.AP(dwqkv, c0, [[3 * D, 128], [128 * 3 * D, 8], [1, 128]])
                dinc(eng.dma_start(out=FA[:, WST:WST + 1024], in_=src), op)
            elif k == "dma_wo":
                dc = a["dc"]
                dinc(eng.dma_start(out=FA[:, WST:WST + 1024],
                                   in_=dwo[dc * 128:(dc + 1) * 128, :]), op)
            elif k == "qe_swap":
                buf = a["buf"]
                for blk in range(4):
                    src = [32, 0, 96, 64][blk]
                    dinc(eng.dma_start(out=qe2sl(buf, blk * 32, (blk + 1) * 32),
                                       in_=qe1sl(buf, src, src + 32)), op)
            elif k == "rearr":
                emit_rearr(eng, op)
            elif k == "xnt_dma":
                emit_xnt(eng, op)
            elif k == "dma_out":
                tt = a["tt"]
                dinc(eng.dma_start(out=dout[tt * 128:(tt + 1) * 128, :],
                                   in_=FA[:, TRU + (tt % 2) * D: TRU + (tt % 2 + 1) * D]), op)
            elif k == "final":
                pass
            else:
                raise KeyError(k)

        def e_act(eng, op):
            k, a = op.kind, op.args
            if k == "gates":
                eng.activation(out=sm[:, C_PIG:C_PIG + 8], in_=sm[:, C_PIG:C_PIG + 8],
                               func=AF.Sigmoid)
                eng.activation(out=sm[:, C_EG:C_EG + 16], in_=sm[:, C_EG:C_EG + 16],
                               func=AF.Sigmoid)
                fin(eng.activation(out=sm[:, C_LX:C_LX + 16], in_=sm[:, C_LX:C_LX + 16],
                                   func=AF.Exp, scale=-1.0), op)
            elif k == "xnt_dma":
                emit_xnt(eng, op)
            elif k == "ln_sqrt":
                b = (a["tt"] % 2) * 8
                fin(eng.activation(out=lns[:, b + 4:b + 5], in_=lns[:, b + 3:b + 4],
                                   func=AF.Sqrt, bias=sm[:, 95:96]), op)
            elif k == "trig_u":
                hp, half = a["hp"], a["half"]
                fin(eng.activation(out=FA[:, TRU:TRU + 1024],
                                   in_=FA[:, XBUF + half * 1024: XBUF + (half + 1) * 1024],
                                   func=AF.Copy, scale=sm[:, C_PIG + hp:C_PIG + hp + 1]), op)
            elif k == "trig_sin":
                out = (A2S[:, PACKS:PACKS + 1024] if a["half"] == 0
                       else Wbig[:, 0:1024])
                fin(eng.activation(out=out, in_=FA[:, TRF:TRF + 1024],
                                   func=AF.Sin, scale=2 * np.pi), op)
            elif k == "trig_cos":
                out = (A2S[:, PACKC:PACKC + 1024] if a["half"] == 0
                       else Wbig[:, 1024:2048])
                fin(eng.activation(out=out, in_=FA[:, TRT:TRT + 1024],
                                   func=AF.Sin, scale=2 * np.pi), op)
            elif k == "qe_ev":
                bank, buf = a["bank"], a["buf"]
                fin(eng.copy(out=qe1sl(buf), in_=pb[4 + bank][:]), op)
            elif k == "p_exp":
                bank, slot, n = a["bank"], a["slot"], a["n"]
                col = BTAB + a["head"] * 40 + QGKC_IDX[(a["qg"], a["kc"])]
                fin(eng.activation(out=Pbuf[:, slot * 512: slot * 512 + n],
                                   in_=pb[bank][:, 0:n], func=AF.Exp, scale=0.125,
                                   bias=FA[:, col:col + 1]), op)
            else:
                raise KeyError(k)

        def e_dve(eng, op):
            k, a = op.kind, op.args
            if k == "btab_a":
                fin(eng.tensor_mul(out=sm[:, C_EG:C_EG + 16], in0=sm[:, C_EG:C_EG + 16],
                                   in1=sm[:, C_LX:C_LX + 16]), op)
            elif k == "btab_hp":
                hp = a["hp"]
                for head in range(2):
                    h = 2 * hp + head
                    i = eng.tensor_scalar_mul(out=FA[:, BTAB + head * 40:
                                                     BTAB + (head + 1) * 40],
                                              in0=sm[:, C_PAR:C_PAR + 40],
                                              scalar1=sm[:, C_EG + h:C_EG + h + 1])
                fin(i, op)
            elif k == "v_ev":
                tt = a["tt"]
                p0, p1 = (pb[0], pb[1]) if tt % 2 == 0 else (pb[2], pb[3])
                eng.tensor_copy(out=bass.AP(Vsb, tt * NH * 65,
                                            [[16640, 128], [65, 8], [1, 64]]),
                                in_=p0[:])
                fin(eng.tensor_copy(out=bass.AP(Vsb, tt * NH * 65 + 8 * 65,
                                                [[16640, 128], [65, 8], [1, 64]]),
                                    in_=p1[:]), op)
            elif k == "ln_norm":
                tt = a["tt"]
                b4, b = tt % 4, (tt % 2) * 8
                sl = FA[:, XBUF + b4 * D: XBUF + (b4 + 1) * D]
                fin(eng.tensor_scalar(out=sl, in0=sl,
                                      scalar1=lns[:, b + 5:b + 6],
                                      scalar2=lns[:, b + 6:b + 7],
                                      op0=AluOpType.mult, op1=AluOpType.add), op)
            elif k == "ln_stats":
                tt = a["tt"]
                b4, b = tt % 4, (tt % 2) * 8
                xa = FA[:, XBUF + b4 * D: XBUF + (b4 + 1) * D]
                eng.reduce_sum(out=lns[:, b:b + 1], in_=xa, axis=mybir.AxisListType.X)
                fin(eng.scalar_tensor_tensor(out=Pbuf[:, 0:1024], in0=xa,
                                             scalar=1.0, in1=xa, op0=AluOpType.mult,
                                             op1=AluOpType.mult,
                                             accum_out=lns[:, b + 1:b + 2]), op)
            elif k == "ln_fin1":
                b = (a["tt"] % 2) * 8
                c = lambda j: lns[:, b + j:b + j + 1]
                fin(eng.tensor_scalar_mul(out=c(2), in0=c(0), scalar1=1.0 / D), op)
            elif k == "ln_fin1b":
                b = (a["tt"] % 2) * 8
                c = lambda j: lns[:, b + j:b + j + 1]
                fin(eng.scalar_tensor_tensor(out=c(7), in0=c(2), scalar=0.0, in1=c(2),
                                             op0=AluOpType.add, op1=AluOpType.mult), op)
            elif k == "ln_fin1c":
                b = (a["tt"] % 2) * 8
                c = lambda j: lns[:, b + j:b + j + 1]
                fin(eng.scalar_tensor_tensor(out=c(3), in0=c(1), scalar=1.0 / D,
                                             in1=c(7), op0=AluOpType.mult,
                                             op1=AluOpType.subtract), op)
            elif k == "ln_fin2":
                b = (a["tt"] % 2) * 8
                c = lambda j: lns[:, b + j:b + j + 1]
                fin(eng.reciprocal(out=c(5), in_=c(4)), op)
            elif k == "ln_fin2b":
                b = (a["tt"] % 2) * 8
                c = lambda j: lns[:, b + j:b + j + 1]
                fin(eng.scalar_tensor_tensor(out=c(6), in0=c(2), scalar=-1.0, in1=c(5),
                                             op0=AluOpType.mult, op1=AluOpType.mult), op)
            elif k == "xnt_ev":
                tt, grp = a["tt"], a["grp"]
                bank = pb[6 + grp]
                i = None
                for j in range(4):
                    ic = grp * 4 + j
                    i = eng.tensor_copy(out=xnT[:, ic * N_T + tt * 128:
                                                ic * N_T + tt * 128 + 128],
                                        in_=bank[:, j * 128:(j + 1) * 128])
                fin(i, op)
            elif k == "wv_cast":
                ic = a["ic"]
                fin(eng.tensor_copy(out=Wbig[:, ic * 1024:(ic + 1) * 1024],
                                    in_=FA[:, WST:WST + 1024]), op)
            elif k == "consts_set":
                eng.memset(sonesb[:], 1.0)
                for tt in range(NTT):
                    eng.memset(bass.AP(Vsb, tt * NH * 65 + 64,
                                       [[16640, 128], [65, NH]]), 1.0)
                fin(eng.tensor_copy(out=sbiasb[:], in_=sm[0:1, C_BIAS:C_BIAS + D]), op)
            elif k == "wqk_cast":
                slot = a["slot"]
                if slot < 2:
                    dst = A2S[:, WQKB + slot * 1024: WQKB + (slot + 1) * 1024]
                else:
                    dst = Wbig[:, 2048 + (slot - 2) * 1024:
                               2048 + (slot - 1) * 1024]
                fin(eng.tensor_copy(out=dst, in_=FA[:, WST:WST + 1024]), op)
            elif k == "wo_cast":
                dc = a["dc"]
                fin(eng.tensor_copy(out=A2S[:, dc * 1024:(dc + 1) * 1024],
                                    in_=FA[:, WST:WST + 1024]), op)
            elif k == "trig_rt":
                fin(eng.tensor_scalar(out=FA[:, TRT:TRT + 1024],
                                      in0=FA[:, TRU:TRU + 1024],
                                      scalar1=MAGIC, scalar2=-MAGIC,
                                      op0=AluOpType.add, op1=AluOpType.add), op)
            elif k == "trig_frs":
                fin(eng.scalar_tensor_tensor(out=FA[:, TRF:TRF + 1024],
                                             in0=FA[:, TRU:TRU + 1024], scalar=0.0,
                                             in1=FA[:, TRT:TRT + 1024],
                                             op0=AluOpType.add,
                                             op1=AluOpType.subtract), op)
            elif k == "trig_u25":
                fin(eng.tensor_scalar_add(out=FA[:, TRU:TRU + 1024],
                                          in0=FA[:, TRU:TRU + 1024], scalar1=0.25), op)
            elif k == "trig_rt2":
                fin(eng.tensor_scalar(out=FA[:, TRT:TRT + 1024],
                                      in0=FA[:, TRU:TRU + 1024],
                                      scalar1=MAGIC, scalar2=-MAGIC,
                                      op0=AluOpType.add, op1=AluOpType.add), op)
            elif k == "trig_frc":
                fin(eng.scalar_tensor_tensor(out=FA[:, TRT:TRT + 1024],
                                             in0=FA[:, TRU:TRU + 1024], scalar=0.0,
                                             in1=FA[:, TRT:TRT + 1024],
                                             op0=AluOpType.add,
                                             op1=AluOpType.subtract), op)
            elif k == "rot_c":
                buf, tg = a["buf"], a["tg"]
                t2 = (tg % 2) * 512
                c_sl = (A2S[:, PACKC + t2:PACKC + t2 + 512] if tg < 2
                        else Wbig[:, 1024 + t2:1024 + t2 + 512])
                fin(eng.tensor_mul(out=qe1sl(buf), in0=qe1sl(buf), in1=c_sl), op)
            elif k == "rot_s":
                buf, tg = a["buf"], a["tg"]
                t2 = (tg % 2) * 512
                s_sl = (A2S[:, PACKS + t2:PACKS + t2 + 512] if tg < 2
                        else Wbig[:, t2:t2 + 512])
                fin(eng.tensor_mul(out=qe2sl(buf), in0=qe2sl(buf), in1=s_sl), op)
            elif k == "rot_add":
                sec, hp, tg, buf = a["sec"], a["hp"], a["tg"], a["buf"]
                dst = (qT if sec == 0 else kT)[:, hp * N_T + tg * 512:
                                              hp * N_T + (tg + 1) * 512]
                fin(eng.tensor_add(out=dst, in0=qe1sl(buf), in1=qe2sl(buf)), op)
            elif k == "mprime_ev":
                m, ch = a["m"], a["ch"]
                reg = TRU if m == 0 else XBUF
                fin(eng.tensor_copy(out=FA[:, reg + ch * 256: reg + (ch + 1) * 256],
                                    in_=pb[a["bank"]][:, 0:256]), op)
            elif k == "wct_cast":
                m, dcq, which = a["m"], a["dcq"], a["which"]
                src = FA[:, TRF:TRF + 512]
                if m == 0 and which == 0:
                    fin(eng.tensor_copy(out=Wbig[:, dcq * 1024: dcq * 1024 + 512],
                                        in_=src), op)
                elif m == 0 and which == 1:
                    fin(eng.tensor_copy(
                        out=Wbig[:, (4 + dcq) * 1024 + 512:(4 + dcq) * 1024 + 1024],
                        in_=src), op)
                elif m == 1 and which == 0:
                    fin(eng.tensor_copy(out=Wbig[:, dcq * 1024 + 512: dcq * 1024 + 1024],
                                        in_=src), op)
                else:
                    fin(eng.tensor_scalar_mul(
                        out=Wbig[:, (4 + dcq) * 1024:(4 + dcq) * 1024 + 512],
                        in0=src, scalar1=-1.0), op)
            elif k == "p_mask":
                slot = a["slot"]
                fin(eng.tensor_mul(out=Pbuf[:, slot * 512: slot * 512 + 128],
                                   in0=Pbuf[:, slot * 512: slot * 512 + 128],
                                   in1=smaskb[:, 0:128]), op)
            elif k == "o_ev":
                head, par = a["head"], a["par"]
                ob = pb[3 + 2 * head + par]
                fin(eng.tensor_copy(out=odiv32[0:64, head * 512:(head + 1) * 512],
                                    in_=ob[0:64, :]), op)
            elif k == "l_ev":
                head, par = a["head"], a["par"]
                ob = pb[3 + 2 * head + par]
                fin(eng.tensor_copy(out=FA[0:1, LSB + head * 512:
                                           LSB + (head + 1) * 512],
                                    in_=ob[64:65, 0:512]), op)
            elif k == "recl_hi":
                head = a["head"]
                fin(eng.tensor_copy(out=hilo[:, head * 4: head * 4 + 4],
                                    in_=FA[:, LQ + head * 4: LQ + (head + 1) * 4]), op)
            elif k == "recl_hif":
                head = a["head"]
                fin(eng.tensor_copy(out=FA[:, LQ + 8 + head * 4:
                                           LQ + 8 + (head + 1) * 4],
                                    in_=hilo[:, head * 4: head * 4 + 4]), op)
            elif k == "recl_lo":
                head = a["head"]
                fin(eng.scalar_tensor_tensor(
                    out=hilo[:, 8 + head * 4: 8 + (head + 1) * 4],
                    in0=FA[:, LQ + head * 4: LQ + (head + 1) * 4], scalar=0.0,
                    in1=FA[:, LQ + 8 + head * 4: LQ + 8 + (head + 1) * 4],
                    op0=AluOpType.add, op1=AluOpType.subtract), op)
            elif k == "recl":
                head = a["head"]
                fin(eng.reciprocal(out=FA[:, LQ + head * 4: LQ + (head + 1) * 4],
                                   in_=FA[:, LQ + head * 4: LQ + (head + 1) * 4]), op)
            elif k == "div_mult":
                hp, qg, head = a["hp"], a["qg"], a["head"]
                r0 = head * 64
                dst = qT[r0:r0 + 64, hp * N_T + qg * 512: hp * N_T + (qg + 1) * 512]
                fin(eng.tensor_mul(out=dst,
                                   in0=odiv32[0:64, head * 512:(head + 1) * 512],
                                   in1=pb[7][0:64, :]), op)
            elif k == "res_xy":
                tt, oh = a["tt"], a["oh"]
                b = tt % 2
                sl = slice(b * D + oh * 512, b * D + (oh + 1) * 512)
                fin(eng.tensor_add(out=FA[:, TRU + sl.start: TRU + sl.stop],
                                   in0=FA[:, XBUF + sl.start: XBUF + sl.stop],
                                   in1=pb[oh][:, :]), op)
            elif k == "res_f":
                tt, oh = a["tt"], a["oh"]
                b = tt % 2
                sl = slice(b * D + oh * 512, b * D + (oh + 1) * 512)
                fin(eng.tensor_add(out=FA[:, TRU + sl.start: TRU + sl.stop],
                                   in0=FA[:, TRU + sl.start: TRU + sl.stop],
                                   in1=pb[2 + oh][:, :]), op)
            else:
                raise KeyError(k)

        def e_pe(eng, op):
            k, a = op.kind, op.args
            if k == "v_mm":
                tt, ic, oh = a["tt"], a["ic"], a["oh"]
                p = pb[oh] if tt % 2 == 0 else pb[2 + oh]
                fin(eng.matmul(p[:], xnt(ic, tt * 128, 128),
                               Wbig[:, ic * 1024 + oh * 512: ic * 1024 + (oh + 1) * 512],
                               start=a["start"], stop=a["stop"]), op)
            elif k == "qk_mm":
                tg, ic, bank, slot = a["tg"], a["ic"], a["bank"], a["slot"]
                if slot < 2:
                    wsl = A2S[:, WQKB + slot * 1024 + ic * 128:
                              WQKB + slot * 1024 + (ic + 1) * 128]
                else:
                    wsl = Wbig[:, 2048 + (slot - 2) * 1024 + ic * 128:
                               2048 + (slot - 2) * 1024 + (ic + 1) * 128]
                fin(eng.matmul(pb[4 + bank][:], wsl, xnt(ic, tg * 512, 512),
                               start=a["start"], stop=a["stop"]), op)
            elif k == "xnt_tp":
                tt, grp = a["tt"], a["grp"]
                b4 = tt % 4
                bank = pb[6 + grp]
                i = None
                for j in range(4):
                    ic = grp * 4 + j
                    i = eng.transpose(bank[:, j * 128:(j + 1) * 128],
                                      FA[:, XBUF + b4 * D + ic * 128:
                                         XBUF + b4 * D + (ic + 1) * 128],
                                      FA[:, TRF:TRF + 128])
                fin(i, op)
            elif k == "mprime_mm":
                m, ch = a["m"], a["ch"]
                fin(eng.matmul(pb[a["bank"]][:, 0:256],
                               FA[0:4, XBUF + m * 1024 + ch * 128:
                                  XBUF + m * 1024 + (ch + 1) * 128],
                               FA[0:4, TRF + 512 + m * 256: TRF + 512 + (m + 1) * 256],
                               start=True, stop=True), op)
            elif k == "s_mm":
                hp, qg, kc, head, bank = a["hp"], a["qg"], a["kc"], a["head"], a["bank"]
                n, off = a["n"], a["off"]
                r0 = head * 64
                fin(eng.matmul(pb[bank][:, 0:n],
                               kT[r0:r0 + 64, hp * N_T + kc * 128:
                                  hp * N_T + (kc + 1) * 128],
                               qT[r0:r0 + 64, hp * N_T + qg * 512 + off:
                                  hp * N_T + (qg + 1) * 512],
                               start=True, stop=True), op)
            elif k == "pv_mm":
                hp, kc, head, par = a["hp"], a["kc"], a["head"], a["par"]
                slot, n, off = a["slot"], a["n"], a["off"]
                h = 2 * hp + head
                fin(eng.matmul(pb[3 + 2 * head + par][0:65, off:512],
                               Vsb[:, kc * NH * 65 + h * 65:
                                   kc * NH * 65 + (h + 1) * 65],
                               Pbuf[:, slot * 512: slot * 512 + n],
                               start=a["start"], stop=a["stop"]), op)
            elif k == "rb_bcast":
                head = a["head"]
                fin(eng.matmul(pb[7][0:64, :], sonesb[0:2, 0:64],
                               rrows[0:2, head * 1024: head * 1024 + 512],
                               start=True, stop=True), op)
            elif k == "wo_mm":
                tt, oh, dc = a["tt"], a["oh"], a["dc"]
                fin(eng.matmul(pb[oh][:],
                               qT[:, dc * N_T + tt * 128: dc * N_T + (tt + 1) * 128],
                               A2S[:, dc * 1024 + oh * 512: dc * 1024 + (oh + 1) * 512],
                               start=a["start"], stop=a["stop"]), op)
            elif k == "ffn_pre":
                oh = a["oh"]
                fin(eng.matmul(pb[2 + oh][:], sonesb[0:1, :],
                               sbiasb[0:1, oh * 512:(oh + 1) * 512],
                               start=True, stop=False), op)
            elif k == "ffn_mm":
                tt, oh, ic = a["tt"], a["oh"], a["ic"]
                fin(eng.matmul(pb[2 + oh][:], xnt(ic, tt * 128, 128),
                               Wbig[:, ic * 1024 + oh * 512: ic * 1024 + (oh + 1) * 512],
                               start=False, stop=a["stop"]), op)
            else:
                raise KeyError(k)

        @block.sync
        def _(eng):
            for op in sched.streams["sync"]:
                waits(eng, op)
                e_sync(eng, op)

        @block.gpsimd
        def _(eng):
            for op in sched.streams["gp"]:
                waits(eng, op)
                e_gp(eng, op)

        @block.scalar
        def _(eng):
            for op in sched.streams["act"]:
                waits(eng, op)
                e_act(eng, op)

        @block.vector
        def _(eng):
            for op in sched.streams["dve"]:
                waits(eng, op)
                e_dve(eng, op)

        @block.tensor
        def _(eng):
            for op in sched.streams["pe"]:
                waits(eng, op)
                e_pe(eng, op)

    return nc


_NC_CACHE = None


def _get_nc():
    global _NC_CACHE
    if _NC_CACHE is None:
        _NC_CACHE = build_nc()
    return _NC_CACHE


def _host_inputs(L0_attn_x, L0_attn_Wqkv, L0_attn_Wo, L0_attn_log_xi,
                 L0_attn_pi_gate_logit, L0_attn_e_gate_logit,
                 L0_ffn_G1_R, L0_ffn_G2_R, L0_ffn_G1_I, L0_ffn_G2_I,
                 L0_ffn_bias_real, L0_ffn_bias_imag):
    import ml_dtypes
    x = np.asarray(L0_attn_x, dtype=np.float32)
    Wqkv = np.asarray(L0_attn_Wqkv, dtype=np.float32)
    Wo = np.asarray(L0_attn_Wo, dtype=np.float32)

    idx = np.empty(D, dtype=np.int64)
    for h in range(NH):
        for par in range(2):
            for j in range(32):
                idx[h * 64 + par * 32 + j] = h * 64 + 2 * j + par
    WqkvT = np.ascontiguousarray(Wqkv.T)
    perm = np.concatenate([idx, D + idx, 2 * D + np.arange(D)])
    WqkvT = np.ascontiguousarray(WqkvT[:, perm])
    WoT = np.ascontiguousarray(Wo.T)

    G1R = np.asarray(L0_ffn_G1_R, dtype=np.float32)[0]
    G2R = np.asarray(L0_ffn_G2_R, dtype=np.float32)[:, :, 0, :]
    G1I = np.asarray(L0_ffn_G1_I, dtype=np.float32)[0]
    G2I = np.asarray(L0_ffn_G2_I, dtype=np.float32)[:, :, 0, :]
    g1 = np.concatenate(
        [np.ascontiguousarray(G1R.transpose(1, 0, 2)).reshape(4, 1024),
         np.ascontiguousarray(G1I.transpose(1, 0, 2)).reshape(4, 1024)], axis=1)
    g2 = np.concatenate(
        [np.ascontiguousarray(G2R.transpose(0, 2, 1)).reshape(4, 256),
         np.ascontiguousarray(G2I.transpose(0, 2, 1)).reshape(4, 256)], axis=1)

    sm = np.zeros((128, 1120), dtype=np.float32)
    pig = np.asarray(L0_attn_pi_gate_logit, dtype=np.float32)
    eg = np.asarray(L0_attn_e_gate_logit, dtype=np.float32)
    lx = np.asarray(L0_attn_log_xi, dtype=np.float32)
    for hp in range(NHP):
        sm[0:64, hp] = pig[2 * hp]
        sm[64:128, hp] = pig[2 * hp + 1]
    sm[:, 8:24] = eg[None, :]
    sm[:, 24:40] = lx[None, :]
    p = np.arange(128, dtype=np.float32)
    for i, (qg, kc) in enumerate(QGKC):
        sm[:, 40 + i] = p + 128 * kc - 512 * qg - 511
    sm[:, 95] = 1e-5
    sm[0, 96:96 + D] = np.concatenate(
        [np.asarray(L0_ffn_bias_real, np.float32),
         np.asarray(L0_ffn_bias_imag, np.float32)])

    tri = np.triu(np.ones((128, 128), np.float32))
    maskb = np.concatenate([tri, np.eye(128, dtype=np.float32)],
                           axis=1).astype(ml_dtypes.bfloat16)

    j = np.arange(32, dtype=np.float64)
    f = np.pi ** (1.0 - j / 32.0)
    t = np.arange(N_T, dtype=np.float64)
    base = np.outer(f, t) / (2 * np.pi)
    pospm = np.zeros((128, N_T), dtype=np.float32)
    for H in range(2):
        pospm[H * 64 + 0:H * 64 + 32] = -base
        pospm[H * 64 + 32:H * 64 + 64] = base
    ident = np.eye(128, dtype=np.float32)
    return x, WqkvT, WoT, g1, g2, sm, maskb, pospm, ident


def kernel(**inputs):
    x, WqkvT, WoT, g1, g2, sm, maskb, pospm, ident = _host_inputs(**inputs)
    nc = _get_nc()
    in_maps = [dict(x=np.ascontiguousarray(x[c]), wqkvT=WqkvT, woT=WoT, g1=g1,
                    g2=g2, smalls=sm, maskb=maskb, pospm=pospm, ident=ident)
               for c in range(8)]
    res = run_bass_kernel_spmd(nc, in_maps, core_ids=list(range(8)))
    return np.stack([np.asarray(r["out"]) for r in res.results],
                    axis=0).astype(np.float32)


# version 36 (final)
# speedup vs baseline: 1.1140x; 1.0546x over previous
import numpy as np

import concourse.bass as bass
import concourse.mybir as mybir
from concourse.bass_utils import run_bass_kernel_spmd
from concourse.alu_op_type import AluOpType

F32 = mybir.dt.float32
BF16 = mybir.dt.bfloat16
AF = mybir.ActivationFunctionType
MAGIC = 12582912.0

N_T = 2048
D = 1024
NH = 16
NTT = 16
NQG = 4
NHP = 8

QGKC = [(qg, kc) for qg in range(NQG) for kc in range(4 * qg + 4)]
QGKC_IDX = {p: i for i, p in enumerate(QGKC)}


def n_cols(qg, kc):
    off = max(0, kc * 128 - qg * 512)
    return 512 - off, off


class Op:
    __slots__ = ["kind", "args", "waits", "inc", "stream"]

    def __init__(self, kind, args):
        self.kind, self.args, self.waits, self.inc, self.stream = kind, args, [], False, None


class Sched:
    def __init__(self):
        self.streams = {n: [] for n in ["pe", "act", "dve", "sync", "gp"]}

    def emit(self, stream, kind, waits=(), **args):
        op = Op(kind, args)
        op.stream = stream
        for w in waits:
            if w is not None:
                op.waits.append(w)
        self.streams[stream].append(op)
        return (stream, len(self.streams[stream]) - 1)

    NDMA = {"dma_small": 3, "dma_pospm": 1, "dma_x": 1, "dma_x2": 1,
            "dma_g": 2, "dma_wv": 1, "dma_wqk": 1, "dma_wo": 1,
            "qe_swap": 4, "rearr": 4, "dma_out": 1, "final": 0,
            "xnt_dma": 4, "ldiv_d1": 1, "ldiv_d2": 2}

    def finalize(self):
        for ops in self.streams.values():
            for op in ops:
                for (sn, i) in op.waits:
                    self.streams[sn][i].inc = True
        self.incval = {}
        for name, ops in self.streams.items():
            c, vals = 0, []
            for op in ops:
                if name in ("sync", "gp") or op.kind in self.NDMA:
                    c += 16 * self.NDMA[op.kind]
                    op.inc = True
                elif op.inc:
                    c += 1
                vals.append(c)
            self.incval[name] = vals

    def wval(self, sn, i):
        return self.incval[sn][i]


def build_schedule():
    s = Sched()
    E = s.emit

    ld_small = E("sync", "dma_small")
    sig = E("act", "gates", waits=[ld_small])
    btab_a = E("dve", "btab_a", waits=[sig])

    ln_norm = [None] * NTT
    xnt_ev = [None] * NTT
    tp_last = [None] * NTT
    sqs = [None] * NTT
    tp_ops = [[None, None] for _ in range(NTT)]
    ev_ops = [[None, None] for _ in range(NTT)]
    stage_chain = None
    wv_cast = []
    wqk_cast = {}

    def emit_back(t):
        f2 = E("dve", "ln_fin2", waits=[sqs[t]], tt=t)
        f2b = E("dve", "ln_fin2b", waits=[f2], tt=t)
        ln_norm[t] = E("dve", "ln_norm", waits=[f2b], tt=t)
        if t >= 1:
            for grp in range(2):
                ev_ops[t - 1][grp] = E("dve", "xnt_ev",
                                       waits=[tp_ops[t - 1][grp]],
                                       tt=t - 1, grp=grp)
            xnt_ev[t - 1] = (ev_ops[t - 1][0], ev_ops[t - 1][1])
        for grp in range(2):
            tw = [ln_norm[t]]
            if t >= 1:
                tw.append(ev_ops[t - 1][grp])
            else:
                tw.append(ld_small)
            tp_ops[t][grp] = E("pe", "xnt_tp", waits=tw, tt=t, grp=grp)
        tp_last[t] = tp_ops[t][1]

    for tt in range(NTT):
        w = [tp_last[tt - 4]] if tt >= 4 else []
        xd = E("sync", "dma_x", waits=w, tt=tt)
        st0 = E("dve", "ln_stats", waits=[xd], tt=tt)
        f1 = E("dve", "ln_fin1", waits=[st0], tt=tt)
        f1b = E("dve", "ln_fin1b", waits=[f1], tt=tt)
        f1c = E("dve", "ln_fin1c", waits=[f1b], tt=tt)
        sqs[tt] = E("act", "ln_sqrt", waits=[f1c], tt=tt)
        if tt < 8:
            d = E("gp", "dma_wv", waits=[stage_chain], ic=tt)
            stage_chain = E("dve", "wv_cast", waits=[d], ic=tt)
            wv_cast.append(stage_chain)
        elif tt < 10:
            sec = tt - 8
            d = E("gp", "dma_wqk", waits=[stage_chain], sec=sec, hp=0)
            stage_chain = E("dve", "wqk_cast", waits=[d], sec=sec, hp=0, slot=sec)
            wqk_cast[(sec, 0)] = stage_chain
        if tt >= 1:
            emit_back(tt - 1)
    emit_back(NTT - 1)
    for grp in range(2):
        ev_ops[NTT - 1][grp] = E("dve", "xnt_ev",
                                 waits=[tp_ops[NTT - 1][grp]],
                                 tt=NTT - 1, grp=grp)
    xnt_ev[NTT - 1] = (ev_ops[NTT - 1][0], ev_ops[NTT - 1][1])
    tp_last[NTT - 1] = tp_ops[NTT - 1][1]

    ones_set = E("dve", "consts_set")
    v_ev = [None] * NTT
    for tt in range(NTT):
        last = None
        for ic in range(8):
            for oh in range(2):
                w = []
                if ic == 0 and oh == 0:
                    w = list(xnt_ev[tt]) + [wv_cast[-1]]
                    if tt >= 2:
                        w.append(v_ev[tt - 2])
                last = E("pe", "v_mm", waits=w, tt=tt, ic=ic, oh=oh,
                         start=(ic == 0), stop=(ic == 7))
        v_ev[tt] = E("dve", "v_ev", waits=[last, ones_set], tt=tt)

    ld_pospm = E("sync", "dma_pospm",
                 waits=[ln_norm[NTT - 1], tp_last[NTT - 1]])
    last_qk_sect = {}
    rot = {}
    prev_sin = prev_cos = prev_frc = None
    pack_rot_guard = {0: None, 1: None}
    rc_of_pair = {}
    rs_of_pair = {}
    rot_by_buf = [None, None, None, None]
    scos_by_half = {}
    g = 0
    halves = [(hp, half) for hp in range(NHP) for half in range(2)]

    def trig_full(hp, half):
        nonlocal prev_sin, prev_cos, prev_frc
        u = E("act", "trig_u", waits=[ld_pospm, sig, prev_frc], hp=hp, half=half)
        rt = E("dve", "trig_rt", waits=[u, prev_cos], hp=hp, half=half)
        fs = E("dve", "trig_frs", waits=[prev_sin, rt], hp=hp, half=half)
        sw_sin = [fs, pack_rot_guard[half]]
        if hp == 0 and half == 1:
            sw_sin += [v_ev[NTT - 1], v_ev[NTT - 2]]
        ssin = E("act", "trig_sin", waits=sw_sin, hp=hp, half=half)
        u25 = E("dve", "trig_u25", waits=[fs], hp=hp, half=half)
        rt2 = E("dve", "trig_rt2", waits=[u25], hp=hp, half=half)
        frc = E("dve", "trig_frc", waits=[ssin, rt2], hp=hp, half=half)
        scos = E("act", "trig_cos", waits=[frc], hp=hp, half=half)
        prev_sin, prev_cos, prev_frc = ssin, scos, frc
        scos_by_half[(hp, half)] = scos

    trig_full(*halves[0])
    for i, (hp, half) in enumerate(halves):
        scos = scos_by_half[(hp, half)]
        last_rot_this_half = None
        for sec in range(2):
            slot = sec if hp == 0 else 2 + ((hp - 1) * 2 + sec) % 6
            if half == 0 and hp > 0:
                d = E("gp", "dma_wqk", waits=[stage_chain], sec=sec, hp=hp)
                stage_chain = E("dve", "wqk_cast",
                                waits=[d, last_qk_sect.get((sec, hp - 3))],
                                sec=sec, hp=hp, slot=slot)
                wqk_cast[(sec, hp)] = stage_chain
            for tg in (2 * half, 2 * half + 1):
                b = g % 2
                buf = g % 4
                w = ([wqk_cast[(sec, hp)], rc_of_pair.get(g - 2)]
                     + list(xnt_ev[4 * tg + 3]))
                if g == 1:
                    w += [ev_ops[NTT - 1][0], ev_ops[NTT - 1][1]]
                last = None
                for ic in range(8):
                    last = E("pe", "qk_mm", waits=(w if ic == 0 else []),
                             sec=sec, hp=hp, tg=tg, ic=ic, pbank=4 + 2 * b,
                             slot=slot, start=(ic == 0), stop=(ic == 7))
                sww = [rs_of_pair.get(g - 2)]
                lastsw = E("pe", "qksw_mm", waits=sww, sec=sec, hp=hp, tg=tg,
                           pbank=5 + 2 * b, slot=slot)
                last_qk_sect[(sec, hp)] = lastsw
                rc = E("dve", "rot_c", waits=[last, scos, rot_by_buf[buf]],
                       tg=tg, buf=buf, pbank=4 + 2 * b)
                rs = E("dve", "rot_s", waits=[lastsw], tg=tg, buf=buf,
                       pbank=5 + 2 * b)
                r3 = E("dve", "rot_add", waits=[rs, rc], sec=sec, hp=hp, tg=tg, buf=buf)
                rot[(sec, hp, tg)] = r3
                rot_by_buf[buf] = r3
                rc_of_pair[g] = rc
                rs_of_pair[g] = rs
                last_rot_this_half = r3
                g += 1
        pack_rot_guard[half] = last_rot_this_half
        if i + 1 < len(halves):
            trig_full(*halves[i + 1])
    a2_close = prev_cos

    ld_g = E("sync", "dma_g", waits=[a2_close, rot_by_buf[0], rot_by_buf[1],
                                     rot_by_buf[2], rot_by_buf[3]])
    mpev_all = {}
    mpev_by_bank = {4: None, 6: None}
    oev_par1 = {0: None, 1: None}
    rearr_all = []
    wct_cast = []
    mp_state = {"i": 0}
    rearr_slots = {}
    for si, (shp, sqg) in enumerate([(h, q) for h in (3, 4, 5, 6)
                                     for q in range(4)]):
        m = si // 8
        dcq = (si % 8) // 2
        cis = (0, 1, 2, 3) if si % 2 == 0 else (4, 5, 6, 7)
        rearr_slots[(shp, sqg)] = (m, dcq, cis)

    def emit_mprime_pair(bank_guard):
        for t in range(2):
            mpi = mp_state["i"]
            m, ch = divmod(mpi, 8)
            bank = 4 + 2 * t
            mm = E("pe", "mprime_mm", waits=[ld_g, bank_guard],
                   m=m, ch=ch, bank=bank)
            ev = E("dve", "mprime_ev",
                   waits=[mm] + ([a2_close] if mpi == 0 else []),
                   m=m, ch=ch, bank=bank)
            mpev_all[(m, ch)] = ev
            mpev_by_bank[bank] = ev
            mp_state["i"] += 1

    def emit_rearr_batch(hp, qg):
        m, dcq, cis = rearr_slots[(hp, qg)]
        for ci in cis:
            rr = E("sync" if ci % 2 else "gp", "rearr",
                   waits=[mpev_all[(m, ci)],
                          (wct_cast[-1] if wct_cast else None)],
                   m=m, ci=ci, dcq=dcq)
            rearr_all.append(rr)
        if cis[0] == 4:
            w0 = [rearr_all[-1], rearr_all[-2]]
            if not wct_cast:
                w0 += [v_ev[NTT - 1], v_ev[NTT - 2]]
            wct_cast.append(E("dve", "wct_cast", waits=w0, m=m, dcq=dcq,
                              which=0))
            wct_cast.append(E("dve", "wct_cast", waits=[], m=m, dcq=dcq,
                              which=1))

    wo_cast = []
    for dc in range(8):
        d = E("gp", "dma_wo", waits=[stage_chain if dc == 0 else wo_cast[-1],
                                     a2_close if dc == 0 else None], dc=dc)
        cw = [d]
        if dc == 0:
            cw += [a2_close, rot_by_buf[0], rot_by_buf[1], rot_by_buf[2],
                   rot_by_buf[3]] + list(xnt_ev[15]) + list(xnt_ev[14])
        wo_cast.append(E("dve", "wo_cast", waits=cw, dc=dc))

    s_hist = []
    exp_hist = []
    pv_done = []
    div_hist = []
    prev_div = {}
    prev_lev = {}
    prev_d1 = {}
    prev_d2 = {}
    prev_oev = {}
    first_pv_guard = {(0, 0): v_ev[NTT - 1], (0, 1): rc_of_pair[62],
                      (1, 0): rs_of_pair[62], (1, 1): rc_of_pair[63]}
    pending = []
    for hp in range(NHP):
        btab = E("dve", "btab_hp",
                 waits=[btab_a, exp_hist[-1] if exp_hist else None], hp=hp)
        for qg in range(NQG):
            w = 4 * qg + 4
            gates = {}

            def emit_s(kc):
                for head in range(2):
                    n, off = n_cols(qg, kc)
                    si = len(s_hist)
                    waits = []
                    if kc == 0 and head == 0:
                        waits = [btab, rot[(1, hp, qg)], rot[(0, hp, qg)]]
                        if si < 4:
                            waits += [v_ev[NTT - 1], v_ev[NTT - 2]]
                    if si == 3:
                        waits.append(rs_of_pair[63])
                    if si >= 4:
                        waits.append(exp_hist[si - 4])
                    smm = E("pe", "s_mm", waits=waits, hp=hp, qg=qg, kc=kc,
                            head=head, bank=si % 4, n=n, off=off)
                    s_hist.append(smm)
                    ei = len(exp_hist)
                    ew = [smm]
                    if ei >= 4:
                        ew.append(pv_done[ei - 4])
                    ex = E("act", "p_exp", waits=ew, hp=hp, qg=qg, kc=kc,
                           head=head, bank=si % 4, slot=ei % 4, n=n)
                    exp_hist.append(ex)
                    gate = ex
                    if kc >= 4 * qg:
                        gate = E("dve", "p_mask", waits=[ex], slot=ei % 4)
                    gates[(kc, head)] = (gate, ei % 4)

            def emit_pv(kc):
                n, off = n_cols(qg, kc)
                par = qg % 2
                for head in range(2):
                    gate, slot = gates[(kc, head)]
                    waits_ = [gate]
                    if kc == 0:
                        waits_.append(prev_div.get((head, par),
                                                   first_pv_guard[(head, par)]))
                        if par == 1:
                            waits_.append(mpev_by_bank[4 + 2 * head])
                    pv = E("pe", "pv_mm", waits=waits_, hp=hp, qg=qg, kc=kc,
                           head=head, par=par, slot=slot, n=n, off=off,
                           start=(kc == 0), stop=(kc == w - 1))
                    pv_done.append(pv)

            emit_s(0)
            emit_s(1)
            for kc in range(w):
                if kc == min(6, w - 1):
                    while pending:
                        args = pending.pop(0)
                        bc = E("pe", "rb_bcast",
                               waits=[args["d2"], args["oev"]],
                               head=args["head"], par=args["par"])
                        dv = E("dve", "div_mult", waits=[bc, args["oev"]],
                               hp=args["hp"], qg=args["qg"], head=args["head"],
                               par=args["par"])
                        div_hist.append(dv)
                        prev_div[(args["head"], args["par"])] = dv
                if kc == 8 and qg == 2 and hp <= 3:
                    emit_mprime_pair(div_hist[-1])
                emit_pv(kc)
                if kc + 2 < w:
                    emit_s(kc + 2)
            if qg == 0 and 1 <= hp <= 4:
                emit_mprime_pair(div_hist[-1])
            par = qg % 2
            for head in range(2):
                lastA = pv_done[-1]
                dh = div_hist[-1] if div_hist else None
                lev = E("dve", "l_ev", waits=[lastA, prev_d1.get(head), dh],
                        head=head, par=par)
                oev = E("dve", "o_ev", waits=[lastA, dh], head=head, par=par)
                if par == 1:
                    oev_par1[head] = oev
                d1 = E("sync", "ldiv_d1", waits=[lev, prev_d2.get(head)], head=head)
                rc = E("dve", "recl", waits=[d1], head=head)
                rhi = E("dve", "recl_hi", waits=[rc], head=head)
                rhf = E("dve", "recl_hif", waits=[rhi], head=head)
                rlo = E("dve", "recl_lo", waits=[rhf], head=head)
                d2 = E("sync", "ldiv_d2", waits=[rlo, dh], head=head)
                pending.append({"d2": d2, "oev": oev, "hp": hp, "qg": qg,
                                "head": head, "par": par})
                prev_d1[head] = d1
                prev_d2[head] = d2
            if (hp, qg) in rearr_slots:
                emit_rearr_batch(hp, qg)
    while pending:
        args = pending.pop(0)
        bc = E("pe", "rb_bcast", waits=[args["d2"], args["oev"]],
               head=args["head"], par=args["par"])
        dv = E("dve", "div_mult", waits=[bc, args["oev"]], hp=args["hp"],
               qg=args["qg"], head=args["head"], par=args["par"])
        div_hist.append(dv)
        prev_div[(args["head"], args["par"])] = dv
    div_all = div_hist[-1]
    wct_done = wct_cast[-1]

    res = {}
    outs = []
    for tt in range(NTT):
        w = [wct_done]
        if tt >= 2:
            w += [res[(tt - 2, 0, "xy")], res[(tt - 2, 1, "xy")]]
        x2 = E("sync", "dma_x2", waits=w, tt=tt)
        for oh in range(2):
            wlast = None
            for dc in range(8):
                ww = []
                if dc == 0:
                    ww = [wo_cast[-1], div_all]
                    if tt >= 1:
                        ww.append(res[(tt - 1, oh, "xy")])
                wlast = E("pe", "wo_mm", waits=ww, tt=tt, oh=oh, dc=dc,
                          start=(dc == 0), stop=(dc == 7))
            fw = [wct_done, div_all]
            if tt >= 1:
                fw.append(res[(tt - 1, oh, "f")])
            flast = E("pe", "ffn_pre", waits=fw, tt=tt, oh=oh)
            for ic in range(8):
                flast = E("pe", "ffn_mm", waits=[], tt=tt, oh=oh, ic=ic,
                          stop=(ic == 7))
            rw = [x2, wlast]
            if tt >= 2:
                rw.append(outs[tt - 2])
            r1 = E("dve", "res_xy", waits=rw, tt=tt, oh=oh)
            r2 = E("dve", "res_f", waits=[flast, r1], tt=tt, oh=oh)
            res[(tt, oh, "xy")] = r1
            res[(tt, oh, "f")] = r2
        outs.append(E("gp", "dma_out", waits=[res[(tt, 0, "f")], res[(tt, 1, "f")]],
                      tt=tt))
    E("gp", "final", waits=[outs[-1]])
    s.finalize()
    return s


def build_nc():
    sched = build_schedule()
    nc = bass.Bass("TRN2", target_bir_lowering=False)

    dx = nc.dram_tensor("x", [N_T, D], F32, kind="ExternalInput")
    dwqkv = nc.dram_tensor("wqkvT", [D, 3 * D], F32, kind="ExternalInput")
    dwo = nc.dram_tensor("woT", [D, D], F32, kind="ExternalInput")
    dg1 = nc.dram_tensor("g1", [4, 2048], F32, kind="ExternalInput")
    dg2 = nc.dram_tensor("g2", [4, 512], F32, kind="ExternalInput")
    dsm = nc.dram_tensor("smalls", [128, 1120], F32, kind="ExternalInput")
    dmaskb = nc.dram_tensor("maskb", [128, 256], BF16, kind="ExternalInput")
    dpospm = nc.dram_tensor("pospm", [128, N_T], F32, kind="ExternalInput")
    dident = nc.dram_tensor("ident", [128, 128], F32, kind="ExternalInput")
    dout = nc.dram_tensor("out", [N_T, D], F32, kind="ExternalOutput")

    from contextlib import ExitStack
    with ExitStack() as stk:
        en = stk.enter_context
        xnT = en(nc.sbuf_tensor("xnT", [128, 8 * N_T], BF16))
        qT = en(nc.sbuf_tensor("qT", [128, 8 * N_T], BF16))
        kT = en(nc.sbuf_tensor("kT", [128, 8 * N_T], BF16))
        Vsb = en(nc.sbuf_tensor("Vsb", [128, NTT * NH * 65], BF16))
        Wbig = en(nc.sbuf_tensor("Wbig", [128, 8192], BF16))
        Pbuf = en(nc.sbuf_tensor("Pbuf", [128, 4 * 512], BF16))
        A2S = en(nc.sbuf_tensor("A2S", [128, 8192], BF16))
        sbiasb = en(nc.sbuf_tensor("sbiasb", [1, D], BF16))
        smaskb = en(nc.sbuf_tensor("smaskb", [128, 128], BF16))
        sonesb = en(nc.sbuf_tensor("sonesb", [2, 128], BF16))
        XNTILE, WQKB, QE1, QE2, PACKC, PACKS = 0, 2048, 4096, 5120, 6144, 7168

        FA = en(nc.sbuf_tensor("FA", [128, 7264], F32))
        XBUF, TRU, TRT, TRF, WST = 0, 2048, 3072, 4096, 5120
        LSB, LQ, BTAB = 6144, 7168, 7184
        FAW = 7264
        sm = en(nc.sbuf_tensor("sm", [128, 1120], F32))
        odiv32 = en(nc.sbuf_tensor("odiv32", [64, 1024], F32))
        print("SBUF bytes remaining after alloc:", nc.sbuf_bytes_remaining)
        lns = en(nc.sbuf_tensor("lns", [128, 16], F32))
        hilo = en(nc.sbuf_tensor("hilo", [128, 16], BF16))
        rrows = en(nc.sbuf_tensor("rrows", [128, 2048], BF16))
        C_PIG, C_EG, C_LX, C_PAR, C_BIAS = 0, 8, 24, 40, 96

        pb = [en(nc.psum_tensor("pb%d" % i, [128, 512], F32)) for i in range(8)]

        sems = {n: en(nc.semaphore("sem_" + n)) for n in sched.streams}
        block = en(nc.Block())

        def waits(eng, op):
            for (sn, i) in op.waits:
                eng.wait_ge(sems[sn], sched.wval(sn, i))

        def fin(instr, op):
            if op.inc:
                instr.then_inc(sems[op.stream],
                               16 if op.stream in ("sync", "gp") else 1)

        def dinc(instr, op):
            instr.then_inc(sems[op.stream], 16)
            return instr

        def xnt(ic, t0, n):
            return xnT[:, ic * N_T + t0: ic * N_T + t0 + n]

        def qe1sl(buf, r0=0, r1=128):
            if buf < 2:
                return A2S[r0:r1, QE1 + buf * 512: QE1 + (buf + 1) * 512]
            return A2S[r0:r1, XNTILE + (buf - 2) * 512: XNTILE + (buf - 1) * 512]

        def qe2sl(buf, r0=0, r1=128):
            if buf < 2:
                return A2S[r0:r1, QE2 + buf * 512: QE2 + (buf + 1) * 512]
            return A2S[r0:r1, XNTILE + 1024 + (buf - 2) * 512:
                       XNTILE + 1024 + (buf - 1) * 512]

        def emit_xnt(eng, op):
            a = op.args
            tt, half = a["tt"], a["half"]
            b2 = tt % 2
            for ic in range(half * 4, half * 4 + 4):
                dinc(eng.dma_start_transpose(
                    xnT[:, ic * N_T + tt * 128: ic * N_T + tt * 128 + 128],
                    A2S[:, XNTILE + b2 * D + ic * 128:
                        XNTILE + b2 * D + (ic + 1) * 128]), op)

        def emit_rearr(eng, op):
            a = op.args
            m, ci, dcq = a["m"], a["ci"], a["dcq"]
            reg = TRU if m == 0 else XBUF
            for bp in range(4):
                p0 = 8 * dcq + 32 * bp
                src = FA[p0:p0 + 8, reg + ci * 256: reg + (ci + 1) * 256]
                src = src.rearrange("p (g e) -> p g e", g=16)
                c0 = TRF + (4 * ci + bp) * 16
                dst = FA[:, c0:c0 + 16]
                dinc(eng.dma_start(out=dst, in_=src), op)

        def e_sync(eng, op):
            k, a = op.kind, op.args
            if k == "dma_small":
                dinc(eng.dma_start(out=sm[:], in_=dsm[:, :]), op)
                dinc(eng.dma_start(out=smaskb[:], in_=dmaskb[:, 0:128]), op)
                dinc(eng.dma_start(out=FA[:, TRF:TRF + 128], in_=dident[:, :]), op)
            elif k == "dma_x":
                tt = a["tt"]
                b = tt % 4
                dinc(eng.dma_start(out=FA[:, XBUF + b * D: XBUF + (b + 1) * D],
                                   in_=dx[tt * 128:(tt + 1) * 128, :]), op)
            elif k == "dma_x2":
                tt = a["tt"]
                b = tt % 2
                dinc(eng.dma_start(out=FA[:, XBUF + b * D: XBUF + (b + 1) * D],
                                   in_=dx[tt * 128:(tt + 1) * 128, :]), op)
            elif k == "xnt_dma":
                emit_xnt(eng, op)
            elif k == "dma_pospm":
                dinc(eng.dma_start(out=FA[:, XBUF:XBUF + N_T], in_=dpospm[:, :]), op)
            elif k == "rearr":
                emit_rearr(eng, op)
            elif k == "xnt_dma":
                emit_xnt(eng, op)
            elif k == "qe_swap":
                buf = a["buf"]
                for blk in range(4):
                    src = [32, 0, 96, 64][blk]
                    dinc(eng.dma_start(out=qe2sl(buf, blk * 32, (blk + 1) * 32),
                                       in_=qe1sl(buf, src, src + 32)), op)
            elif k == "ldiv_d1":
                head = a["head"]
                src = FA[0:1, LSB + head * 512: LSB + (head + 1) * 512]
                dst = FA[:, LQ + head * 4: LQ + (head + 1) * 4]
                dinc(eng.dma_start(out=dst, in_=src), op)
            elif k == "ldiv_d2":
                head = a["head"]
                dinc(eng.dma_start(
                    out=rrows[0:1, head * 1024: head * 1024 + 512],
                    in_=hilo[:, head * 4: head * 4 + 4]), op)
                dinc(eng.dma_start(
                    out=rrows[1:2, head * 1024: head * 1024 + 512],
                    in_=hilo[:, 8 + head * 4: 8 + (head + 1) * 4]), op)
            elif k == "dma_g":
                dinc(eng.dma_start(out=FA[0:4, XBUF:XBUF + 2048], in_=dg1[:, :]), op)
                dinc(eng.dma_start(out=FA[0:4, TRF + 512:TRF + 1024], in_=dg2[:, :]), op)
            else:
                raise KeyError(k)

        def e_gp(eng, op):
            k, a = op.kind, op.args
            if k == "dma_wv":
                ic = a["ic"]
                dinc(eng.dma_start(out=FA[:, WST:WST + 1024],
                                   in_=dwqkv[ic * 128:(ic + 1) * 128, 2048:3072]), op)
            elif k == "dma_wqk":
                sec, hp = a["sec"], a["hp"]
                c0 = sec * D + hp * 128
                src = bass.AP(dwqkv, c0, [[3 * D, 128], [128 * 3 * D, 8], [1, 128]])
                dinc(eng.dma_start(out=FA[:, WST:WST + 1024], in_=src), op)
            elif k == "dma_wo":
                dc = a["dc"]
                dinc(eng.dma_start(out=FA[:, WST:WST + 1024],
                                   in_=dwo[dc * 128:(dc + 1) * 128, :]), op)
            elif k == "qe_swap":
                buf = a["buf"]
                for blk in range(4):
                    src = [32, 0, 96, 64][blk]
                    dinc(eng.dma_start(out=qe2sl(buf, blk * 32, (blk + 1) * 32),
                                       in_=qe1sl(buf, src, src + 32)), op)
            elif k == "rearr":
                emit_rearr(eng, op)
            elif k == "xnt_dma":
                emit_xnt(eng, op)
            elif k == "dma_out":
                tt = a["tt"]
                dinc(eng.dma_start(out=dout[tt * 128:(tt + 1) * 128, :],
                                   in_=FA[:, TRU + (tt % 2) * D: TRU + (tt % 2 + 1) * D]), op)
            elif k == "final":
                pass
            else:
                raise KeyError(k)

        def e_act(eng, op):
            k, a = op.kind, op.args
            if k == "gates":
                eng.activation(out=sm[:, C_PIG:C_PIG + 8], in_=sm[:, C_PIG:C_PIG + 8],
                               func=AF.Sigmoid)
                eng.activation(out=sm[:, C_EG:C_EG + 16], in_=sm[:, C_EG:C_EG + 16],
                               func=AF.Sigmoid)
                fin(eng.activation(out=sm[:, C_LX:C_LX + 16], in_=sm[:, C_LX:C_LX + 16],
                                   func=AF.Exp, scale=-1.0), op)
            elif k == "xnt_dma":
                emit_xnt(eng, op)
            elif k == "ln_sqrt":
                b = (a["tt"] % 2) * 8
                fin(eng.activation(out=lns[:, b + 4:b + 5], in_=lns[:, b + 3:b + 4],
                                   func=AF.Sqrt, bias=sm[:, 95:96]), op)
            elif k == "trig_u":
                hp, half = a["hp"], a["half"]
                fin(eng.activation(out=FA[:, TRU:TRU + 1024],
                                   in_=FA[:, XBUF + half * 1024: XBUF + (half + 1) * 1024],
                                   func=AF.Copy, scale=sm[:, C_PIG + hp:C_PIG + hp + 1]), op)
            elif k == "trig_sin":
                out = (A2S[:, PACKS:PACKS + 1024] if a["half"] == 0
                       else Wbig[:, 0:1024])
                fin(eng.activation(out=out, in_=FA[:, TRF:TRF + 1024],
                                   func=AF.Sin, scale=2 * np.pi), op)
            elif k == "trig_cos":
                out = (A2S[:, PACKC:PACKC + 1024] if a["half"] == 0
                       else Wbig[:, 1024:2048])
                fin(eng.activation(out=out, in_=FA[:, TRT:TRT + 1024],
                                   func=AF.Sin, scale=2 * np.pi), op)
            elif k == "qe_ev":
                bank, buf = a["bank"], a["buf"]
                fin(eng.copy(out=qe1sl(buf), in_=pb[4 + bank][:]), op)
            elif k == "p_exp":
                bank, slot, n = a["bank"], a["slot"], a["n"]
                col = BTAB + a["head"] * 40 + QGKC_IDX[(a["qg"], a["kc"])]
                pbs = pb[bank] if bank < 3 else pb[7]
                fin(eng.activation(out=Pbuf[:, slot * 512: slot * 512 + n],
                                   in_=pbs[:, 0:n], func=AF.Exp, scale=0.125,
                                   bias=FA[:, col:col + 1]), op)
            else:
                raise KeyError(k)

        def e_dve(eng, op):
            k, a = op.kind, op.args
            if k == "btab_a":
                fin(eng.tensor_mul(out=sm[:, C_EG:C_EG + 16], in0=sm[:, C_EG:C_EG + 16],
                                   in1=sm[:, C_LX:C_LX + 16]), op)
            elif k == "btab_hp":
                hp = a["hp"]
                for head in range(2):
                    h = 2 * hp + head
                    i = eng.tensor_scalar_mul(out=FA[:, BTAB + head * 40:
                                                     BTAB + (head + 1) * 40],
                                              in0=sm[:, C_PAR:C_PAR + 40],
                                              scalar1=sm[:, C_EG + h:C_EG + h + 1])
                fin(i, op)
            elif k == "v_ev":
                tt = a["tt"]
                p0, p1 = (pb[0], pb[1]) if tt % 2 == 0 else (pb[2], pb[3])
                eng.tensor_copy(out=bass.AP(Vsb, tt * NH * 65,
                                            [[16640, 128], [65, 8], [1, 64]]),
                                in_=p0[:])
                fin(eng.tensor_copy(out=bass.AP(Vsb, tt * NH * 65 + 8 * 65,
                                                [[16640, 128], [65, 8], [1, 64]]),
                                    in_=p1[:]), op)
            elif k == "ln_norm":
                tt = a["tt"]
                b4, b = tt % 4, (tt % 2) * 8
                sl = FA[:, XBUF + b4 * D: XBUF + (b4 + 1) * D]
                fin(eng.tensor_scalar(out=sl, in0=sl,
                                      scalar1=lns[:, b + 5:b + 6],
                                      scalar2=lns[:, b + 6:b + 7],
                                      op0=AluOpType.mult, op1=AluOpType.add), op)
            elif k == "ln_stats":
                tt = a["tt"]
                b4, b = tt % 4, (tt % 2) * 8
                xa = FA[:, XBUF + b4 * D: XBUF + (b4 + 1) * D]
                eng.reduce_sum(out=lns[:, b:b + 1], in_=xa, axis=mybir.AxisListType.X)
                fin(eng.scalar_tensor_tensor(out=Pbuf[:, 0:1024], in0=xa,
                                             scalar=1.0, in1=xa, op0=AluOpType.mult,
                                             op1=AluOpType.mult,
                                             accum_out=lns[:, b + 1:b + 2]), op)
            elif k == "ln_fin1":
                b = (a["tt"] % 2) * 8
                c = lambda j: lns[:, b + j:b + j + 1]
                fin(eng.tensor_scalar_mul(out=c(2), in0=c(0), scalar1=1.0 / D), op)
            elif k == "ln_fin1b":
                b = (a["tt"] % 2) * 8
                c = lambda j: lns[:, b + j:b + j + 1]
                fin(eng.scalar_tensor_tensor(out=c(7), in0=c(2), scalar=0.0, in1=c(2),
                                             op0=AluOpType.add, op1=AluOpType.mult), op)
            elif k == "ln_fin1c":
                b = (a["tt"] % 2) * 8
                c = lambda j: lns[:, b + j:b + j + 1]
                fin(eng.scalar_tensor_tensor(out=c(3), in0=c(1), scalar=1.0 / D,
                                             in1=c(7), op0=AluOpType.mult,
                                             op1=AluOpType.subtract), op)
            elif k == "ln_fin2":
                b = (a["tt"] % 2) * 8
                c = lambda j: lns[:, b + j:b + j + 1]
                fin(eng.reciprocal(out=c(5), in_=c(4)), op)
            elif k == "ln_fin2b":
                b = (a["tt"] % 2) * 8
                c = lambda j: lns[:, b + j:b + j + 1]
                fin(eng.scalar_tensor_tensor(out=c(6), in0=c(2), scalar=-1.0, in1=c(5),
                                             op0=AluOpType.mult, op1=AluOpType.mult), op)
            elif k == "xnt_ev":
                tt, grp = a["tt"], a["grp"]
                bank = pb[6 + grp]
                i = None
                for j in range(4):
                    ic = grp * 4 + j
                    i = eng.tensor_copy(out=xnT[:, ic * N_T + tt * 128:
                                                ic * N_T + tt * 128 + 128],
                                        in_=bank[:, j * 128:(j + 1) * 128])
                fin(i, op)
            elif k == "wv_cast":
                ic = a["ic"]
                fin(eng.tensor_copy(out=Wbig[:, ic * 1024:(ic + 1) * 1024],
                                    in_=FA[:, WST:WST + 1024]), op)
            elif k == "consts_set":
                eng.memset(sonesb[:], 1.0)
                for tt in range(NTT):
                    eng.memset(bass.AP(Vsb, tt * NH * 65 + 64,
                                       [[16640, 128], [65, NH]]), 1.0)
                fin(eng.tensor_copy(out=sbiasb[:], in_=sm[0:1, C_BIAS:C_BIAS + D]), op)
            elif k == "wqk_cast":
                slot = a["slot"]
                if slot < 2:
                    dst = A2S[:, WQKB + slot * 1024: WQKB + (slot + 1) * 1024]
                else:
                    dst = Wbig[:, 2048 + (slot - 2) * 1024:
                               2048 + (slot - 1) * 1024]
                fin(eng.tensor_copy(out=dst, in_=FA[:, WST:WST + 1024]), op)
            elif k == "wo_cast":
                dc = a["dc"]
                fin(eng.tensor_copy(out=A2S[:, dc * 1024:(dc + 1) * 1024],
                                    in_=FA[:, WST:WST + 1024]), op)
            elif k == "trig_rt":
                fin(eng.tensor_scalar(out=FA[:, TRT:TRT + 1024],
                                      in0=FA[:, TRU:TRU + 1024],
                                      scalar1=MAGIC, scalar2=-MAGIC,
                                      op0=AluOpType.add, op1=AluOpType.add), op)
            elif k == "trig_frs":
                fin(eng.scalar_tensor_tensor(out=FA[:, TRF:TRF + 1024],
                                             in0=FA[:, TRU:TRU + 1024], scalar=0.0,
                                             in1=FA[:, TRT:TRT + 1024],
                                             op0=AluOpType.add,
                                             op1=AluOpType.subtract), op)
            elif k == "trig_u25":
                fin(eng.tensor_scalar_add(out=FA[:, TRU:TRU + 1024],
                                          in0=FA[:, TRU:TRU + 1024], scalar1=0.25), op)
            elif k == "trig_rt2":
                fin(eng.tensor_scalar(out=FA[:, TRT:TRT + 1024],
                                      in0=FA[:, TRU:TRU + 1024],
                                      scalar1=MAGIC, scalar2=-MAGIC,
                                      op0=AluOpType.add, op1=AluOpType.add), op)
            elif k == "trig_frc":
                fin(eng.scalar_tensor_tensor(out=FA[:, TRT:TRT + 1024],
                                             in0=FA[:, TRU:TRU + 1024], scalar=0.0,
                                             in1=FA[:, TRT:TRT + 1024],
                                             op0=AluOpType.add,
                                             op1=AluOpType.subtract), op)
            elif k == "rot_c":
                buf, tg = a["buf"], a["tg"]
                t2 = (tg % 2) * 512
                c_sl = (A2S[:, PACKC + t2:PACKC + t2 + 512] if tg < 2
                        else Wbig[:, 1024 + t2:1024 + t2 + 512])
                fin(eng.tensor_mul(out=qe1sl(buf), in0=pb[a["pbank"]][:],
                                   in1=c_sl), op)
            elif k == "rot_s":
                buf, tg = a["buf"], a["tg"]
                t2 = (tg % 2) * 512
                s_sl = (A2S[:, PACKS + t2:PACKS + t2 + 512] if tg < 2
                        else Wbig[:, t2:t2 + 512])
                fin(eng.tensor_mul(out=qe2sl(buf), in0=pb[a["pbank"]][:],
                                   in1=s_sl), op)
            elif k == "rot_add":
                sec, hp, tg, buf = a["sec"], a["hp"], a["tg"], a["buf"]
                dst = (qT if sec == 0 else kT)[:, hp * N_T + tg * 512:
                                              hp * N_T + (tg + 1) * 512]
                fin(eng.tensor_add(out=dst, in0=qe1sl(buf), in1=qe2sl(buf)), op)
            elif k == "mprime_ev":
                m, ch = a["m"], a["ch"]
                reg = TRU if m == 0 else XBUF
                fin(eng.tensor_copy(out=FA[:, reg + ch * 256: reg + (ch + 1) * 256],
                                    in_=pb[a["bank"]][:, 0:256]), op)
            elif k == "wct_cast":
                m, dcq, which = a["m"], a["dcq"], a["which"]
                src = FA[:, TRF:TRF + 512]
                if m == 0 and which == 0:
                    fin(eng.tensor_copy(out=Wbig[:, dcq * 1024: dcq * 1024 + 512],
                                        in_=src), op)
                elif m == 0 and which == 1:
                    fin(eng.tensor_copy(
                        out=Wbig[:, (4 + dcq) * 1024 + 512:(4 + dcq) * 1024 + 1024],
                        in_=src), op)
                elif m == 1 and which == 0:
                    fin(eng.tensor_copy(out=Wbig[:, dcq * 1024 + 512: dcq * 1024 + 1024],
                                        in_=src), op)
                else:
                    fin(eng.tensor_scalar_mul(
                        out=Wbig[:, (4 + dcq) * 1024:(4 + dcq) * 1024 + 512],
                        in0=src, scalar1=-1.0), op)
            elif k == "p_mask":
                slot = a["slot"]
                fin(eng.tensor_mul(out=Pbuf[:, slot * 512: slot * 512 + 128],
                                   in0=Pbuf[:, slot * 512: slot * 512 + 128],
                                   in1=smaskb[:, 0:128]), op)
            elif k == "o_ev":
                head, par = a["head"], a["par"]
                ob = pb[3 + 2 * head + par]
                fin(eng.tensor_copy(out=odiv32[0:64, head * 512:(head + 1) * 512],
                                    in_=ob[0:64, :]), op)
            elif k == "l_ev":
                head, par = a["head"], a["par"]
                ob = pb[3 + 2 * head + par]
                fin(eng.tensor_copy(out=FA[0:1, LSB + head * 512:
                                           LSB + (head + 1) * 512],
                                    in_=ob[64:65, 0:512]), op)
            elif k == "recl_hi":
                head = a["head"]
                fin(eng.tensor_copy(out=hilo[:, head * 4: head * 4 + 4],
                                    in_=FA[:, LQ + head * 4: LQ + (head + 1) * 4]), op)
            elif k == "recl_hif":
                head = a["head"]
                fin(eng.tensor_copy(out=FA[:, LQ + 8 + head * 4:
                                           LQ + 8 + (head + 1) * 4],
                                    in_=hilo[:, head * 4: head * 4 + 4]), op)
            elif k == "recl_lo":
                head = a["head"]
                fin(eng.scalar_tensor_tensor(
                    out=hilo[:, 8 + head * 4: 8 + (head + 1) * 4],
                    in0=FA[:, LQ + head * 4: LQ + (head + 1) * 4], scalar=0.0,
                    in1=FA[:, LQ + 8 + head * 4: LQ + 8 + (head + 1) * 4],
                    op0=AluOpType.add, op1=AluOpType.subtract), op)
            elif k == "recl":
                head = a["head"]
                fin(eng.reciprocal(out=FA[:, LQ + head * 4: LQ + (head + 1) * 4],
                                   in_=FA[:, LQ + head * 4: LQ + (head + 1) * 4]), op)
            elif k == "div_mult":
                hp, qg, head, par = a["hp"], a["qg"], a["head"], a["par"]
                r0 = head * 64
                dst = qT[r0:r0 + 64, hp * N_T + qg * 512: hp * N_T + (qg + 1) * 512]
                fin(eng.tensor_mul(out=dst,
                                   in0=odiv32[0:64, head * 512:(head + 1) * 512],
                                   in1=pb[3 + 2 * head + par][0:64, 0:512]), op)
            elif k == "res_xy":
                tt, oh = a["tt"], a["oh"]
                b = tt % 2
                sl = slice(b * D + oh * 512, b * D + (oh + 1) * 512)
                fin(eng.tensor_add(out=FA[:, TRU + sl.start: TRU + sl.stop],
                                   in0=FA[:, XBUF + sl.start: XBUF + sl.stop],
                                   in1=pb[oh][:, :]), op)
            elif k == "res_f":
                tt, oh = a["tt"], a["oh"]
                b = tt % 2
                sl = slice(b * D + oh * 512, b * D + (oh + 1) * 512)
                fin(eng.tensor_add(out=FA[:, TRU + sl.start: TRU + sl.stop],
                                   in0=FA[:, TRU + sl.start: TRU + sl.stop],
                                   in1=pb[2 + oh][:, :]), op)
            else:
                raise KeyError(k)

        def e_pe(eng, op):
            k, a = op.kind, op.args
            if k == "v_mm":
                tt, ic, oh = a["tt"], a["ic"], a["oh"]
                p = pb[oh] if tt % 2 == 0 else pb[2 + oh]
                fin(eng.matmul(p[:], xnt(ic, tt * 128, 128),
                               Wbig[:, ic * 1024 + oh * 512: ic * 1024 + (oh + 1) * 512],
                               start=a["start"], stop=a["stop"]), op)
            elif k == "qk_mm":
                tg, ic, pbank, slot = a["tg"], a["ic"], a["pbank"], a["slot"]
                if slot < 2:
                    wsl = A2S[:, WQKB + slot * 1024 + ic * 128:
                              WQKB + slot * 1024 + (ic + 1) * 128]
                else:
                    wsl = Wbig[:, 2048 + (slot - 2) * 1024 + ic * 128:
                               2048 + (slot - 2) * 1024 + (ic + 1) * 128]
                fin(eng.matmul(pb[pbank][:], wsl, xnt(ic, tg * 512, 512),
                               start=a["start"], stop=a["stop"]), op)
            elif k == "qksw_mm":
                tg, pbank, slot = a["tg"], a["pbank"], a["slot"]
                c0 = (WQKB + slot * 1024 if slot < 2
                      else 2048 + (slot - 2) * 1024)
                wt = A2S if slot < 2 else Wbig
                i = None
                for ic in range(8):
                    for j, src in enumerate((32, 0, 96, 64)):
                        i = eng.matmul(pb[pbank][32 * j:32 * j + 32, :],
                                       wt[:, c0 + ic * 128 + src:
                                          c0 + ic * 128 + src + 32],
                                       xnt(ic, tg * 512, 512),
                                       start=(ic == 0), stop=(ic == 7),
                                       tile_position=(0, 32 * j))
                fin(i, op)
            elif k == "xnt_tp":
                tt, grp = a["tt"], a["grp"]
                b4 = tt % 4
                bank = pb[6 + grp]
                i = None
                for j in range(4):
                    ic = grp * 4 + j
                    i = eng.transpose(bank[:, j * 128:(j + 1) * 128],
                                      FA[:, XBUF + b4 * D + ic * 128:
                                         XBUF + b4 * D + (ic + 1) * 128],
                                      FA[:, TRF:TRF + 128])
                fin(i, op)
            elif k == "mprime_mm":
                m, ch = a["m"], a["ch"]
                fin(eng.matmul(pb[a["bank"]][:, 0:256],
                               FA[0:4, XBUF + m * 1024 + ch * 128:
                                  XBUF + m * 1024 + (ch + 1) * 128],
                               FA[0:4, TRF + 512 + m * 256: TRF + 512 + (m + 1) * 256],
                               start=True, stop=True), op)
            elif k == "s_mm":
                hp, qg, kc, head, bank = a["hp"], a["qg"], a["kc"], a["head"], a["bank"]
                n, off = a["n"], a["off"]
                r0 = head * 64
                fin(eng.matmul((pb[bank] if bank < 3 else pb[7])[:, 0:n],
                               kT[r0:r0 + 64, hp * N_T + kc * 128:
                                  hp * N_T + (kc + 1) * 128],
                               qT[r0:r0 + 64, hp * N_T + qg * 512 + off:
                                  hp * N_T + (qg + 1) * 512],
                               start=True, stop=True), op)
            elif k == "pv_mm":
                hp, kc, head, par = a["hp"], a["kc"], a["head"], a["par"]
                slot, n, off = a["slot"], a["n"], a["off"]
                h = 2 * hp + head
                fin(eng.matmul(pb[3 + 2 * head + par][0:65, off:512],
                               Vsb[:, kc * NH * 65 + h * 65:
                                   kc * NH * 65 + (h + 1) * 65],
                               Pbuf[:, slot * 512: slot * 512 + n],
                               start=a["start"], stop=a["stop"]), op)
            elif k == "rb_bcast":
                head, par = a["head"], a["par"]
                fin(eng.matmul(pb[3 + 2 * head + par][0:64, 0:512],
                               sonesb[0:2, 0:64],
                               rrows[0:2, head * 1024: head * 1024 + 512],
                               start=True, stop=True), op)
            elif k == "wo_mm":
                tt, oh, dc = a["tt"], a["oh"], a["dc"]
                fin(eng.matmul(pb[oh][:],
                               qT[:, dc * N_T + tt * 128: dc * N_T + (tt + 1) * 128],
                               A2S[:, dc * 1024 + oh * 512: dc * 1024 + (oh + 1) * 512],
                               start=a["start"], stop=a["stop"]), op)
            elif k == "ffn_pre":
                oh = a["oh"]
                fin(eng.matmul(pb[2 + oh][:], sonesb[0:1, :],
                               sbiasb[0:1, oh * 512:(oh + 1) * 512],
                               start=True, stop=False), op)
            elif k == "ffn_mm":
                tt, oh, ic = a["tt"], a["oh"], a["ic"]
                fin(eng.matmul(pb[2 + oh][:], xnt(ic, tt * 128, 128),
                               Wbig[:, ic * 1024 + oh * 512: ic * 1024 + (oh + 1) * 512],
                               start=False, stop=a["stop"]), op)
            else:
                raise KeyError(k)

        @block.sync
        def _(eng):
            for op in sched.streams["sync"]:
                waits(eng, op)
                e_sync(eng, op)

        @block.gpsimd
        def _(eng):
            for op in sched.streams["gp"]:
                waits(eng, op)
                e_gp(eng, op)

        @block.scalar
        def _(eng):
            for op in sched.streams["act"]:
                waits(eng, op)
                e_act(eng, op)

        @block.vector
        def _(eng):
            for op in sched.streams["dve"]:
                waits(eng, op)
                e_dve(eng, op)

        @block.tensor
        def _(eng):
            for op in sched.streams["pe"]:
                waits(eng, op)
                e_pe(eng, op)

    return nc


_NC_CACHE = None


def _get_nc():
    global _NC_CACHE
    if _NC_CACHE is None:
        _NC_CACHE = build_nc()
    return _NC_CACHE


def _host_inputs(L0_attn_x, L0_attn_Wqkv, L0_attn_Wo, L0_attn_log_xi,
                 L0_attn_pi_gate_logit, L0_attn_e_gate_logit,
                 L0_ffn_G1_R, L0_ffn_G2_R, L0_ffn_G1_I, L0_ffn_G2_I,
                 L0_ffn_bias_real, L0_ffn_bias_imag):
    import ml_dtypes
    x = np.asarray(L0_attn_x, dtype=np.float32)
    Wqkv = np.asarray(L0_attn_Wqkv, dtype=np.float32)
    Wo = np.asarray(L0_attn_Wo, dtype=np.float32)

    idx = np.empty(D, dtype=np.int64)
    for h in range(NH):
        for par in range(2):
            for j in range(32):
                idx[h * 64 + par * 32 + j] = h * 64 + 2 * j + par
    WqkvT = np.ascontiguousarray(Wqkv.T)
    perm = np.concatenate([idx, D + idx, 2 * D + np.arange(D)])
    WqkvT = np.ascontiguousarray(WqkvT[:, perm])
    WoT = np.ascontiguousarray(Wo.T)

    G1R = np.asarray(L0_ffn_G1_R, dtype=np.float32)[0]
    G2R = np.asarray(L0_ffn_G2_R, dtype=np.float32)[:, :, 0, :]
    G1I = np.asarray(L0_ffn_G1_I, dtype=np.float32)[0]
    G2I = np.asarray(L0_ffn_G2_I, dtype=np.float32)[:, :, 0, :]
    g1 = np.concatenate(
        [np.ascontiguousarray(G1R.transpose(1, 0, 2)).reshape(4, 1024),
         np.ascontiguousarray(G1I.transpose(1, 0, 2)).reshape(4, 1024)], axis=1)
    g2 = np.concatenate(
        [np.ascontiguousarray(G2R.transpose(0, 2, 1)).reshape(4, 256),
         np.ascontiguousarray(G2I.transpose(0, 2, 1)).reshape(4, 256)], axis=1)

    sm = np.zeros((128, 1120), dtype=np.float32)
    pig = np.asarray(L0_attn_pi_gate_logit, dtype=np.float32)
    eg = np.asarray(L0_attn_e_gate_logit, dtype=np.float32)
    lx = np.asarray(L0_attn_log_xi, dtype=np.float32)
    for hp in range(NHP):
        sm[0:64, hp] = pig[2 * hp]
        sm[64:128, hp] = pig[2 * hp + 1]
    sm[:, 8:24] = eg[None, :]
    sm[:, 24:40] = lx[None, :]
    p = np.arange(128, dtype=np.float32)
    for i, (qg, kc) in enumerate(QGKC):
        sm[:, 40 + i] = p + 128 * kc - 512 * qg - 511
    sm[:, 95] = 1e-5
    sm[0, 96:96 + D] = np.concatenate(
        [np.asarray(L0_ffn_bias_real, np.float32),
         np.asarray(L0_ffn_bias_imag, np.float32)])

    tri = np.triu(np.ones((128, 128), np.float32))
    maskb = np.concatenate([tri, np.eye(128, dtype=np.float32)],
                           axis=1).astype(ml_dtypes.bfloat16)

    j = np.arange(32, dtype=np.float64)
    f = np.pi ** (1.0 - j / 32.0)
    t = np.arange(N_T, dtype=np.float64)
    base = np.outer(f, t) / (2 * np.pi)
    pospm = np.zeros((128, N_T), dtype=np.float32)
    for H in range(2):
        pospm[H * 64 + 0:H * 64 + 32] = -base
        pospm[H * 64 + 32:H * 64 + 64] = base
    ident = np.eye(128, dtype=np.float32)
    return x, WqkvT, WoT, g1, g2, sm, maskb, pospm, ident


def kernel(**inputs):
    x, WqkvT, WoT, g1, g2, sm, maskb, pospm, ident = _host_inputs(**inputs)
    nc = _get_nc()
    in_maps = [dict(x=np.ascontiguousarray(x[c]), wqkvT=WqkvT, woT=WoT, g1=g1,
                    g2=g2, smalls=sm, maskb=maskb, pospm=pospm, ident=ident)
               for c in range(8)]
    res = run_bass_kernel_spmd(nc, in_maps, core_ids=list(range(8)))
    return np.stack([np.asarray(r["out"]) for r in res.results],
                    axis=0).astype(np.float32)
